# version 17
# baseline (speedup 1.0000x reference)
import numpy as np
import concourse.bass as bass
import concourse.mybir as mybir
from concourse.bass_utils import run_bass_kernel_spmd

F32 = mybir.dt.float32
BF16 = mybir.dt.bfloat16
AF = mybir.ActivationFunctionType
ALU = mybir.AluOpType
AX = mybir.AxisListType

D = 1024
SEQ = 8192
NSLOT = 16
NPANEL = 8
PKT = 8
NEXP = 16
INTER = 512
EPS = 1e-5
BIG = 1.0e30


class _Res:
    __slots__ = ("w", "r", "name")
    def __init__(self, name):
        self.name = name
        self.w = []
        self.r = []


class _Prog:
    ENGINES = ("tensor", "vector", "scalar", "sync", "gpsimd")

    def __init__(self, nc):
        self.nc = nc
        self.q = {e: [] for e in self.ENGINES}
        self.cnt = {}
        self.semh = {}
        self.waited = {e: {} for e in self.ENGINES}

    def sem(self, name, stack):
        self.semh[name] = stack.enter_context(self.nc.semaphore(name))
        self.cnt[name] = 0

    ESEM = {"tensor": "pe", "vector": "ve", "scalar": "sc", "gpsimd": "gp",
            "sync": None}

    def _resolve(self, eng, toks):
        out = {}
        for t in toks:
            s, v, src = t
            if s is None:
                if src == eng:
                    continue
                raise RuntimeError(
                    f"cross-engine dep on un-inc'd op (src={src}, consumer={eng})")
            out[s] = max(out.get(s, 0), v)
        return sorted(out.items())

    def _hazard_toks(self, reads, writes, extra):
        toks = list(extra)
        for r in reads:
            toks += r.w
        for w in writes:
            toks += w.w + w.r
        return toks

    def _record(self, tok, reads, writes):
        for r in reads:
            r.r.append(tok)
        for w in writes:
            w.w = [tok]
            w.r = []

    def emit(self, eng, fn, reads=(), writes=(), extra=(), inc="auto", inc_by=1):
        if inc == "auto":
            inc = self.ESEM[eng]
        waits = self._resolve(eng, self._hazard_toks(reads, writes, extra))
        if inc is not None:
            self.cnt[inc] += inc_by
            tok = (inc, self.cnt[inc], eng)
        else:
            tok = (None, 0, eng)
        self.q[eng].append((fn, waits, inc, inc_by))
        self._record(tok, reads, writes)
        return tok

    def group(self, eng, fns, reads=(), writes=(), extra=(), inc="auto"):
        if inc == "auto":
            inc = self.ESEM[eng]
        waits = self._resolve(eng, self._hazard_toks(reads, writes, extra))
        tok = (None, 0, eng)
        n = len(fns)
        for i, fn in enumerate(fns):
            s = inc if (i == n - 1 and inc is not None) else None
            if s is not None:
                self.cnt[s] += 1
                tok = (s, self.cnt[s], eng)
            self.q[eng].append((fn, waits if i == 0 else [], s, 1))
        self._record(tok, reads, writes)
        return tok

    def flush_block(self):
        nc = self.nc
        snap = dict(self.cnt)
        qs = {e: list(self.q[e]) for e in self.ENGINES}
        for e in self.ENGINES:
            self.q[e] = []

        def body(engobj, e):
            for fn, waits, inc, inc_by in qs[e]:
                for s, v in waits:
                    engobj.wait_ge(self.semh[s], v)
                inst = fn(engobj)
                if inc is not None:
                    inst.then_inc(self.semh[inc], inc_by)

        with nc.Block() as block:
            if qs["tensor"]:
                block.tensor(lambda t: body(t, "tensor"))
            if qs["vector"]:
                block.vector(lambda t: body(t, "vector"))
            if qs["scalar"]:
                block.scalar(lambda t: body(t, "scalar"))
            if qs["sync"]:
                block.sync(lambda t: body(t, "sync"))
            if qs["gpsimd"]:
                block.gpsimd(lambda t: body(t, "gpsimd"))
        del snap


class _Stream:
    def __init__(self, prog, sem, eng):
        self.P, self.sem, self.eng = prog, sem, eng

    def _idle_wait(self):
        if self.P.cnt[self.sem]:
            return [(self.sem, self.P.cnt[self.sem], "dma")]
        return []

    def issue(self, fn, reads=(), writes=()):
        return self.P.emit(self.eng, fn, reads=reads, writes=writes,
                           extra=self._idle_wait(), inc=self.sem, inc_by=16)

    def issue_batch(self, items):
        P = self.P
        extra = self._idle_wait()
        rws = []
        for i, (fn, reads, writes) in enumerate(items):
            waits = P._resolve(self.eng, P._hazard_toks(
                reads, writes, extra if i == 0 else ()))
            P.cnt[self.sem] += 16
            P.q[self.eng].append((fn, waits, self.sem, 16))
            rws.append((reads, writes))
        tok = (self.sem, P.cnt[self.sem], "dma")
        for reads, writes in rws:
            P._record(tok, reads, writes)
        return tok


def _bcast(ap, inner):
    return bass.AP(ap.tensor, ap.offset,
                   [list(ap.ap[0]), list(ap.ap[1]), [0, inner]])



def build_program(attn_only=False):
    import contextlib
    nc = bass.Bass("TRN2", target_bir_lowering=False)

    x_seq = nc.dram_tensor("x_seq", [SEQ, D], F32, kind="ExternalInput").ap()
    x_own = nc.dram_tensor("x_own", [NSLOT, 128, D], F32, kind="ExternalInput").ap()
    maskd = nc.dram_tensor("mask", [128, 512], F32, kind="ExternalInput").ap()
    identd = nc.dram_tensor("ident", [128, 128], F32, kind="ExternalInput").ap()
    qk_img = nc.dram_tensor("qk_img", [128, 8 * D], F32, kind="ExternalInput").ap()
    ov_img = nc.dram_tensor("ov_img", [128, 8 * D], F32, kind="ExternalInput").ap()
    gwt_img = nc.dram_tensor("gwt_img", [128, 128], F32, kind="ExternalInput").ap()
    w1_img = nc.dram_tensor("w1_img", [NEXP, 128, 8 * INTER], F32, kind="ExternalInput").ap()
    w3_img = nc.dram_tensor("w3_img", [NEXP, 128, 8 * INTER], F32, kind="ExternalInput").ap()
    w2_img = nc.dram_tensor("w2_img", [NEXP, 128, 4 * D], F32, kind="ExternalInput").ap()
    ws1_img = nc.dram_tensor("ws1_img", [128, 8, D], F32, kind="ExternalInput").ap()
    ws3_img = nc.dram_tensor("ws3_img", [128, 8, D], F32, kind="ExternalInput").ap()
    ws2_img = nc.dram_tensor("ws2_img", [128, 8, D], F32, kind="ExternalInput").ap()
    out_d = nc.dram_tensor("out", [NSLOT, 128, D], F32, kind="ExternalOutput").ap()

    P = _Prog(nc)
    NT = NSLOT * 128

    with contextlib.ExitStack() as stk:
        for s in ("d_si", "d_gi", "d_sx0", "d_sx1", "d_sx2", "d_sf",
                  "d_so0", "d_so1", "d_so2", "d_so3",
                  "d_sw0", "d_sw1", "d_tp0", "d_tp1", "pe", "ve", "sc", "gp"):
            P.sem(s, stk)
        sSI = _Stream(P, "d_si", "sync")
        sGI = _Stream(P, "d_gi", "gpsimd")
        sSX = [_Stream(P, "d_sx0", "sync"), _Stream(P, "d_sx1", "sync"),
               _Stream(P, "d_sx2", "sync")]
        sSF = _Stream(P, "d_sf", "sync")
        sSO = [_Stream(P, "d_so0", "sync"), _Stream(P, "d_so1", "sync"),
               _Stream(P, "d_so2", "sync"), _Stream(P, "d_so3", "sync")]
        sSW = [_Stream(P, "d_sw0", "gpsimd"), _Stream(P, "d_sw1", "gpsimd")]
        sTP = [_Stream(P, "d_tp0", "sync"), _Stream(P, "d_tp1", "sync")]

        sb = lambda name, shape, dt: stk.enter_context(nc.sbuf_tensor(name, shape, dt))
        CONST = sb("CONST", [128, 4], F32)
        SM = sb("SM", [128, 96], F32)
        ACC = sb("ACC", [128, NSLOT * D], F32)
        STG = sb("STG", [128, 3 * D], F32)
        XOWNF = sb("XOWNF", [128, D], F32)
        IDB = sb("IDB", [128, 128], BF16)
        XLNBF = sb("XLNBF", [128, 2 * D], BF16)
        MASKBB = sb("MASKBB", [128, 512], BF16)

        acc = lambda m: ACC[:, m * D:(m + 1) * D]
        smc = lambda c, n=1: SM[:, c:c + n]
        stg = lambda i: STG[:, (i % 3) * D:((i % 3) + 1) * D]
        xlnb = lambda lane: XLNBF[:, lane * D:(lane + 1) * D]

        C_NM, C_LL = 0, 16
        LN_MSUM, LN_SSQ, LN_NMEAN, LN_NM2, LN_VAR, LN_RSTD, LN_NMR = range(7)
        C_NRM, C_TMIN, C_TMD, C_ALPHA, C_RSUM, C_RL = 48, 50, 52, 54, 57, 59
        C_GS, C_GR = 64, 80

        R = _Res
        rCONST = R("const")
        rLN = [R("ln0"), R("ln1")]
        rNM = [R(f"nm{m}") for m in range(NSLOT)]
        rLL = [R(f"ll{m}") for m in range(NSLOT)]
        rNRM = [R("nrm0"), R("nrm1")]
        rTMIN = [R("tmin0"), R("tmin1")]
        rTMD = [R("tmd0"), R("tmd1")]
        rALPHA = [R("al0"), R("al1"), R("al2")]
        rRSUM = [R("rs0"), R("rs1")]
        rRL = [R("rl0"), R("rl1")]
        rACC = [R(f"acc{m}") for m in range(NSLOT)]
        rSTG = [R("stg0"), R("stg1"), R("stg2")]
        rOVB, rMASK, rIDB = R("ovb"), R("mask"), R("idb")
        rMASKBB = R("maskbb")
        rXLNBF = [R("xlnbf0"), R("xlnbf1")]
        rSCR = [R("scr0"), R("scr1")]
        rXOWNs = R("xown")

        def ln_chain(src_ap, src_res, dst_ap, dst_res, lane, se="vector",
                     sum_on_scalar=False):
            rl = rLN[lane]
            base = 32 + 8 * lane
            c = lambda k: smc(base + k)
            dump = xlnb(lane)
            rdump = rXLNBF[lane]
            if sum_on_scalar:
                P.emit("scalar", lambda s: s.activation(
                    dump, src_ap, AF.Identity, accum_out=c(LN_MSUM)),
                    reads=[src_res], writes=[rl, rdump])
            else:
                P.emit(se, lambda v: v.tensor_scalar(
                    dump, src_ap, 1.0, 0.0, ALU.mult, ALU.add,
                    accum_out=c(LN_MSUM)),
                    reads=[src_res], writes=[rl, rdump])
            P.emit(se, lambda v: v.scalar_tensor_tensor(
                dump, src_ap, 1.0, src_ap, ALU.mult, ALU.mult,
                accum_out=c(LN_SSQ)),
                reads=[src_res], writes=[rl, rdump])
            P.emit(se, lambda v: v.tensor_scalar(
                c(LN_NMEAN), c(LN_MSUM), -1.0 / D, None, ALU.mult),
                reads=[rl], writes=[rl])
            P.emit(se, lambda v: v.scalar_tensor_tensor(
                c(LN_NM2), c(LN_NMEAN), -1.0, c(LN_NMEAN), ALU.mult, ALU.mult),
                reads=[rl], writes=[rl])
            P.emit(se, lambda v: v.scalar_tensor_tensor(
                c(LN_VAR), c(LN_SSQ), 1.0 / D, c(LN_NM2), ALU.mult, ALU.add),
                reads=[rl], writes=[rl])
            P.emit("scalar", lambda s: s.activation(
                c(LN_VAR), c(LN_VAR), AF.Ln, bias=CONST[:, 2:3]),
                reads=[rCONST, rl], writes=[rl])
            P.emit("scalar", lambda s: s.activation(
                c(LN_RSTD), c(LN_VAR), AF.Exp, scale=CONST[:, 1:2]),
                reads=[rCONST, rl], writes=[rl])
            P.emit("scalar", lambda s: s.activation(
                c(LN_NMR), c(LN_RSTD), AF.Identity, scale=c(LN_NMEAN)),
                reads=[rl], writes=[rl])
            wl = [dst_res] if dst_res is not None else []
            tnorm = P.emit("scalar", lambda s: s.activation(
                dst_ap, src_ap, AF.Identity, scale=c(LN_RSTD), bias=c(LN_NMR)),
                reads=[src_res, rl], writes=wl)
            return tnorm

        P.emit("gpsimd", lambda g: g.memset(SM[:, :], 0.0), writes=[rLN[0], rLN[1]])
        P.emit("gpsimd", lambda g: g.memset(CONST[:, 0:1], 1.0 / 32.0), writes=[rCONST])
        P.emit("gpsimd", lambda g: g.memset(CONST[:, 1:2], -0.5), writes=[rCONST])
        P.emit("gpsimd", lambda g: g.memset(CONST[:, 2:3], EPS), writes=[rCONST], inc="gp")

        with (
            nc.sbuf_tensor("Q", [128, 8 * NT], BF16) as Q,
            nc.sbuf_tensor("OVB", [128, 8 * D], BF16) as OVB,
        ):
            rQ = R("Q")
            qsl = lambda dch, lo, n: Q[:, dch * NT + lo: dch * NT + lo + n]

            with (
                nc.sbuf_tensor("XLNT", [128, 8 * NT], BF16) as XLNT,
                nc.sbuf_tensor("QKB", [128, 8 * D], BF16) as QKB,
                nc.psum_tensor("PSA", [128, 512], F32) as PSA,
                nc.psum_tensor("PSB", [128, 512], F32) as PSB,
                nc.psum_tensor("PSC", [128, 512], F32) as PSC,
                nc.psum_tensor("PSD", [128, 512], F32) as PSD,
                nc.psum_tensor("PSPT", [128, 1024], BF16) as PSPT,
            ):
                rXLNT, rQKB = R("XLNT"), R("QKB")
                rPSPT = R("PSPT")
                QPS = [(PSA, R("PSA")), (PSB, R("PSB")),
                       (PSC, R("PSC")), (PSD, R("PSD"))]
                sGI.issue_batch([
                    (lambda g: g.dma_start(IDB[:, :], identd), (), [rIDB]),
                    (lambda g: g.dma_start(MASKBB[:, :], maskd), (), [rMASKBB]),
                    (lambda g: g.dma_start(OVB[:, :], ov_img), (), [rOVB]),
                    (lambda g: g.dma_start(QKB[:, :], qk_img), (), [rQKB]),
                ])

                def pre_ln(m):
                    lane = m % 2
                    sSX[m % 3].issue(lambda s: s.dma_start(stg(m), x_own[m]),
                                     writes=[rSTG[m % 3]])
                    ln_chain(stg(m), rSTG[m % 3], xlnb(lane), rXLNBF[lane], lane)

                def pre_tp(m):
                    lane = m % 2
                    sTP[m % 2].issue(lambda s: s.dma_start(
                        XLNT[:, :].rearrange("p (a b) -> p a b", b=NT)
                        [:, :, m * 128:(m + 1) * 128],
                        xlnb(lane)[:, :].rearrange("p (c q) -> p c q", q=128),
                        transpose=True),
                        reads=[rXLNBF[lane]], writes=[rXLNT])

                NSG = NT // 512

                def q_group(g):
                    ech, sg = g // NSG, g % NSG
                    bank, rbank = QPS[g % 4]
                    fns = [(lambda t, dch=dch: t.matmul(
                        bank[:, 0:512],
                        QKB[:, dch * D + ech * 128: dch * D + (ech + 1) * 128],
                        XLNT[:, dch * NT + sg * 512: dch * NT + (sg + 1) * 512],
                        start=(dch == 0), stop=(dch == 7))) for dch in range(8)]
                    P.group("tensor", fns, reads=[rQKB, rXLNT], writes=[rbank],
                            inc="pe")
                    if g % 2 == 0:
                        P.emit("scalar", lambda s: s.activation(
                            qsl(ech, sg * 512, 512), bank[:, 0:512], AF.Identity,
                            scale=CONST[:, 0:1]),
                            reads=[rbank, rCONST], writes=[rQ], inc="sc")
                    else:
                        P.emit("vector", lambda v: v.tensor_scalar_mul(
                            qsl(ech, sg * 512, 512), bank[:, 0:512],
                            CONST[:, 0:1]),
                            reads=[rbank, rCONST], writes=[rQ], inc="ve")

                qdone = set()
                for m in range(NSLOT):
                    pre_ln(m)
                    if m >= 1:
                        pre_tp(m - 1)
                    if m % 4 == 3 and m >= 7:
                        sg = (m - 4) // 4
                        qdone.add(sg)
                        for ech in range(8):
                            q_group(ech * NSG + sg)
                pre_tp(NSLOT - 1)
                for sg in range(NSG):
                    if sg not in qdone:
                        for ech in range(8):
                            q_group(ech * NSG + sg)
                P.flush_block()

            with (
                nc.sbuf_tensor("KP0", [128, 8 * 1024], BF16) as KP0,
                nc.sbuf_tensor("KP1", [128, 8 * 1024], BF16) as KP1,
                nc.sbuf_tensor("VP0", [128, 8 * 1024], BF16) as VP0,
                nc.sbuf_tensor("VP1", [128, 8 * 1024], BF16) as VP1,
                nc.sbuf_tensor("PBF", [128, 2048], BF16) as PBF,
                nc.sbuf_tensor("PTS", [128, 3072], BF16) as PTS,
                nc.psum_tensor("PS_SE", [128, 1024], F32) as PS_SE,
                nc.psum_tensor("PS_SO", [128, 1024], F32) as PS_SO,
                nc.psum_tensor("PS_PV", [128, 1024], F32) as PS_PV,
                nc.psum_tensor("PSPT2", [128, 1024], BF16) as PSPT2,
                nc.psum_tensor("PSPT2B", [128, 1024], BF16) as PSPT2B,
            ):
                KP, VP = [KP0, KP1], [VP0, VP1]
                rKP = [R("kp0"), R("kp1")]
                rVP = [R("vp0"), R("vp1")]
                rVPk = [[R(f"vp{b}k{k}") for k in range(PKT)] for b in range(2)]
                rPBF = [R("pbf0"), R("pbf1")]
                rPTS = [R("pts0"), R("pts1"), R("pts2")]
                FINBF, FINT = PBF, PTS
                rFINBF, rFINT = rPBF[0], rPTS[0]
                rSE, rSO, rPV, rPT2 = R("se"), R("so"), R("pv"), R("pt2")
                rPT2B = R("pt2b")

                def kslice(buf, dch, lo, n):
                    return KP[buf][:, dch * 1024 + lo: dch * 1024 + lo + n]

                def vslice(buf, kt, lo, n):
                    return VP[buf][:, kt * 1024 + lo: kt * 1024 + lo + n]

                def emit_prod_load(p, kt):
                    buf = p % 2
                    gi = p * PKT + kt
                    sSX[gi % 3].issue(lambda s: s.dma_start(
                        stg(gi),
                        x_seq[p * 1024 + kt * 128: p * 1024 + (kt + 1) * 128, :]),
                        writes=[rSTG[gi % 3]])
                    ln_chain(stg(gi), rSTG[gi % 3],
                             vslice(buf, kt, 0, 1024), rVPk[buf][kt], kt % 2)

                def emit_prod_tp(p, kt):
                    buf = p % 2
                    sTP[kt % 2].issue(lambda s: s.dma_start(
                        KP[buf][:, :].rearrange("p (a b) -> p a b", b=1024)
                        [:, :, kt * 128:(kt + 1) * 128],
                        vslice(buf, kt, 0, 1024)
                        .rearrange("p (c q) -> p c q", q=128),
                        transpose=True),
                        reads=[rVPk[buf][kt]], writes=[rKP[buf]])

                def emit_qk(p, buf, i, m):
                    nkt = 4 if m == 2 * p else 8
                    nbank = nkt // 4
                    mask_bank = (0 if m == 2 * p else
                                 (1 if m == 2 * p + 1 else None))
                    SP, rSP = ((PS_SE, rSE) if i % 2 == 0 else (PS_SO, rSO))
                    fns = []
                    if mask_bank is not None:
                        fns.append(lambda t, mb=mask_bank: t.matmul(
                            SP[:, mb * 512:(mb + 1) * 512],
                            IDB[:, :], MASKBB[:, :],
                            start=True, stop=False))
                    for bank in range(nbank):
                        for dch in range(8):
                            st = (dch == 0) and (bank != mask_bank)
                            sp_ = (dch == 7)
                            fns.append(lambda t, bank=bank, dch=dch, st=st,
                                       sp_=sp_: t.matmul(
                                SP[:, bank * 512:(bank + 1) * 512],
                                qsl(dch, m * 128, 128),
                                kslice(buf, dch, bank * 512, 512),
                                start=st, stop=sp_))
                    rd = [rQ, rKP[buf]]
                    if mask_bank is not None:
                        rd += [rIDB, rMASKBB]
                    P.group("tensor", fns, reads=rd, writes=[rSP], inc="pe")
                    return nkt, SP, rSP

                def emit_max(p, i, m, nkt, SP, rSP):
                    par = i % 2
                    ext = nkt * 128
                    P.emit("vector", lambda v: v.reduce_max(
                        smc(C_NRM + par), SP[:, 0:ext], axis=AX.X, negate=True),
                        reads=[rSP], writes=[rNRM[par]], inc="ve")
                    if p == 0:
                        P.emit("vector", lambda v: v.tensor_copy(
                            smc(C_NM + m), smc(C_NRM + par)),
                            reads=[rNRM[par]], writes=[rNM[m]], inc="ve")
                    else:
                        P.emit("vector", lambda v: v.tensor_sub(
                            smc(C_TMD + par), smc(C_NRM + par), smc(C_NM + m)),
                            reads=[rNRM[par], rNM[m]], writes=[rTMD[par]])
                        P.emit("vector", lambda v: v.tensor_scalar_min(
                            smc(C_TMD + par), smc(C_TMD + par), 0.0),
                            reads=[rTMD[par]], writes=[rTMD[par]])
                        P.emit("vector", lambda v: v.tensor_add(
                            smc(C_NM + m), smc(C_NM + m), smc(C_TMD + par)),
                            reads=[rTMD[par], rNM[m]], writes=[rNM[m]], inc="ve")

                def emit_exp(p, i, m, nkt, SP, rSP):
                    par = i % 2
                    al = i % 3
                    ext = nkt * 128
                    if p > 0:
                        P.emit("scalar", lambda s: s.activation(
                            smc(C_ALPHA + al), smc(C_TMD + par), AF.Exp),
                            reads=[rTMD[par]], writes=[rALPHA[al]], inc="sc")
                    P.emit("scalar", lambda s: s.activation(
                        PBF[:, par * 1024: par * 1024 + ext], SP[:, 0:ext], AF.Exp,
                        bias=smc(C_NM + m), accum_out=smc(C_RSUM + par)),
                        reads=[rSP, rNM[m]], writes=[rPBF[par], rRSUM[par]],
                        inc="sc")

                def emit_llupd(p, i, m):
                    par = i % 2
                    al = i % 3
                    if p == 0:
                        P.emit("vector", lambda v: v.tensor_copy(
                            smc(C_LL + m), smc(C_RSUM + par)),
                            reads=[rRSUM[par]], writes=[rLL[m]], inc="ve")
                    else:
                        P.emit("vector", lambda v: v.scalar_tensor_tensor(
                            smc(C_LL + m), smc(C_LL + m), smc(C_ALPHA + al),
                            smc(C_RSUM + par), ALU.mult, ALU.add),
                            reads=[rRSUM[par], rALPHA[al]], writes=[rLL[m]],
                            inc="ve")

                def emit_t(p, buf, i, m, nkt):
                    par = i % 2
                    sl3 = i % 3
                    fns = [(lambda t, kt=kt: t.transpose(
                        PSPT2[:, kt * 128:(kt + 1) * 128],
                        PBF[:, par * 1024 + kt * 128: par * 1024 + (kt + 1) * 128],
                        IDB[:, :])) for kt in range(nkt)]
                    P.group("tensor", fns, reads=[rPBF[par], rIDB], writes=[rPT2],
                            inc="pe")
                    P.emit("scalar", lambda s: s.copy(
                        PTS[:, sl3 * 1024: sl3 * 1024 + nkt * 128],
                        PSPT2[:, 0:nkt * 128]),
                        reads=[rPT2], writes=[rPTS[sl3]], inc="sc")

                def emit_pv(p, buf, i, m, nkt):
                    al = i % 3
                    sl3 = i % 3
                    fns = []
                    for kt in range(nkt):
                        for half in range(2):
                            fns.append(lambda t, kt=kt, half=half: t.matmul(
                                PS_PV[:, half * 512:(half + 1) * 512],
                                PTS[:, sl3 * 1024 + kt * 128: sl3 * 1024 + (kt + 1) * 128],
                                vslice(buf, kt, half * 512, 512),
                                start=(kt == 0), stop=(kt == nkt - 1)))
                    P.group("tensor", fns,
                            reads=[rPTS[sl3]] + [rVPk[buf][kt] for kt in range(nkt)],
                            writes=[rPV], inc="pe")
                    if p == 0:
                        P.emit("vector", lambda v: v.tensor_copy(acc(m), PS_PV[:, :]),
                               reads=[rPV], writes=[rACC[m]], inc="ve")
                    else:
                        P.emit("vector", lambda v: v.scalar_tensor_tensor(
                            acc(m), acc(m), smc(C_ALPHA + al), PS_PV[:, :],
                            ALU.mult, ALU.add),
                            reads=[rPV, rALPHA[al]], writes=[rACC[m]], inc="ve")

                def emit_finalize(ma, mb):
                    items = ((ma, 0, PS_SE, rSE), (mb, 1, PS_SO, rSO))
                    sSF.issue(lambda s: s.dma_start(XOWNF[:, :], x_own[ma]),
                              writes=[rXOWNs])
                    for m, h, _, _ in items:
                        P.emit("vector", lambda v, m=m, h=h: v.reciprocal(
                            smc(C_RL + h), smc(C_LL + m)),
                            reads=[rLL[m]], writes=[rRL[h]], inc="ve")
                    for m, h, _, _ in items:
                        P.emit("scalar", lambda s, m=m, h=h: s.activation(
                            PBF[:, h * 1024:(h + 1) * 1024], acc(m), AF.Identity,
                            scale=smc(C_RL + h)),
                            reads=[rACC[m], rRL[h]], writes=[rPBF[h]], inc="sc")
                    for m, h, _, _ in items:
                        fns = [(lambda t, dch=dch, h=h: t.transpose(
                            PSPT2B[:, dch * 128:(dch + 1) * 128] if h == 0 else
                            PSPT2[:, dch * 128:(dch + 1) * 128],
                            PBF[:, h * 1024 + dch * 128: h * 1024 + (dch + 1) * 128],
                            IDB[:, :])) for dch in range(8)]
                        P.group("tensor", fns, reads=[rPBF[h], rIDB],
                                writes=[rPT2B if h == 0 else rPT2], inc="pe")
                        P.emit("scalar", lambda s, h=h: s.copy(
                            PTS[:, h * 1024:(h + 1) * 1024],
                            PSPT2B[:, :] if h == 0 else PSPT2[:, :]),
                            reads=[rPT2B if h == 0 else rPT2],
                            writes=[rPTS[h]], inc="sc")
                    for m, h, PSX, rX in items:
                        fns = []
                        for half2 in range(2):
                            for dch in range(8):
                                fns.append(lambda t, dch=dch, half2=half2,
                                           h=h, PSX=PSX: t.matmul(
                                    PSX[:, half2 * 512:(half2 + 1) * 512],
                                    PTS[:, h * 1024 + dch * 128: h * 1024 + (dch + 1) * 128],
                                    OVB[:, dch * D + half2 * 512: dch * D + (half2 + 1) * 512],
                                    start=(dch == 0), stop=(dch == 7)))
                        P.group("tensor", fns, reads=[rPTS[h], rOVB],
                                writes=[rX], inc="pe")
                        if h == 0:
                            P.emit("vector", lambda v, m=m, PSX=PSX: v.tensor_add(
                                acc(m), XOWNF[:, :], PSX[:, :]),
                                reads=[rXOWNs, rX], writes=[rACC[m]], inc="ve")
                            sSF.issue(lambda s, mb=mb: s.dma_start(
                                XOWNF[:, :], x_own[mb]),
                                writes=[rXOWNs])
                        else:
                            P.emit("vector", lambda v, m=m, PSX=PSX: v.tensor_add(
                                acc(m), XOWNF[:, :], PSX[:, :]),
                                reads=[rXOWNs, rX], writes=[rACC[m]], inc="ve")

                for kt in range(PKT):
                    emit_prod_load(0, kt)
                    if kt >= 1:
                        emit_prod_tp(0, kt - 1)
                emit_prod_tp(0, PKT - 1)

                def do_panel(p):
                    buf = p % 2
                    slots = list(range(2 * p, NSLOT))
                    prod = [(p + 1, kt) for kt in range(PKT)] if p < NPANEL - 1 else []
                    nld = ntp = 0
                    n = len(slots)
                    hist = []
                    for i, m in enumerate(slots):
                        nkt, SP, rSP = emit_qk(p, buf, i, m)
                        hist.append((p, buf, i, m, nkt, SP, rSP))
                        emit_max(p, i, m, nkt, SP, rSP)
                        if i >= 1:
                            emit_t(*hist[i - 1][:5])
                        emit_exp(p, i, m, nkt, SP, rSP)
                        if i >= 2:
                            emit_pv(*hist[i - 2][:5])
                        if i >= 1:
                            emit_llupd(p, i - 1, hist[i - 1][3])
                        want_ld = (len(prod) * (i + 1) + n - 1) // n
                        while nld < min(want_ld, len(prod)):
                            emit_prod_load(*prod[nld])
                            nld += 1
                        while ntp < nld - 4:
                            emit_prod_tp(*prod[ntp])
                            ntp += 1
                    emit_t(*hist[-1][:5])
                    if len(hist) >= 2:
                        emit_pv(*hist[-2][:5])
                    emit_pv(*hist[-1][:5])
                    emit_llupd(p, hist[-1][2], hist[-1][3])
                    while nld < len(prod):
                        emit_prod_load(*prod[nld])
                        nld += 1
                    while ntp < len(prod):
                        emit_prod_tp(*prod[ntp])
                        ntp += 1
                    emit_finalize(2 * p, 2 * p + 1)

                for p in range(NPANEL):
                    do_panel(p)
                P.flush_block()

        if attn_only:
            dbg = nc.dram_tensor("dbg", [128, 96 + 512], F32,
                                 kind="ExternalOutput").ap()
            sSO[0].issue(lambda s: s.dma_start(dbg[:, 0:96], SM[:, :]))
            for m in range(NSLOT):
                sSO[m % 2].issue(lambda s, m=m: s.dma_start(out_d[m], acc(m)),
                                 reads=[rACC[m]])
            P.emit("sync", lambda s: s.nop(),
                   extra=[("d_so0", P.cnt["d_so0"], "dma"),
                          ("d_so1", P.cnt["d_so1"], "dma")])
            P.flush_block()
            return nc

        GT = 16 * NSLOT
        TCH = min(1024, NT)
        NTH = NT // TCH
        s2 = contextlib.ExitStack()
        XF = s2.enter_context(nc.sbuf_tensor("XF", [128, 8 * NT], BF16))
        SCRF32 = s2.enter_context(nc.sbuf_tensor("SCRF32", [128, 2 * D], F32))
        W1B = s2.enter_context(nc.sbuf_tensor("W1B", [128, 2 * 4096], BF16))
        W3B = s2.enter_context(nc.sbuf_tensor("W3B", [128, 2 * 4096], BF16))
        W2B = s2.enter_context(nc.sbuf_tensor("W2B", [128, 2 * 4096], BF16))
        APR = s2.enter_context(nc.sbuf_tensor("APR", [128, 4 * NT], BF16))
        CC = s2.enter_context(nc.sbuf_tensor("CC", [128, GT], F32))
        LOG = s2.enter_context(nc.sbuf_tensor("LOG", [128, GT], F32))
        EXPS = s2.enter_context(nc.sbuf_tensor("EXPS", [128, GT], F32))
        GSC = s2.enter_context(nc.sbuf_tensor("GSC", [128, GT], F32))
        MS = LOG
        WRK = s2.enter_context(nc.sbuf_tensor("WRK", [128, GT], F32))
        SEL = EXPS
        GMAX = WRK[:, 2 * 4 * NSLOT: 3 * 4 * NSLOT]
        GTH = WRK[:, 3 * 4 * NSLOT: 4 * 4 * NSLOT]
        GW = s2.enter_context(nc.sbuf_tensor("GW", [128, 256], BF16))
        XLO = s2.enter_context(nc.sbuf_tensor("XLO", [128, 2048], BF16))
        XLOT = s2.enter_context(nc.sbuf_tensor("XLOT", [128, 2048], BF16))
        if True:
            rXF, rAPR, rCC, rLOG, rEXPS, rGSC = (
                R("xf"), R("apr"), R("cc"), R("log"), R("exps"), R("gsc"))
            rWRK = R("wrk")
            rMS = rLOG
            rGMAX = rGTH = rWRK
            rSEL = rEXPS
            rW1 = [R("w1a"), R("w1b")]
            rW3 = [R("w3a"), R("w3b")]
            rW2 = [R("w2a"), R("w2b")]
            GWF = SCRF32[:, 0:128]
            rGW = R("gw")
            rGWF = rSCR[0]
            rXLOT = [R("xlot0"), R("xlot1")]
            rXLO = [R("xlo0"), R("xlo1")]
            rGSUM = R("gsum")

            def xf(dch, lo, n):
                return XF[:, dch * NT + lo: dch * NT + lo + n]

            def prefetch_expert(e, pi):
                d1 = W1B[:, pi * 4096:(pi + 1) * 4096]
                d3 = W3B[:, pi * 4096:(pi + 1) * 4096]
                d2 = W2B[:, pi * 4096:(pi + 1) * 4096]
                if e < NEXP:
                    s1, s3, s2 = w1_img[e], w3_img[e], w2_img[e]
                else:
                    h = e - NEXP
                    s1 = ws1_img[:, :, h * 512:(h + 1) * 512]
                    s3 = ws3_img[:, :, h * 512:(h + 1) * 512]
                    s2 = ws2_img[:, h * 4:(h + 1) * 4, :]
                    d1 = d1.rearrange("p (a b) -> p a b", b=512)
                    d3 = d3.rearrange("p (a b) -> p a b", b=512)
                    d2 = d2.rearrange("p (a b) -> p a b", b=1024)
                sSW[pi].issue_batch([
                    (lambda g: g.dma_start(d1, s1), (), [rW1[pi]]),
                    (lambda g: g.dma_start(d3, s3), (), [rW3[pi]]),
                    (lambda g: g.dma_start(d2, s2), (), [rW2[pi]]),
                ])

            s2a = contextlib.ExitStack()
            PT2A = s2a.enter_context(nc.psum_tensor("PT2A", [128, 1024], BF16))
            PT2B = s2a.enter_context(nc.psum_tensor("PT2B", [128, 1024], BF16))
            PG0 = s2a.enter_context(nc.psum_tensor("PG0", [128, 512], F32))
            PG1 = s2a.enter_context(nc.psum_tensor("PG1", [128, 512], F32))
            if True:
                rPT2A, rPT2B = R("pt2a"), R("pt2b")
                PG, rPG = [PG0, PG1], [R("pg0"), R("pg1")]

                prefetch_expert(0, 0)
                sSI.issue(lambda s: s.dma_start(GWF, gwt_img),
                          writes=[rGWF])
                P.emit("vector", lambda v: v.tensor_copy(GW[:, 0:128], GWF),
                       reads=[rGWF], writes=[rGW], inc="ve")
                P.emit("vector", lambda v: v.tensor_sub(
                    GW[:, 128:256], GWF, GW[:, 0:128]),
                    reads=[rGWF], writes=[rGW], inc="ve")

                scr = lambda lane: SCRF32[:, lane * D:(lane + 1) * D]
                xlo = lambda lane: XLO[:, lane * D:(lane + 1) * D]

                def do_tt_ln(m):
                    lane = m % 2
                    ln_chain(acc(m), rACC[m], scr(lane), rSCR[lane], lane)
                    P.emit("vector", lambda v, lane=lane: v.tensor_copy(
                        xlnb(lane), scr(lane)),
                        reads=[rSCR[lane]], writes=[rXLNBF[lane]])
                    P.emit("vector", lambda v, lane=lane: v.tensor_sub(
                        xlo(lane), scr(lane), xlnb(lane)),
                        reads=[rSCR[lane], rXLNBF[lane]], writes=[rXLO[lane]],
                        inc="ve")

                def do_tt_tp(m):
                    lane = m % 2
                    sTP[m % 2].issue(lambda s: s.dma_start(
                        XF[:, :].rearrange("p (a b) -> p a b", b=NT)
                        [:, :, m * 128:(m + 1) * 128],
                        xlnb(lane)[:, :].rearrange("p (c q) -> p c q", q=128),
                        transpose=True),
                        reads=[rXLNBF[lane]], writes=[rXF])
                    sTP[(m + 1) % 2].issue(lambda s, lane=lane: s.dma_start(
                        XLOT[:, lane * D:(lane + 1) * D]
                        .rearrange("p (c q) -> p c q", q=128),
                        xlo(lane)[:, :].rearrange("p (c q) -> p c q", q=128),
                        transpose=True),
                        reads=[rXLO[lane]], writes=[rXLOT[lane]])

                def do_tt_gate(m):
                    lane = m % 2
                    pg, rpg = PG[m % 2], rPG[m % 2]
                    fns = []
                    n = 24
                    idx = 0
                    for (sn, goff) in (("hi", 0), ("lo", 0), ("hi", 128)):
                        for dch in range(8):
                            def mk(sn=sn, goff=goff, dch=dch, idx=idx, lane=lane,
                                   pg=pg):
                                def fn(t):
                                    lhs = (xf(dch, m * 128, 128) if sn == "hi"
                                           else XLOT[:, lane * D + dch * 128:
                                                     lane * D + (dch + 1) * 128])
                                    return t.matmul(
                                        pg[:, 0:16], lhs,
                                        GW[:, goff + dch * 16: goff + (dch + 1) * 16],
                                        start=(idx == 0), stop=(idx == n - 1))
                                return fn
                            fns.append(mk())
                            idx += 1
                    P.group("tensor", fns, reads=[rXF, rXLOT[lane], rGW],
                            writes=[rpg], inc="pe")
                    P.emit("vector", lambda v, pg=pg: v.tensor_copy(
                        LOG[:, m * 16:(m + 1) * 16], pg[:, 0:16]),
                        reads=[rpg], writes=[rLOG], inc="ve")

                for m in range(NSLOT):
                    if m >= 2:
                        do_tt_gate(m - 2)
                    do_tt_ln(m)
                    if m >= 1:
                        do_tt_tp(m - 1)
                do_tt_tp(NSLOT - 1)
                do_tt_gate(NSLOT - 2)
                do_tt_gate(NSLOT - 1)


                b3 = lambda a, inner: a.rearrange("p (a b) -> p a b", b=inner)
                P.emit("scalar", lambda s: s.activation(
                    EXPS[:, :], LOG[:, :], AF.Exp),
                    reads=[rLOG], writes=[rEXPS], inc="sc")
                P.emit("vector", lambda v: v.reduce_sum(
                    smc(C_GS, NSLOT), b3(EXPS[:, :], 16), axis=AX.X),
                    reads=[rEXPS], writes=[rGSUM])
                P.emit("vector", lambda v: v.reciprocal(
                    smc(C_GR, NSLOT), smc(C_GS, NSLOT)),
                    reads=[rGSUM], writes=[rGSUM])
                P.emit("vector", lambda v: v.tensor_tensor(
                    b3(GSC[:, :], 16), b3(EXPS[:, :], 16),
                    _bcast(smc(C_GR, NSLOT), 16), ALU.mult),
                    reads=[rEXPS, rGSUM], writes=[rGSC])
                P.emit("vector", lambda v: v.reduce_max(
                    GMAX, b3(GSC[:, :], 4), axis=AX.X),
                    reads=[rGSC], writes=[rGMAX])
                P.emit("vector", lambda v: v.reduce_max(
                    smc(C_GS, NSLOT), b3(GMAX, 4), axis=AX.X, negate=True),
                    reads=[rGMAX], writes=[rGSUM])
                P.emit("vector", lambda v: v.tensor_scalar(
                    smc(C_GR, NSLOT), smc(C_GS, NSLOT), -1.0, None, ALU.mult),
                    reads=[rGSUM], writes=[rGSUM])
                P.emit("vector", lambda v: v.tensor_tensor(
                    b3(GTH, 4), b3(GMAX, 4),
                    _bcast(smc(C_GR, NSLOT), 4), ALU.is_ge),
                    reads=[rGMAX, rGSUM], writes=[rGTH])
                P.emit("vector", lambda v: v.scalar_tensor_tensor(
                    WRK[:, 0:4 * NSLOT], GTH, -BIG, GMAX,
                    ALU.mult, ALU.add),
                    reads=[rGTH, rGMAX], writes=[rWRK])
                P.emit("vector", lambda v: v.reduce_max(
                    smc(C_GS, NSLOT), b3(WRK[:, 0:4 * NSLOT], 4), axis=AX.X,
                    negate=True),
                    reads=[rWRK], writes=[rGSUM])
                P.emit("vector", lambda v: v.tensor_scalar(
                    smc(C_GR, NSLOT), smc(C_GS, NSLOT), -1.0, None, ALU.mult),
                    reads=[rGSUM], writes=[rGSUM])
                P.emit("vector", lambda v: v.tensor_tensor(
                    b3(GTH, 4), b3(GMAX, 4),
                    _bcast(smc(C_GR, NSLOT), 4), ALU.is_ge),
                    reads=[rGMAX, rGSUM], writes=[rGTH])
                P.emit("vector", lambda v: v.tensor_tensor(
                    b3(MS[:, :], 4), b3(GSC[:, :], 4),
                    _bcast(GTH, 4), ALU.mult),
                    reads=[rGSC, rGTH], writes=[rMS])
                P.emit("vector", lambda v: v.tensor_copy(WRK[:, :], MS[:, :]),
                       reads=[rMS], writes=[rWRK])
                for _ in range(3):
                    P.emit("vector", lambda v: v.reduce_max(
                        smc(C_GS, NSLOT), b3(WRK[:, :], 16), axis=AX.X, negate=True),
                        reads=[rWRK], writes=[rGSUM])
                    P.emit("vector", lambda v: v.tensor_scalar(
                        smc(C_GR, NSLOT), smc(C_GS, NSLOT), -1.0, None, ALU.mult),
                        reads=[rGSUM], writes=[rGSUM])
                    P.emit("vector", lambda v: v.tensor_tensor(
                        b3(SEL[:, :], 16), b3(WRK[:, :], 16),
                        _bcast(smc(C_GR, NSLOT), 16), ALU.is_ge),
                        reads=[rWRK, rGSUM], writes=[rSEL])
                    P.emit("vector", lambda v: v.scalar_tensor_tensor(
                        WRK[:, :], SEL[:, :], -BIG, WRK[:, :], ALU.mult, ALU.add),
                        reads=[rSEL], writes=[rWRK])
                P.emit("vector", lambda v: v.reduce_max(
                    smc(C_GS, NSLOT), b3(WRK[:, :], 16), axis=AX.X, negate=True),
                    reads=[rWRK], writes=[rGSUM])
                P.emit("vector", lambda v: v.tensor_scalar(
                    smc(C_GR, NSLOT), smc(C_GS, NSLOT), -1.0, None, ALU.mult),
                    reads=[rGSUM], writes=[rGSUM])
                P.emit("vector", lambda v: v.tensor_tensor(
                    b3(SEL[:, :], 16), b3(MS[:, :], 16),
                    _bcast(smc(C_GR, NSLOT), 16), ALU.is_ge),
                    reads=[rMS, rGSUM], writes=[rSEL])
                P.emit("vector", lambda v: v.tensor_mul(
                    CC[:, :], GSC[:, :], SEL[:, :]),
                    reads=[rGSC, rSEL], writes=[rCC], inc="ve")
                P.flush_block()
            s2a.close()

            s2b = contextlib.ExitStack()
            H0 = s2b.enter_context(nc.psum_tensor("H0", [128, 1024], F32))
            H1 = s2b.enter_context(nc.psum_tensor("H1", [128, 1024], F32))
            H2 = s2b.enter_context(nc.psum_tensor("H2", [128, 1024], F32))
            H3 = s2b.enter_context(nc.psum_tensor("H3", [128, 1024], F32))
            if True:
                HH = [H0, H1, H2, H3]
                rH = [R("h0"), R("h1"), R("h2"), R("h3")]

                def do_expert(e):
                    pi = e % 2
                    if e + 1 < NEXP + 2:
                        prefetch_expert(e + 1, 1 - pi)

                    def w2_tt(tt):
                        yp, ryp = HH[tt % 2], rH[tt % 2]
                        fns = []
                        for ich in range(4):
                            for half in range(2):
                                fns.append(
                                    lambda t, ich=ich, half=half, tt=tt, yp=yp,
                                    pi=pi: t.matmul(
                                        yp[:, half * 512:(half + 1) * 512],
                                        APR[:, ich * NT + tt * 128:
                                            ich * NT + (tt + 1) * 128],
                                        W2B[:, pi * 4096 + ich * 1024 + half * 512:
                                            pi * 4096 + ich * 1024 + (half + 1) * 512],
                                        start=(ich == 0), stop=(ich == 3)))
                        P.group("tensor", fns, reads=[rAPR, rW2[pi]], writes=[ryp],
                                inc="pe")
                        if e < NEXP:
                            P.emit("vector",
                                   lambda v, tt=tt, yp=yp, e=e: v.scalar_tensor_tensor(
                                       acc(tt), yp[:, :],
                                       CC[:, tt * 16 + e: tt * 16 + e + 1],
                                       acc(tt), ALU.mult, ALU.add),
                                   reads=[ryp, rCC], writes=[rACC[tt]], inc="ve")
                        else:
                            P.emit("vector", lambda v, tt=tt, yp=yp: v.tensor_add(
                                acc(tt), acc(tt), yp[:, :]),
                                reads=[ryp], writes=[rACC[tt]], inc="ve")
                        if e == NEXP + 1:
                            sSO[tt % 4].issue(lambda s, tt=tt: s.dma_start(
                                out_d[tt], acc(tt)),
                                reads=[rACC[tt]])

                    for th in range(NTH):
                        for ich in range(4):
                            hp = (th * 4 + ich) % 2
                            pairs = ((HH[2 * hp], rH[2 * hp], W1B, rW1[pi]),
                                     (HH[2 * hp + 1], rH[2 * hp + 1], W3B, rW3[pi]))
                            for (ps, rps, wbuf, rwb) in pairs:
                                fns = []
                                for dch in range(8):
                                    for half in range(TCH // 512):
                                        fns.append(
                                            lambda t, dch=dch, half=half, ps=ps,
                                            wbuf=wbuf, ich=ich, th=th, pi=pi: t.matmul(
                                                ps[:, half * 512:(half + 1) * 512],
                                                wbuf[:, pi * 4096 + dch * 512 + ich * 128:
                                                     pi * 4096 + dch * 512 + (ich + 1) * 128],
                                                xf(dch, th * TCH + half * 512, 512),
                                                start=(dch == 0), stop=(dch == 7)))
                                P.group("tensor", fns, reads=[rwb, rXF],
                                        writes=[rps], inc="pe")
                            h1ps, rh1 = HH[2 * hp], rH[2 * hp]
                            h3ps, rh3 = HH[2 * hp + 1], rH[2 * hp + 1]
                            P.emit("scalar", lambda s, h1ps=h1ps: s.activation(
                                SCRF32[:, 0:TCH], h1ps[:, 0:TCH], AF.Silu),
                                reads=[rh1], writes=[rSCR[0]], inc="sc")
                            P.emit("vector",
                                   lambda v, h3ps=h3ps, ich=ich, th=th: v.tensor_mul(
                                       APR[:, ich * NT + th * TCH:
                                           ich * NT + (th + 1) * TCH],
                                       SCRF32[:, 0:TCH], h3ps[:, 0:TCH]),
                                   reads=[rSCR[0], rh3], writes=[rAPR], inc="ve")
                        if th == NTH - 1:
                            for tt in range(8 * th, NSLOT):
                                w2_tt(tt)
                        else:
                            for tt in range(8 * th, 8 * (th + 1)):
                                w2_tt(tt)

                for e in range(NEXP + 2):
                    do_expert(e)

                P.emit("sync", lambda s: s.nop(),
                       extra=[("d_so0", P.cnt["d_so0"], "dma"),
                              ("d_so1", P.cnt["d_so1"], "dma"),
                              ("d_so2", P.cnt["d_so2"], "dma"),
                              ("d_so3", P.cnt["d_so3"], "dma")])
                P.flush_block()
            s2b.close()
        s2.close()

    return nc



def _img_km(M, r):
    R_, C = M.shape
    assert R_ == r * 128
    return np.ascontiguousarray(M.reshape(r, 128, C).transpose(1, 0, 2))


_CACHED = {}


def _get_nc(attn_only=False):
    if attn_only not in _CACHED:
        _CACHED[attn_only] = build_program(attn_only)
    return _CACHED[attn_only]


def kernel(L0_attn_input_data, L0_attn_qk, L0_attn_ov, L0_ffn_gate_w,
           L0_ffn_w1, L0_ffn_w2, L0_ffn_w3, L0_ffn_ws1, L0_ffn_ws2, L0_ffn_ws3,
           _attn_only=False):
    x = np.ascontiguousarray(np.asarray(L0_attn_input_data, dtype=np.float32))
    qk = np.asarray(L0_attn_qk, dtype=np.float32)
    ov = np.asarray(L0_attn_ov, dtype=np.float32)
    gw = np.asarray(L0_ffn_gate_w, dtype=np.float32)
    w1 = np.asarray(L0_ffn_w1, dtype=np.float32)
    w2 = np.asarray(L0_ffn_w2, dtype=np.float32)
    w3 = np.asarray(L0_ffn_w3, dtype=np.float32)
    ws1 = np.asarray(L0_ffn_ws1, dtype=np.float32)
    ws2 = np.asarray(L0_ffn_ws2, dtype=np.float32)
    ws3 = np.asarray(L0_ffn_ws3, dtype=np.float32)

    qk_i = _img_km(qk, 8).reshape(128, 8 * D)
    ov_i = _img_km(ov, 8).reshape(128, 8 * D)
    gwt_i = _img_km(np.ascontiguousarray(gw.T), 8).reshape(128, 128)
    w1_i = np.stack([_img_km(w1[e], 8).reshape(128, 8 * INTER) for e in range(NEXP)])
    w3_i = np.stack([_img_km(w3[e], 8).reshape(128, 8 * INTER) for e in range(NEXP)])
    w2_i = np.stack([_img_km(w2[e], 4).reshape(128, 4 * D) for e in range(NEXP)])
    ws1_i = _img_km(ws1, 8)
    ws3_i = _img_km(ws3, 8)
    ws2_i = _img_km(ws2, 8)
    ident = np.eye(128, dtype=np.float32)

    in_maps = []
    for c in range(8):
        b, c4 = c // 4, c % 4
        xb = np.ascontiguousarray(x[b])
        xo = np.ascontiguousarray(xb.reshape(64, 128, D)[c4::4])
        jj = np.arange(512)[None, :]
        ii = np.arange(128)[:, None]
        mask = np.where(jj <= ii + 128 * c4, 0.0, -BIG).astype(np.float32)
        in_maps.append({
            "x_seq": xb, "x_own": xo, "mask": mask, "ident": ident,
            "qk_img": qk_i, "ov_img": ov_i, "gwt_img": gwt_i,
            "w1_img": w1_i, "w3_img": w3_i, "w2_img": w2_i,
            "ws1_img": ws1_i, "ws3_img": ws3_i, "ws2_img": ws2_i,
        })

    nc = _get_nc(_attn_only)
    res = run_bass_kernel_spmd(nc, in_maps, core_ids=list(range(8)))

    out = np.empty((2, SEQ, D), dtype=np.float32)
    for c in range(8):
        b, c4 = c // 4, c % 4
        out[b].reshape(64, 128, D)[c4::4] = res.results[c]["out"]
    return out


# version 18
# speedup vs baseline: 1.0139x; 1.0139x over previous
import numpy as np
import concourse.bass as bass
import concourse.mybir as mybir
from concourse.bass_utils import run_bass_kernel_spmd

F32 = mybir.dt.float32
BF16 = mybir.dt.bfloat16
AF = mybir.ActivationFunctionType
ALU = mybir.AluOpType
AX = mybir.AxisListType

D = 1024
SEQ = 8192
NSLOT = 16
NPANEL = 8
PKT = 8
NEXP = 16
INTER = 512
EPS = 1e-5
BIG = 1.0e30


class _Res:
    __slots__ = ("w", "r", "name")
    def __init__(self, name):
        self.name = name
        self.w = []
        self.r = []


class _Prog:
    ENGINES = ("tensor", "vector", "scalar", "sync", "gpsimd")

    def __init__(self, nc):
        self.nc = nc
        self.q = {e: [] for e in self.ENGINES}
        self.cnt = {}
        self.semh = {}
        self.waited = {e: {} for e in self.ENGINES}

    def sem(self, name, stack):
        self.semh[name] = stack.enter_context(self.nc.semaphore(name))
        self.cnt[name] = 0

    ESEM = {"tensor": "pe", "vector": "ve", "scalar": "sc", "gpsimd": "gp",
            "sync": None}

    def _resolve(self, eng, toks):
        out = {}
        for t in toks:
            s, v, src = t
            if s is None:
                if src == eng:
                    continue
                raise RuntimeError(
                    f"cross-engine dep on un-inc'd op (src={src}, consumer={eng})")
            out[s] = max(out.get(s, 0), v)
        return sorted(out.items())

    def _hazard_toks(self, reads, writes, extra):
        toks = list(extra)
        for r in reads:
            toks += r.w
        for w in writes:
            toks += w.w + w.r
        return toks

    def _record(self, tok, reads, writes):
        for r in reads:
            r.r.append(tok)
        for w in writes:
            w.w = [tok]
            w.r = []

    def emit(self, eng, fn, reads=(), writes=(), extra=(), inc="auto", inc_by=1):
        if inc == "auto":
            inc = self.ESEM[eng]
        waits = self._resolve(eng, self._hazard_toks(reads, writes, extra))
        if inc is not None:
            self.cnt[inc] += inc_by
            tok = (inc, self.cnt[inc], eng)
        else:
            tok = (None, 0, eng)
        self.q[eng].append((fn, waits, inc, inc_by))
        self._record(tok, reads, writes)
        return tok

    def group(self, eng, fns, reads=(), writes=(), extra=(), inc="auto"):
        if inc == "auto":
            inc = self.ESEM[eng]
        waits = self._resolve(eng, self._hazard_toks(reads, writes, extra))
        tok = (None, 0, eng)
        n = len(fns)
        for i, fn in enumerate(fns):
            s = inc if (i == n - 1 and inc is not None) else None
            if s is not None:
                self.cnt[s] += 1
                tok = (s, self.cnt[s], eng)
            self.q[eng].append((fn, waits if i == 0 else [], s, 1))
        self._record(tok, reads, writes)
        return tok

    def flush_block(self):
        nc = self.nc
        snap = dict(self.cnt)
        qs = {e: list(self.q[e]) for e in self.ENGINES}
        for e in self.ENGINES:
            self.q[e] = []

        def body(engobj, e):
            for fn, waits, inc, inc_by in qs[e]:
                for s, v in waits:
                    engobj.wait_ge(self.semh[s], v)
                inst = fn(engobj)
                if inc is not None:
                    inst.then_inc(self.semh[inc], inc_by)

        with nc.Block() as block:
            if qs["tensor"]:
                block.tensor(lambda t: body(t, "tensor"))
            if qs["vector"]:
                block.vector(lambda t: body(t, "vector"))
            if qs["scalar"]:
                block.scalar(lambda t: body(t, "scalar"))
            if qs["sync"]:
                block.sync(lambda t: body(t, "sync"))
            if qs["gpsimd"]:
                block.gpsimd(lambda t: body(t, "gpsimd"))
        del snap


class _Stream:
    def __init__(self, prog, sem, eng):
        self.P, self.sem, self.eng = prog, sem, eng

    def _idle_wait(self):
        if self.P.cnt[self.sem]:
            return [(self.sem, self.P.cnt[self.sem], "dma")]
        return []

    def issue(self, fn, reads=(), writes=()):
        return self.P.emit(self.eng, fn, reads=reads, writes=writes,
                           extra=self._idle_wait(), inc=self.sem, inc_by=16)

    def issue_batch(self, items):
        P = self.P
        extra = self._idle_wait()
        rws = []
        for i, (fn, reads, writes) in enumerate(items):
            waits = P._resolve(self.eng, P._hazard_toks(
                reads, writes, extra if i == 0 else ()))
            P.cnt[self.sem] += 16
            P.q[self.eng].append((fn, waits, self.sem, 16))
            rws.append((reads, writes))
        tok = (self.sem, P.cnt[self.sem], "dma")
        for reads, writes in rws:
            P._record(tok, reads, writes)
        return tok


def _bcast(ap, inner):
    return bass.AP(ap.tensor, ap.offset,
                   [list(ap.ap[0]), list(ap.ap[1]), [0, inner]])



def build_program(attn_only=False):
    import contextlib
    nc = bass.Bass("TRN2", target_bir_lowering=False)

    x_seq = nc.dram_tensor("x_seq", [SEQ, D], F32, kind="ExternalInput").ap()
    x_own = nc.dram_tensor("x_own", [NSLOT, 128, D], F32, kind="ExternalInput").ap()
    maskd = nc.dram_tensor("mask", [128, 512], F32, kind="ExternalInput").ap()
    identd = nc.dram_tensor("ident", [128, 128], F32, kind="ExternalInput").ap()
    qk_img = nc.dram_tensor("qk_img", [128, 8 * D], F32, kind="ExternalInput").ap()
    ov_img = nc.dram_tensor("ov_img", [128, 8 * D], F32, kind="ExternalInput").ap()
    gwt_img = nc.dram_tensor("gwt_img", [128, 128], F32, kind="ExternalInput").ap()
    w1_img = nc.dram_tensor("w1_img", [NEXP, 128, 8 * INTER], F32, kind="ExternalInput").ap()
    w3_img = nc.dram_tensor("w3_img", [NEXP, 128, 8 * INTER], F32, kind="ExternalInput").ap()
    w2_img = nc.dram_tensor("w2_img", [NEXP, 128, 4 * D], F32, kind="ExternalInput").ap()
    ws1_img = nc.dram_tensor("ws1_img", [128, 8, D], F32, kind="ExternalInput").ap()
    ws3_img = nc.dram_tensor("ws3_img", [128, 8, D], F32, kind="ExternalInput").ap()
    ws2_img = nc.dram_tensor("ws2_img", [128, 8, D], F32, kind="ExternalInput").ap()
    out_d = nc.dram_tensor("out", [NSLOT, 128, D], F32, kind="ExternalOutput").ap()

    P = _Prog(nc)
    NT = NSLOT * 128

    with contextlib.ExitStack() as stk:
        for s in ("d_si", "d_gi", "d_sx0", "d_sx1", "d_sx2", "d_sf",
                  "d_so0", "d_so1", "d_so2", "d_so3",
                  "d_sw0", "d_sw1", "d_tp0", "d_tp1", "d_tp2", "d_tp3",
                  "pe", "ve", "sc", "gp"):
            P.sem(s, stk)
        sSI = _Stream(P, "d_si", "sync")
        sGI = _Stream(P, "d_gi", "gpsimd")
        sSX = [_Stream(P, "d_sx0", "sync"), _Stream(P, "d_sx1", "sync"),
               _Stream(P, "d_sx2", "sync")]
        sSF = _Stream(P, "d_sf", "sync")
        sSO = [_Stream(P, "d_so0", "sync"), _Stream(P, "d_so1", "sync"),
               _Stream(P, "d_so2", "sync"), _Stream(P, "d_so3", "sync")]
        sSW = [_Stream(P, "d_sw0", "gpsimd"), _Stream(P, "d_sw1", "gpsimd")]
        sTP = [_Stream(P, "d_tp0", "sync"), _Stream(P, "d_tp1", "sync"),
               _Stream(P, "d_tp2", "sync"), _Stream(P, "d_tp3", "sync")]

        sb = lambda name, shape, dt: stk.enter_context(nc.sbuf_tensor(name, shape, dt))
        CONST = sb("CONST", [128, 4], F32)
        SM = sb("SM", [128, 96], F32)
        ACC = sb("ACC", [128, NSLOT * D], F32)
        STG = sb("STG", [128, 3 * D], F32)
        XOWNF = sb("XOWNF", [128, D], F32)
        IDB = sb("IDB", [128, 128], BF16)
        XLNBF = sb("XLNBF", [128, 2 * D], BF16)
        MASKBB = sb("MASKBB", [128, 512], BF16)

        acc = lambda m: ACC[:, m * D:(m + 1) * D]
        smc = lambda c, n=1: SM[:, c:c + n]
        stg = lambda i: STG[:, (i % 3) * D:((i % 3) + 1) * D]
        xlnb = lambda lane: XLNBF[:, lane * D:(lane + 1) * D]

        C_NM, C_LL = 0, 16
        LN_MSUM, LN_SSQ, LN_NMEAN, LN_NM2, LN_VAR, LN_RSTD, LN_NMR = range(7)
        C_NRM, C_TMIN, C_TMD, C_ALPHA, C_RSUM, C_RL = 48, 50, 52, 54, 57, 59
        C_GS, C_GR = 64, 80

        R = _Res
        rCONST = R("const")
        rLN = [R("ln0"), R("ln1")]
        rNM = [R(f"nm{m}") for m in range(NSLOT)]
        rLL = [R(f"ll{m}") for m in range(NSLOT)]
        rNRM = [R("nrm0"), R("nrm1")]
        rTMIN = [R("tmin0"), R("tmin1")]
        rTMD = [R("tmd0"), R("tmd1")]
        rALPHA = [R("al0"), R("al1"), R("al2")]
        rRSUM = [R("rs0"), R("rs1")]
        rRL = [R("rl0"), R("rl1")]
        rACC = [R(f"acc{m}") for m in range(NSLOT)]
        rSTG = [R("stg0"), R("stg1"), R("stg2")]
        rOVB, rMASK, rIDB = R("ovb"), R("mask"), R("idb")
        rMASKBB = R("maskbb")
        rXLNBF = [R("xlnbf0"), R("xlnbf1")]
        rSCR = [R("scr0"), R("scr1")]
        rXOWNs = R("xown")

        def ln_chain(src_ap, src_res, dst_ap, dst_res, lane, se="vector",
                     sum_on_scalar=False):
            rl = rLN[lane]
            base = 32 + 8 * lane
            c = lambda k: smc(base + k)
            dump = xlnb(lane)
            rdump = rXLNBF[lane]
            if sum_on_scalar:
                P.emit("scalar", lambda s: s.activation(
                    dump, src_ap, AF.Identity, accum_out=c(LN_MSUM)),
                    reads=[src_res], writes=[rl, rdump])
            else:
                P.emit(se, lambda v: v.tensor_scalar(
                    dump, src_ap, 1.0, 0.0, ALU.mult, ALU.add,
                    accum_out=c(LN_MSUM)),
                    reads=[src_res], writes=[rl, rdump])
            P.emit(se, lambda v: v.scalar_tensor_tensor(
                dump, src_ap, 1.0, src_ap, ALU.mult, ALU.mult,
                accum_out=c(LN_SSQ)),
                reads=[src_res], writes=[rl, rdump])
            P.emit(se, lambda v: v.tensor_scalar(
                c(LN_NMEAN), c(LN_MSUM), -1.0 / D, None, ALU.mult),
                reads=[rl], writes=[rl])
            P.emit(se, lambda v: v.scalar_tensor_tensor(
                c(LN_NM2), c(LN_NMEAN), -1.0, c(LN_NMEAN), ALU.mult, ALU.mult),
                reads=[rl], writes=[rl])
            P.emit(se, lambda v: v.scalar_tensor_tensor(
                c(LN_VAR), c(LN_SSQ), 1.0 / D, c(LN_NM2), ALU.mult, ALU.add),
                reads=[rl], writes=[rl])
            P.emit("scalar", lambda s: s.activation(
                c(LN_VAR), c(LN_VAR), AF.Ln, bias=CONST[:, 2:3]),
                reads=[rCONST, rl], writes=[rl])
            P.emit("scalar", lambda s: s.activation(
                c(LN_RSTD), c(LN_VAR), AF.Exp, scale=CONST[:, 1:2]),
                reads=[rCONST, rl], writes=[rl])
            P.emit("scalar", lambda s: s.activation(
                c(LN_NMR), c(LN_RSTD), AF.Identity, scale=c(LN_NMEAN)),
                reads=[rl], writes=[rl])
            wl = [dst_res] if dst_res is not None else []
            tnorm = P.emit("scalar", lambda s: s.activation(
                dst_ap, src_ap, AF.Identity, scale=c(LN_RSTD), bias=c(LN_NMR)),
                reads=[src_res, rl], writes=wl)
            return tnorm

        P.emit("gpsimd", lambda g: g.memset(SM[:, :], 0.0), writes=[rLN[0], rLN[1]])
        P.emit("gpsimd", lambda g: g.memset(CONST[:, 0:1], 1.0 / 32.0), writes=[rCONST])
        P.emit("gpsimd", lambda g: g.memset(CONST[:, 1:2], -0.5), writes=[rCONST])
        P.emit("gpsimd", lambda g: g.memset(CONST[:, 2:3], EPS), writes=[rCONST], inc="gp")

        with (
            nc.sbuf_tensor("Q", [128, 8 * NT], BF16) as Q,
            nc.sbuf_tensor("OVB", [128, 8 * D], BF16) as OVB,
        ):
            rQ = R("Q")
            qsl = lambda dch, lo, n: Q[:, dch * NT + lo: dch * NT + lo + n]

            with (
                nc.sbuf_tensor("XLNT", [128, 8 * NT], BF16) as XLNT,
                nc.sbuf_tensor("QKB", [128, 8 * D], BF16) as QKB,
                nc.psum_tensor("PSA", [128, 512], F32) as PSA,
                nc.psum_tensor("PSB", [128, 512], F32) as PSB,
                nc.psum_tensor("PSC", [128, 512], F32) as PSC,
                nc.psum_tensor("PSD", [128, 512], F32) as PSD,
                nc.psum_tensor("PSPT", [128, 1024], BF16) as PSPT,
            ):
                rXLNT, rQKB = R("XLNT"), R("QKB")
                rPSPT = R("PSPT")
                QPS = [(PSA, R("PSA")), (PSB, R("PSB")),
                       (PSC, R("PSC")), (PSD, R("PSD"))]
                sGI.issue_batch([
                    (lambda g: g.dma_start(IDB[:, :], identd), (), [rIDB]),
                    (lambda g: g.dma_start(MASKBB[:, :], maskd), (), [rMASKBB]),
                    (lambda g: g.dma_start(OVB[:, :], ov_img), (), [rOVB]),
                    (lambda g: g.dma_start(QKB[:, :], qk_img), (), [rQKB]),
                ])

                def pre_ln(m):
                    lane = m % 2
                    sSX[m % 3].issue(lambda s: s.dma_start(stg(m), x_own[m]),
                                     writes=[rSTG[m % 3]])
                    ln_chain(stg(m), rSTG[m % 3], xlnb(lane), rXLNBF[lane], lane)

                def pre_tp(m):
                    lane = m % 2
                    sTP[m % 4].issue(lambda s: s.dma_start(
                        XLNT[:, :].rearrange("p (a b) -> p a b", b=NT)
                        [:, :, m * 128:(m + 1) * 128],
                        xlnb(lane)[:, :].rearrange("p (c q) -> p c q", q=128),
                        transpose=True),
                        reads=[rXLNBF[lane]], writes=[rXLNT])

                NSG = NT // 512

                def q_group(g):
                    ech, sg = g // NSG, g % NSG
                    bank, rbank = QPS[g % 4]
                    fns = [(lambda t, dch=dch: t.matmul(
                        bank[:, 0:512],
                        QKB[:, dch * D + ech * 128: dch * D + (ech + 1) * 128],
                        XLNT[:, dch * NT + sg * 512: dch * NT + (sg + 1) * 512],
                        start=(dch == 0), stop=(dch == 7))) for dch in range(8)]
                    P.group("tensor", fns, reads=[rQKB, rXLNT], writes=[rbank],
                            inc="pe")
                    if g % 2 == 0:
                        P.emit("scalar", lambda s: s.activation(
                            qsl(ech, sg * 512, 512), bank[:, 0:512], AF.Identity,
                            scale=CONST[:, 0:1]),
                            reads=[rbank, rCONST], writes=[rQ], inc="sc")
                    else:
                        P.emit("vector", lambda v: v.tensor_scalar_mul(
                            qsl(ech, sg * 512, 512), bank[:, 0:512],
                            CONST[:, 0:1]),
                            reads=[rbank, rCONST], writes=[rQ], inc="ve")

                qready = []
                qi = 0
                for m in range(NSLOT):
                    pre_ln(m)
                    if m >= 1:
                        pre_tp(m - 1)
                    if m % 4 == 3 and m >= 7:
                        sg = (m - 4) // 4
                        qready += [ech * NSG + sg for ech in range(8)]
                    while qi < len(qready) and qi < max(0, (m - 6)) * 3:
                        q_group(qready[qi])
                        qi += 1
                pre_tp(NSLOT - 1)
                done = set(qready[:qi])
                rest = [g for g in qready if g not in done]
                rest += [g for g in range(8 * NSG) if g not in done and g not in rest]
                for g in rest:
                    q_group(g)
                P.flush_block()

            with (
                nc.sbuf_tensor("KP0", [128, 8 * 1024], BF16) as KP0,
                nc.sbuf_tensor("KP1", [128, 8 * 1024], BF16) as KP1,
                nc.sbuf_tensor("VP0", [128, 8 * 1024], BF16) as VP0,
                nc.sbuf_tensor("VP1", [128, 8 * 1024], BF16) as VP1,
                nc.sbuf_tensor("PBF", [128, 2048], BF16) as PBF,
                nc.sbuf_tensor("PTS", [128, 3072], BF16) as PTS,
                nc.psum_tensor("PS_SE", [128, 1024], F32) as PS_SE,
                nc.psum_tensor("PS_SO", [128, 1024], F32) as PS_SO,
                nc.psum_tensor("PS_PV", [128, 1024], F32) as PS_PV,
                nc.psum_tensor("PSPT2", [128, 1024], BF16) as PSPT2,
                nc.psum_tensor("PSPT2B", [128, 1024], BF16) as PSPT2B,
            ):
                KP, VP = [KP0, KP1], [VP0, VP1]
                rKP = [R("kp0"), R("kp1")]
                rVP = [R("vp0"), R("vp1")]
                rVPk = [[R(f"vp{b}k{k}") for k in range(PKT)] for b in range(2)]
                rPBF = [R("pbf0"), R("pbf1")]
                rPTS = [R("pts0"), R("pts1"), R("pts2")]
                FINBF, FINT = PBF, PTS
                rFINBF, rFINT = rPBF[0], rPTS[0]
                rSE, rSO, rPV, rPT2 = R("se"), R("so"), R("pv"), R("pt2")
                rPT2B = R("pt2b")

                def kslice(buf, dch, lo, n):
                    return KP[buf][:, dch * 1024 + lo: dch * 1024 + lo + n]

                def vslice(buf, kt, lo, n):
                    return VP[buf][:, kt * 1024 + lo: kt * 1024 + lo + n]

                def emit_prod_load(p, kt):
                    buf = p % 2
                    gi = p * PKT + kt
                    sSX[gi % 3].issue(lambda s: s.dma_start(
                        stg(gi),
                        x_seq[p * 1024 + kt * 128: p * 1024 + (kt + 1) * 128, :]),
                        writes=[rSTG[gi % 3]])
                    ln_chain(stg(gi), rSTG[gi % 3],
                             vslice(buf, kt, 0, 1024), rVPk[buf][kt], kt % 2)

                def emit_prod_tp(p, kt):
                    buf = p % 2
                    sTP[kt % 4].issue(lambda s: s.dma_start(
                        KP[buf][:, :].rearrange("p (a b) -> p a b", b=1024)
                        [:, :, kt * 128:(kt + 1) * 128],
                        vslice(buf, kt, 0, 1024)
                        .rearrange("p (c q) -> p c q", q=128),
                        transpose=True),
                        reads=[rVPk[buf][kt]], writes=[rKP[buf]])

                def emit_qk(p, buf, i, m):
                    nkt = 4 if m == 2 * p else 8
                    nbank = nkt // 4
                    mask_bank = (0 if m == 2 * p else
                                 (1 if m == 2 * p + 1 else None))
                    SP, rSP = ((PS_SE, rSE) if i % 2 == 0 else (PS_SO, rSO))
                    fns = []
                    if mask_bank is not None:
                        fns.append(lambda t, mb=mask_bank: t.matmul(
                            SP[:, mb * 512:(mb + 1) * 512],
                            IDB[:, :], MASKBB[:, :],
                            start=True, stop=False))
                    for bank in range(nbank):
                        for dch in range(8):
                            st = (dch == 0) and (bank != mask_bank)
                            sp_ = (dch == 7)
                            fns.append(lambda t, bank=bank, dch=dch, st=st,
                                       sp_=sp_: t.matmul(
                                SP[:, bank * 512:(bank + 1) * 512],
                                qsl(dch, m * 128, 128),
                                kslice(buf, dch, bank * 512, 512),
                                start=st, stop=sp_))
                    rd = [rQ, rKP[buf]]
                    if mask_bank is not None:
                        rd += [rIDB, rMASKBB]
                    P.group("tensor", fns, reads=rd, writes=[rSP], inc="pe")
                    return nkt, SP, rSP

                def emit_max(p, i, m, nkt, SP, rSP):
                    par = i % 2
                    ext = nkt * 128
                    P.emit("vector", lambda v: v.reduce_max(
                        smc(C_NRM + par), SP[:, 0:ext], axis=AX.X, negate=True),
                        reads=[rSP], writes=[rNRM[par]], inc="ve")
                    if p == 0:
                        P.emit("vector", lambda v: v.tensor_copy(
                            smc(C_NM + m), smc(C_NRM + par)),
                            reads=[rNRM[par]], writes=[rNM[m]], inc="ve")
                    else:
                        P.emit("vector", lambda v: v.tensor_sub(
                            smc(C_TMD + par), smc(C_NRM + par), smc(C_NM + m)),
                            reads=[rNRM[par], rNM[m]], writes=[rTMD[par]])
                        P.emit("vector", lambda v: v.tensor_scalar_min(
                            smc(C_TMD + par), smc(C_TMD + par), 0.0),
                            reads=[rTMD[par]], writes=[rTMD[par]])
                        P.emit("vector", lambda v: v.tensor_add(
                            smc(C_NM + m), smc(C_NM + m), smc(C_TMD + par)),
                            reads=[rTMD[par], rNM[m]], writes=[rNM[m]], inc="ve")

                def emit_exp(p, i, m, nkt, SP, rSP):
                    par = i % 2
                    al = i % 3
                    ext = nkt * 128
                    if p > 0:
                        P.emit("scalar", lambda s: s.activation(
                            smc(C_ALPHA + al), smc(C_TMD + par), AF.Exp),
                            reads=[rTMD[par]], writes=[rALPHA[al]], inc="sc")
                    P.emit("scalar", lambda s: s.activation(
                        PBF[:, par * 1024: par * 1024 + ext], SP[:, 0:ext], AF.Exp,
                        bias=smc(C_NM + m), accum_out=smc(C_RSUM + par)),
                        reads=[rSP, rNM[m]], writes=[rPBF[par], rRSUM[par]],
                        inc="sc")

                def emit_llupd(p, i, m):
                    par = i % 2
                    al = i % 3
                    if p == 0:
                        P.emit("vector", lambda v: v.tensor_copy(
                            smc(C_LL + m), smc(C_RSUM + par)),
                            reads=[rRSUM[par]], writes=[rLL[m]], inc="ve")
                    else:
                        P.emit("vector", lambda v: v.scalar_tensor_tensor(
                            smc(C_LL + m), smc(C_LL + m), smc(C_ALPHA + al),
                            smc(C_RSUM + par), ALU.mult, ALU.add),
                            reads=[rRSUM[par], rALPHA[al]], writes=[rLL[m]],
                            inc="ve")

                def emit_t(p, buf, i, m, nkt):
                    par = i % 2
                    sl3 = i % 3
                    fns = [(lambda t, kt=kt: t.transpose(
                        PSPT2[:, kt * 128:(kt + 1) * 128],
                        PBF[:, par * 1024 + kt * 128: par * 1024 + (kt + 1) * 128],
                        IDB[:, :])) for kt in range(nkt)]
                    P.group("tensor", fns, reads=[rPBF[par], rIDB], writes=[rPT2],
                            inc="pe")
                    P.emit("scalar", lambda s: s.copy(
                        PTS[:, sl3 * 1024: sl3 * 1024 + nkt * 128],
                        PSPT2[:, 0:nkt * 128]),
                        reads=[rPT2], writes=[rPTS[sl3]], inc="sc")

                def emit_pv(p, buf, i, m, nkt):
                    al = i % 3
                    sl3 = i % 3
                    fns = []
                    for kt in range(nkt):
                        for half in range(2):
                            fns.append(lambda t, kt=kt, half=half: t.matmul(
                                PS_PV[:, half * 512:(half + 1) * 512],
                                PTS[:, sl3 * 1024 + kt * 128: sl3 * 1024 + (kt + 1) * 128],
                                vslice(buf, kt, half * 512, 512),
                                start=(kt == 0), stop=(kt == nkt - 1)))
                    P.group("tensor", fns,
                            reads=[rPTS[sl3]] + [rVPk[buf][kt] for kt in range(nkt)],
                            writes=[rPV], inc="pe")
                    if p == 0:
                        P.emit("vector", lambda v: v.tensor_copy(acc(m), PS_PV[:, :]),
                               reads=[rPV], writes=[rACC[m]], inc="ve")
                    else:
                        P.emit("vector", lambda v: v.scalar_tensor_tensor(
                            acc(m), acc(m), smc(C_ALPHA + al), PS_PV[:, :],
                            ALU.mult, ALU.add),
                            reads=[rPV, rALPHA[al]], writes=[rACC[m]], inc="ve")

                def emit_finalize(ma, mb):
                    items = ((ma, 0, PS_SE, rSE), (mb, 1, PS_SO, rSO))
                    sSF.issue(lambda s: s.dma_start(XOWNF[:, :], x_own[ma]),
                              writes=[rXOWNs])
                    for m, h, _, _ in items:
                        P.emit("vector", lambda v, m=m, h=h: v.reciprocal(
                            smc(C_RL + h), smc(C_LL + m)),
                            reads=[rLL[m]], writes=[rRL[h]], inc="ve")
                    for m, h, _, _ in items:
                        P.emit("scalar", lambda s, m=m, h=h: s.activation(
                            PBF[:, h * 1024:(h + 1) * 1024], acc(m), AF.Identity,
                            scale=smc(C_RL + h)),
                            reads=[rACC[m], rRL[h]], writes=[rPBF[h]], inc="sc")
                    for m, h, _, _ in items:
                        fns = [(lambda t, dch=dch, h=h: t.transpose(
                            PSPT2B[:, dch * 128:(dch + 1) * 128] if h == 0 else
                            PSPT2[:, dch * 128:(dch + 1) * 128],
                            PBF[:, h * 1024 + dch * 128: h * 1024 + (dch + 1) * 128],
                            IDB[:, :])) for dch in range(8)]
                        P.group("tensor", fns, reads=[rPBF[h], rIDB],
                                writes=[rPT2B if h == 0 else rPT2], inc="pe")
                        P.emit("scalar", lambda s, h=h: s.copy(
                            PTS[:, h * 1024:(h + 1) * 1024],
                            PSPT2B[:, :] if h == 0 else PSPT2[:, :]),
                            reads=[rPT2B if h == 0 else rPT2],
                            writes=[rPTS[h]], inc="sc")
                    for m, h, PSX, rX in items:
                        fns = []
                        for half2 in range(2):
                            for dch in range(8):
                                fns.append(lambda t, dch=dch, half2=half2,
                                           h=h, PSX=PSX: t.matmul(
                                    PSX[:, half2 * 512:(half2 + 1) * 512],
                                    PTS[:, h * 1024 + dch * 128: h * 1024 + (dch + 1) * 128],
                                    OVB[:, dch * D + half2 * 512: dch * D + (half2 + 1) * 512],
                                    start=(dch == 0), stop=(dch == 7)))
                        P.group("tensor", fns, reads=[rPTS[h], rOVB],
                                writes=[rX], inc="pe")
                        if h == 0:
                            P.emit("vector", lambda v, m=m, PSX=PSX: v.tensor_add(
                                acc(m), XOWNF[:, :], PSX[:, :]),
                                reads=[rXOWNs, rX], writes=[rACC[m]], inc="ve")
                            sSF.issue(lambda s, mb=mb: s.dma_start(
                                XOWNF[:, :], x_own[mb]),
                                writes=[rXOWNs])
                        else:
                            P.emit("vector", lambda v, m=m, PSX=PSX: v.tensor_add(
                                acc(m), XOWNF[:, :], PSX[:, :]),
                                reads=[rXOWNs, rX], writes=[rACC[m]], inc="ve")

                for kt in range(PKT):
                    emit_prod_load(0, kt)
                    if kt >= 1:
                        emit_prod_tp(0, kt - 1)
                emit_prod_tp(0, PKT - 1)

                def do_panel(p):
                    buf = p % 2
                    slots = list(range(2 * p, NSLOT))
                    prod = [(p + 1, kt) for kt in range(PKT)] if p < NPANEL - 1 else []
                    nld = ntp = 0
                    n = len(slots)
                    hist = []
                    for i, m in enumerate(slots):
                        nkt, SP, rSP = emit_qk(p, buf, i, m)
                        hist.append((p, buf, i, m, nkt, SP, rSP))
                        emit_max(p, i, m, nkt, SP, rSP)
                        if i >= 1:
                            emit_t(*hist[i - 1][:5])
                        emit_exp(p, i, m, nkt, SP, rSP)
                        if i >= 2:
                            emit_pv(*hist[i - 2][:5])
                        if i >= 1:
                            emit_llupd(p, i - 1, hist[i - 1][3])
                        want_ld = (len(prod) * (i + 1) + n - 1) // n
                        while nld < min(want_ld, len(prod)):
                            emit_prod_load(*prod[nld])
                            nld += 1
                        while ntp < nld - 4:
                            emit_prod_tp(*prod[ntp])
                            ntp += 1
                    emit_t(*hist[-1][:5])
                    if len(hist) >= 2:
                        emit_pv(*hist[-2][:5])
                    emit_pv(*hist[-1][:5])
                    emit_llupd(p, hist[-1][2], hist[-1][3])
                    while nld < len(prod):
                        emit_prod_load(*prod[nld])
                        nld += 1
                    while ntp < len(prod):
                        emit_prod_tp(*prod[ntp])
                        ntp += 1
                    emit_finalize(2 * p, 2 * p + 1)

                for p in range(NPANEL):
                    do_panel(p)
                P.flush_block()

        if attn_only:
            dbg = nc.dram_tensor("dbg", [128, 96 + 512], F32,
                                 kind="ExternalOutput").ap()
            sSO[0].issue(lambda s: s.dma_start(dbg[:, 0:96], SM[:, :]))
            for m in range(NSLOT):
                sSO[m % 2].issue(lambda s, m=m: s.dma_start(out_d[m], acc(m)),
                                 reads=[rACC[m]])
            P.emit("sync", lambda s: s.nop(),
                   extra=[("d_so0", P.cnt["d_so0"], "dma"),
                          ("d_so1", P.cnt["d_so1"], "dma")])
            P.flush_block()
            return nc

        GT = 16 * NSLOT
        TCH = min(1024, NT)
        NTH = NT // TCH
        s2 = contextlib.ExitStack()
        XF = s2.enter_context(nc.sbuf_tensor("XF", [128, 8 * NT], BF16))
        SCRF32 = s2.enter_context(nc.sbuf_tensor("SCRF32", [128, 2 * D], F32))
        W1B = s2.enter_context(nc.sbuf_tensor("W1B", [128, 2 * 4096], BF16))
        W3B = s2.enter_context(nc.sbuf_tensor("W3B", [128, 2 * 4096], BF16))
        W2B = s2.enter_context(nc.sbuf_tensor("W2B", [128, 2 * 4096], BF16))
        APR = s2.enter_context(nc.sbuf_tensor("APR", [128, 4 * NT], BF16))
        CC = s2.enter_context(nc.sbuf_tensor("CC", [128, GT], F32))
        LOG = s2.enter_context(nc.sbuf_tensor("LOG", [128, GT], F32))
        EXPS = s2.enter_context(nc.sbuf_tensor("EXPS", [128, GT], F32))
        GSC = s2.enter_context(nc.sbuf_tensor("GSC", [128, GT], F32))
        MS = LOG
        WRK = s2.enter_context(nc.sbuf_tensor("WRK", [128, GT], F32))
        SEL = EXPS
        GMAX = WRK[:, 2 * 4 * NSLOT: 3 * 4 * NSLOT]
        GTH = WRK[:, 3 * 4 * NSLOT: 4 * 4 * NSLOT]
        GW = s2.enter_context(nc.sbuf_tensor("GW", [128, 256], BF16))
        XLO = s2.enter_context(nc.sbuf_tensor("XLO", [128, 2048], BF16))
        XLOT = s2.enter_context(nc.sbuf_tensor("XLOT", [128, 2048], BF16))
        if True:
            rXF, rAPR, rCC, rLOG, rEXPS, rGSC = (
                R("xf"), R("apr"), R("cc"), R("log"), R("exps"), R("gsc"))
            rWRK = R("wrk")
            rMS = rLOG
            rGMAX = rGTH = rWRK
            rSEL = rEXPS
            rW1 = [R("w1a"), R("w1b")]
            rW3 = [R("w3a"), R("w3b")]
            rW2 = [R("w2a"), R("w2b")]
            GWF = SCRF32[:, 0:128]
            rGW = R("gw")
            rGWF = rSCR[0]
            rXLOT = [R("xlot0"), R("xlot1")]
            rXLO = [R("xlo0"), R("xlo1")]
            rGSUM = R("gsum")

            def xf(dch, lo, n):
                return XF[:, dch * NT + lo: dch * NT + lo + n]

            def prefetch_expert(e, pi):
                d1 = W1B[:, pi * 4096:(pi + 1) * 4096]
                d3 = W3B[:, pi * 4096:(pi + 1) * 4096]
                d2 = W2B[:, pi * 4096:(pi + 1) * 4096]
                if e < NEXP:
                    s1, s3, s2 = w1_img[e], w3_img[e], w2_img[e]
                else:
                    h = e - NEXP
                    s1 = ws1_img[:, :, h * 512:(h + 1) * 512]
                    s3 = ws3_img[:, :, h * 512:(h + 1) * 512]
                    s2 = ws2_img[:, h * 4:(h + 1) * 4, :]
                    d1 = d1.rearrange("p (a b) -> p a b", b=512)
                    d3 = d3.rearrange("p (a b) -> p a b", b=512)
                    d2 = d2.rearrange("p (a b) -> p a b", b=1024)
                sSW[pi].issue_batch([
                    (lambda g: g.dma_start(d1, s1), (), [rW1[pi]]),
                    (lambda g: g.dma_start(d3, s3), (), [rW3[pi]]),
                    (lambda g: g.dma_start(d2, s2), (), [rW2[pi]]),
                ])

            s2a = contextlib.ExitStack()
            PT2A = s2a.enter_context(nc.psum_tensor("PT2A", [128, 1024], BF16))
            PT2B = s2a.enter_context(nc.psum_tensor("PT2B", [128, 1024], BF16))
            PG0 = s2a.enter_context(nc.psum_tensor("PG0", [128, 512], F32))
            PG1 = s2a.enter_context(nc.psum_tensor("PG1", [128, 512], F32))
            if True:
                rPT2A, rPT2B = R("pt2a"), R("pt2b")
                PG, rPG = [PG0, PG1], [R("pg0"), R("pg1")]

                prefetch_expert(0, 0)
                sSI.issue(lambda s: s.dma_start(GWF, gwt_img),
                          writes=[rGWF])
                P.emit("vector", lambda v: v.tensor_copy(GW[:, 0:128], GWF),
                       reads=[rGWF], writes=[rGW], inc="ve")
                P.emit("vector", lambda v: v.tensor_sub(
                    GW[:, 128:256], GWF, GW[:, 0:128]),
                    reads=[rGWF], writes=[rGW], inc="ve")

                scr = lambda lane: SCRF32[:, lane * D:(lane + 1) * D]
                xlo = lambda lane: XLO[:, lane * D:(lane + 1) * D]

                def do_tt_ln(m):
                    lane = m % 2
                    ln_chain(acc(m), rACC[m], scr(lane), rSCR[lane], lane)
                    P.emit("vector", lambda v, lane=lane: v.tensor_copy(
                        xlnb(lane), scr(lane)),
                        reads=[rSCR[lane]], writes=[rXLNBF[lane]])
                    P.emit("vector", lambda v, lane=lane: v.tensor_sub(
                        xlo(lane), scr(lane), xlnb(lane)),
                        reads=[rSCR[lane], rXLNBF[lane]], writes=[rXLO[lane]],
                        inc="ve")

                def do_tt_tp(m):
                    lane = m % 2
                    sTP[m % 2].issue(lambda s: s.dma_start(
                        XF[:, :].rearrange("p (a b) -> p a b", b=NT)
                        [:, :, m * 128:(m + 1) * 128],
                        xlnb(lane)[:, :].rearrange("p (c q) -> p c q", q=128),
                        transpose=True),
                        reads=[rXLNBF[lane]], writes=[rXF])
                    sTP[2 + (m % 2)].issue(lambda s, lane=lane: s.dma_start(
                        XLOT[:, lane * D:(lane + 1) * D]
                        .rearrange("p (c q) -> p c q", q=128),
                        xlo(lane)[:, :].rearrange("p (c q) -> p c q", q=128),
                        transpose=True),
                        reads=[rXLO[lane]], writes=[rXLOT[lane]])

                def do_tt_gate(m):
                    lane = m % 2
                    pg, rpg = PG[m % 2], rPG[m % 2]
                    fns = []
                    n = 24
                    idx = 0
                    for (sn, goff) in (("hi", 0), ("lo", 0), ("hi", 128)):
                        for dch in range(8):
                            def mk(sn=sn, goff=goff, dch=dch, idx=idx, lane=lane,
                                   pg=pg):
                                def fn(t):
                                    lhs = (xf(dch, m * 128, 128) if sn == "hi"
                                           else XLOT[:, lane * D + dch * 128:
                                                     lane * D + (dch + 1) * 128])
                                    return t.matmul(
                                        pg[:, 0:16], lhs,
                                        GW[:, goff + dch * 16: goff + (dch + 1) * 16],
                                        start=(idx == 0), stop=(idx == n - 1))
                                return fn
                            fns.append(mk())
                            idx += 1
                    P.group("tensor", fns, reads=[rXF, rXLOT[lane], rGW],
                            writes=[rpg], inc="pe")
                    P.emit("vector", lambda v, pg=pg: v.tensor_copy(
                        LOG[:, m * 16:(m + 1) * 16], pg[:, 0:16]),
                        reads=[rpg], writes=[rLOG], inc="ve")

                for m in range(NSLOT):
                    if m >= 2:
                        do_tt_gate(m - 2)
                    do_tt_ln(m)
                    if m >= 1:
                        do_tt_tp(m - 1)
                do_tt_tp(NSLOT - 1)
                do_tt_gate(NSLOT - 2)
                do_tt_gate(NSLOT - 1)


                b3 = lambda a, inner: a.rearrange("p (a b) -> p a b", b=inner)
                P.emit("scalar", lambda s: s.activation(
                    EXPS[:, :], LOG[:, :], AF.Exp),
                    reads=[rLOG], writes=[rEXPS], inc="sc")
                P.emit("vector", lambda v: v.reduce_sum(
                    smc(C_GS, NSLOT), b3(EXPS[:, :], 16), axis=AX.X),
                    reads=[rEXPS], writes=[rGSUM])
                P.emit("vector", lambda v: v.reciprocal(
                    smc(C_GR, NSLOT), smc(C_GS, NSLOT)),
                    reads=[rGSUM], writes=[rGSUM])
                P.emit("vector", lambda v: v.tensor_tensor(
                    b3(GSC[:, :], 16), b3(EXPS[:, :], 16),
                    _bcast(smc(C_GR, NSLOT), 16), ALU.mult),
                    reads=[rEXPS, rGSUM], writes=[rGSC])
                P.emit("vector", lambda v: v.reduce_max(
                    GMAX, b3(GSC[:, :], 4), axis=AX.X),
                    reads=[rGSC], writes=[rGMAX])
                P.emit("vector", lambda v: v.reduce_max(
                    smc(C_GS, NSLOT), b3(GMAX, 4), axis=AX.X, negate=True),
                    reads=[rGMAX], writes=[rGSUM])
                P.emit("vector", lambda v: v.tensor_scalar(
                    smc(C_GR, NSLOT), smc(C_GS, NSLOT), -1.0, None, ALU.mult),
                    reads=[rGSUM], writes=[rGSUM])
                P.emit("vector", lambda v: v.tensor_tensor(
                    b3(GTH, 4), b3(GMAX, 4),
                    _bcast(smc(C_GR, NSLOT), 4), ALU.is_ge),
                    reads=[rGMAX, rGSUM], writes=[rGTH])
                P.emit("vector", lambda v: v.scalar_tensor_tensor(
                    WRK[:, 0:4 * NSLOT], GTH, -BIG, GMAX,
                    ALU.mult, ALU.add),
                    reads=[rGTH, rGMAX], writes=[rWRK])
                P.emit("vector", lambda v: v.reduce_max(
                    smc(C_GS, NSLOT), b3(WRK[:, 0:4 * NSLOT], 4), axis=AX.X,
                    negate=True),
                    reads=[rWRK], writes=[rGSUM])
                P.emit("vector", lambda v: v.tensor_scalar(
                    smc(C_GR, NSLOT), smc(C_GS, NSLOT), -1.0, None, ALU.mult),
                    reads=[rGSUM], writes=[rGSUM])
                P.emit("vector", lambda v: v.tensor_tensor(
                    b3(GTH, 4), b3(GMAX, 4),
                    _bcast(smc(C_GR, NSLOT), 4), ALU.is_ge),
                    reads=[rGMAX, rGSUM], writes=[rGTH])
                P.emit("vector", lambda v: v.tensor_tensor(
                    b3(MS[:, :], 4), b3(GSC[:, :], 4),
                    _bcast(GTH, 4), ALU.mult),
                    reads=[rGSC, rGTH], writes=[rMS])
                P.emit("vector", lambda v: v.tensor_copy(WRK[:, :], MS[:, :]),
                       reads=[rMS], writes=[rWRK])
                for _ in range(3):
                    P.emit("vector", lambda v: v.reduce_max(
                        smc(C_GS, NSLOT), b3(WRK[:, :], 16), axis=AX.X, negate=True),
                        reads=[rWRK], writes=[rGSUM])
                    P.emit("vector", lambda v: v.tensor_scalar(
                        smc(C_GR, NSLOT), smc(C_GS, NSLOT), -1.0, None, ALU.mult),
                        reads=[rGSUM], writes=[rGSUM])
                    P.emit("vector", lambda v: v.tensor_tensor(
                        b3(SEL[:, :], 16), b3(WRK[:, :], 16),
                        _bcast(smc(C_GR, NSLOT), 16), ALU.is_ge),
                        reads=[rWRK, rGSUM], writes=[rSEL])
                    P.emit("vector", lambda v: v.scalar_tensor_tensor(
                        WRK[:, :], SEL[:, :], -BIG, WRK[:, :], ALU.mult, ALU.add),
                        reads=[rSEL], writes=[rWRK])
                P.emit("vector", lambda v: v.reduce_max(
                    smc(C_GS, NSLOT), b3(WRK[:, :], 16), axis=AX.X, negate=True),
                    reads=[rWRK], writes=[rGSUM])
                P.emit("vector", lambda v: v.tensor_scalar(
                    smc(C_GR, NSLOT), smc(C_GS, NSLOT), -1.0, None, ALU.mult),
                    reads=[rGSUM], writes=[rGSUM])
                P.emit("vector", lambda v: v.tensor_tensor(
                    b3(SEL[:, :], 16), b3(MS[:, :], 16),
                    _bcast(smc(C_GR, NSLOT), 16), ALU.is_ge),
                    reads=[rMS, rGSUM], writes=[rSEL])
                P.emit("vector", lambda v: v.tensor_mul(
                    CC[:, :], GSC[:, :], SEL[:, :]),
                    reads=[rGSC, rSEL], writes=[rCC], inc="ve")
                P.flush_block()
            s2a.close()

            s2b = contextlib.ExitStack()
            H0 = s2b.enter_context(nc.psum_tensor("H0", [128, 1024], F32))
            H1 = s2b.enter_context(nc.psum_tensor("H1", [128, 1024], F32))
            H2 = s2b.enter_context(nc.psum_tensor("H2", [128, 1024], F32))
            H3 = s2b.enter_context(nc.psum_tensor("H3", [128, 1024], F32))
            if True:
                HH = [H0, H1, H2, H3]
                rH = [R("h0"), R("h1"), R("h2"), R("h3")]

                def do_expert(e):
                    pi = e % 2
                    if e + 1 < NEXP + 2:
                        prefetch_expert(e + 1, 1 - pi)

                    def w2_tt(tt):
                        yp, ryp = HH[tt % 2], rH[tt % 2]
                        fns = []
                        for ich in range(4):
                            for half in range(2):
                                fns.append(
                                    lambda t, ich=ich, half=half, tt=tt, yp=yp,
                                    pi=pi: t.matmul(
                                        yp[:, half * 512:(half + 1) * 512],
                                        APR[:, ich * NT + tt * 128:
                                            ich * NT + (tt + 1) * 128],
                                        W2B[:, pi * 4096 + ich * 1024 + half * 512:
                                            pi * 4096 + ich * 1024 + (half + 1) * 512],
                                        start=(ich == 0), stop=(ich == 3)))
                        P.group("tensor", fns, reads=[rAPR, rW2[pi]], writes=[ryp],
                                inc="pe")
                        if e < NEXP:
                            P.emit("vector",
                                   lambda v, tt=tt, yp=yp, e=e: v.scalar_tensor_tensor(
                                       acc(tt), yp[:, :],
                                       CC[:, tt * 16 + e: tt * 16 + e + 1],
                                       acc(tt), ALU.mult, ALU.add),
                                   reads=[ryp, rCC], writes=[rACC[tt]], inc="ve")
                        else:
                            P.emit("vector", lambda v, tt=tt, yp=yp: v.tensor_add(
                                acc(tt), acc(tt), yp[:, :]),
                                reads=[ryp], writes=[rACC[tt]], inc="ve")
                        if e == NEXP + 1:
                            sSO[tt % 4].issue(lambda s, tt=tt: s.dma_start(
                                out_d[tt], acc(tt)),
                                reads=[rACC[tt]])

                    for th in range(NTH):
                        for ich in range(4):
                            hp = (th * 4 + ich) % 2
                            pairs = ((HH[2 * hp], rH[2 * hp], W1B, rW1[pi]),
                                     (HH[2 * hp + 1], rH[2 * hp + 1], W3B, rW3[pi]))
                            for (ps, rps, wbuf, rwb) in pairs:
                                fns = []
                                for dch in range(8):
                                    for half in range(TCH // 512):
                                        fns.append(
                                            lambda t, dch=dch, half=half, ps=ps,
                                            wbuf=wbuf, ich=ich, th=th, pi=pi: t.matmul(
                                                ps[:, half * 512:(half + 1) * 512],
                                                wbuf[:, pi * 4096 + dch * 512 + ich * 128:
                                                     pi * 4096 + dch * 512 + (ich + 1) * 128],
                                                xf(dch, th * TCH + half * 512, 512),
                                                start=(dch == 0), stop=(dch == 7)))
                                P.group("tensor", fns, reads=[rwb, rXF],
                                        writes=[rps], inc="pe")
                            h1ps, rh1 = HH[2 * hp], rH[2 * hp]
                            h3ps, rh3 = HH[2 * hp + 1], rH[2 * hp + 1]
                            P.emit("scalar", lambda s, h1ps=h1ps: s.activation(
                                SCRF32[:, 0:TCH], h1ps[:, 0:TCH], AF.Silu),
                                reads=[rh1], writes=[rSCR[0]], inc="sc")
                            P.emit("vector",
                                   lambda v, h3ps=h3ps, ich=ich, th=th: v.tensor_mul(
                                       APR[:, ich * NT + th * TCH:
                                           ich * NT + (th + 1) * TCH],
                                       SCRF32[:, 0:TCH], h3ps[:, 0:TCH]),
                                   reads=[rSCR[0], rh3], writes=[rAPR], inc="ve")
                    for tt in range(NSLOT):
                        w2_tt(tt)

                for e in range(NEXP + 2):
                    do_expert(e)

                P.emit("sync", lambda s: s.nop(),
                       extra=[("d_so0", P.cnt["d_so0"], "dma"),
                              ("d_so1", P.cnt["d_so1"], "dma"),
                              ("d_so2", P.cnt["d_so2"], "dma"),
                              ("d_so3", P.cnt["d_so3"], "dma")])
                P.flush_block()
            s2b.close()
        s2.close()

    return nc



def _img_km(M, r):
    R_, C = M.shape
    assert R_ == r * 128
    return np.ascontiguousarray(M.reshape(r, 128, C).transpose(1, 0, 2))


_CACHED = {}


def _get_nc(attn_only=False):
    if attn_only not in _CACHED:
        _CACHED[attn_only] = build_program(attn_only)
    return _CACHED[attn_only]


def kernel(L0_attn_input_data, L0_attn_qk, L0_attn_ov, L0_ffn_gate_w,
           L0_ffn_w1, L0_ffn_w2, L0_ffn_w3, L0_ffn_ws1, L0_ffn_ws2, L0_ffn_ws3,
           _attn_only=False):
    x = np.ascontiguousarray(np.asarray(L0_attn_input_data, dtype=np.float32))
    qk = np.asarray(L0_attn_qk, dtype=np.float32)
    ov = np.asarray(L0_attn_ov, dtype=np.float32)
    gw = np.asarray(L0_ffn_gate_w, dtype=np.float32)
    w1 = np.asarray(L0_ffn_w1, dtype=np.float32)
    w2 = np.asarray(L0_ffn_w2, dtype=np.float32)
    w3 = np.asarray(L0_ffn_w3, dtype=np.float32)
    ws1 = np.asarray(L0_ffn_ws1, dtype=np.float32)
    ws2 = np.asarray(L0_ffn_ws2, dtype=np.float32)
    ws3 = np.asarray(L0_ffn_ws3, dtype=np.float32)

    qk_i = _img_km(qk, 8).reshape(128, 8 * D)
    ov_i = _img_km(ov, 8).reshape(128, 8 * D)
    gwt_i = _img_km(np.ascontiguousarray(gw.T), 8).reshape(128, 128)
    w1_i = np.stack([_img_km(w1[e], 8).reshape(128, 8 * INTER) for e in range(NEXP)])
    w3_i = np.stack([_img_km(w3[e], 8).reshape(128, 8 * INTER) for e in range(NEXP)])
    w2_i = np.stack([_img_km(w2[e], 4).reshape(128, 4 * D) for e in range(NEXP)])
    ws1_i = _img_km(ws1, 8)
    ws3_i = _img_km(ws3, 8)
    ws2_i = _img_km(ws2, 8)
    ident = np.eye(128, dtype=np.float32)

    in_maps = []
    for c in range(8):
        b, c4 = c // 4, c % 4
        xb = np.ascontiguousarray(x[b])
        xo = np.ascontiguousarray(xb.reshape(64, 128, D)[c4::4])
        jj = np.arange(512)[None, :]
        ii = np.arange(128)[:, None]
        mask = np.where(jj <= ii + 128 * c4, 0.0, -BIG).astype(np.float32)
        in_maps.append({
            "x_seq": xb, "x_own": xo, "mask": mask, "ident": ident,
            "qk_img": qk_i, "ov_img": ov_i, "gwt_img": gwt_i,
            "w1_img": w1_i, "w3_img": w3_i, "w2_img": w2_i,
            "ws1_img": ws1_i, "ws3_img": ws3_i, "ws2_img": ws2_i,
        })

    nc = _get_nc(_attn_only)
    res = run_bass_kernel_spmd(nc, in_maps, core_ids=list(range(8)))

    out = np.empty((2, SEQ, D), dtype=np.float32)
    for c in range(8):
        b, c4 = c // 4, c % 4
        out[b].reshape(64, 128, D)[c4::4] = res.results[c]["out"]
    return out


# version 19
# speedup vs baseline: 1.0260x; 1.0119x over previous
import numpy as np
import concourse.bass as bass
import concourse.mybir as mybir
from concourse.bass_utils import run_bass_kernel_spmd

F32 = mybir.dt.float32
BF16 = mybir.dt.bfloat16
AF = mybir.ActivationFunctionType
ALU = mybir.AluOpType
AX = mybir.AxisListType

D = 1024
SEQ = 8192
NSLOT = 16
NPANEL = 8
PKT = 8
NEXP = 16
INTER = 512
EPS = 1e-5
BIG = 1.0e30


class _Res:
    __slots__ = ("w", "r", "name")
    def __init__(self, name):
        self.name = name
        self.w = []
        self.r = []


class _Prog:
    ENGINES = ("tensor", "vector", "scalar", "sync", "gpsimd")

    def __init__(self, nc):
        self.nc = nc
        self.q = {e: [] for e in self.ENGINES}
        self.cnt = {}
        self.semh = {}
        self.waited = {e: {} for e in self.ENGINES}

    def sem(self, name, stack):
        self.semh[name] = stack.enter_context(self.nc.semaphore(name))
        self.cnt[name] = 0

    ESEM = {"tensor": "pe", "vector": "ve", "scalar": "sc", "gpsimd": "gp",
            "sync": None}

    def _resolve(self, eng, toks):
        out = {}
        for t in toks:
            s, v, src = t
            if s is None:
                if src == eng:
                    continue
                raise RuntimeError(
                    f"cross-engine dep on un-inc'd op (src={src}, consumer={eng})")
            out[s] = max(out.get(s, 0), v)
        return sorted(out.items())

    def _hazard_toks(self, reads, writes, extra):
        toks = list(extra)
        for r in reads:
            toks += r.w
        for w in writes:
            toks += w.w + w.r
        return toks

    def _record(self, tok, reads, writes):
        for r in reads:
            r.r.append(tok)
        for w in writes:
            w.w = [tok]
            w.r = []

    def emit(self, eng, fn, reads=(), writes=(), extra=(), inc="auto", inc_by=1):
        if inc == "auto":
            inc = self.ESEM[eng]
        waits = self._resolve(eng, self._hazard_toks(reads, writes, extra))
        if inc is not None:
            self.cnt[inc] += inc_by
            tok = (inc, self.cnt[inc], eng)
        else:
            tok = (None, 0, eng)
        self.q[eng].append((fn, waits, inc, inc_by))
        self._record(tok, reads, writes)
        return tok

    def group(self, eng, fns, reads=(), writes=(), extra=(), inc="auto"):
        if inc == "auto":
            inc = self.ESEM[eng]
        waits = self._resolve(eng, self._hazard_toks(reads, writes, extra))
        tok = (None, 0, eng)
        n = len(fns)
        for i, fn in enumerate(fns):
            s = inc if (i == n - 1 and inc is not None) else None
            if s is not None:
                self.cnt[s] += 1
                tok = (s, self.cnt[s], eng)
            self.q[eng].append((fn, waits if i == 0 else [], s, 1))
        self._record(tok, reads, writes)
        return tok

    def flush_block(self):
        nc = self.nc
        snap = dict(self.cnt)
        qs = {e: list(self.q[e]) for e in self.ENGINES}
        for e in self.ENGINES:
            self.q[e] = []

        def body(engobj, e):
            for fn, waits, inc, inc_by in qs[e]:
                for s, v in waits:
                    engobj.wait_ge(self.semh[s], v)
                inst = fn(engobj)
                if inc is not None:
                    inst.then_inc(self.semh[inc], inc_by)

        with nc.Block() as block:
            if qs["tensor"]:
                block.tensor(lambda t: body(t, "tensor"))
            if qs["vector"]:
                block.vector(lambda t: body(t, "vector"))
            if qs["scalar"]:
                block.scalar(lambda t: body(t, "scalar"))
            if qs["sync"]:
                block.sync(lambda t: body(t, "sync"))
            if qs["gpsimd"]:
                block.gpsimd(lambda t: body(t, "gpsimd"))
        del snap


class _Stream:
    def __init__(self, prog, sem, eng):
        self.P, self.sem, self.eng = prog, sem, eng

    def _idle_wait(self):
        if self.P.cnt[self.sem]:
            return [(self.sem, self.P.cnt[self.sem], "dma")]
        return []

    def issue(self, fn, reads=(), writes=()):
        return self.P.emit(self.eng, fn, reads=reads, writes=writes,
                           extra=self._idle_wait(), inc=self.sem, inc_by=16)

    def issue_batch(self, items):
        P = self.P
        extra = self._idle_wait()
        rws = []
        for i, (fn, reads, writes) in enumerate(items):
            waits = P._resolve(self.eng, P._hazard_toks(
                reads, writes, extra if i == 0 else ()))
            P.cnt[self.sem] += 16
            P.q[self.eng].append((fn, waits, self.sem, 16))
            rws.append((reads, writes))
        tok = (self.sem, P.cnt[self.sem], "dma")
        for reads, writes in rws:
            P._record(tok, reads, writes)
        return tok


def _bcast(ap, inner):
    return bass.AP(ap.tensor, ap.offset,
                   [list(ap.ap[0]), list(ap.ap[1]), [0, inner]])



def build_program(attn_only=False):
    import contextlib
    nc = bass.Bass("TRN2", target_bir_lowering=False)

    x_seq = nc.dram_tensor("x_seq", [SEQ, D], F32, kind="ExternalInput").ap()
    x_own = nc.dram_tensor("x_own", [NSLOT, 128, D], F32, kind="ExternalInput").ap()
    maskd = nc.dram_tensor("mask", [128, 512], F32, kind="ExternalInput").ap()
    identd = nc.dram_tensor("ident", [128, 128], F32, kind="ExternalInput").ap()
    qk_img = nc.dram_tensor("qk_img", [128, 8 * D], F32, kind="ExternalInput").ap()
    ov_img = nc.dram_tensor("ov_img", [128, 8 * D], F32, kind="ExternalInput").ap()
    gwt_img = nc.dram_tensor("gwt_img", [128, 128], F32, kind="ExternalInput").ap()
    w1_img = nc.dram_tensor("w1_img", [NEXP, 128, 8 * INTER], F32, kind="ExternalInput").ap()
    w3_img = nc.dram_tensor("w3_img", [NEXP, 128, 8 * INTER], F32, kind="ExternalInput").ap()
    w2_img = nc.dram_tensor("w2_img", [NEXP, 128, 4 * D], F32, kind="ExternalInput").ap()
    ws1_img = nc.dram_tensor("ws1_img", [128, 8, D], F32, kind="ExternalInput").ap()
    ws3_img = nc.dram_tensor("ws3_img", [128, 8, D], F32, kind="ExternalInput").ap()
    ws2_img = nc.dram_tensor("ws2_img", [128, 8, D], F32, kind="ExternalInput").ap()
    out_d = nc.dram_tensor("out", [NSLOT, 128, D], F32, kind="ExternalOutput").ap()

    P = _Prog(nc)
    NT = NSLOT * 128

    with contextlib.ExitStack() as stk:
        for s in ("d_si", "d_gi", "d_sx0", "d_sx1", "d_sx2", "d_sf",
                  "d_so0", "d_so1", "d_so2", "d_so3",
                  "d_sw0", "d_sw1", "d_tp0", "d_tp1", "d_tp2", "d_tp3",
                  "pe", "ve", "sc", "gp"):
            P.sem(s, stk)
        sSI = _Stream(P, "d_si", "sync")
        sGI = _Stream(P, "d_gi", "gpsimd")
        sSX = [_Stream(P, "d_sx0", "sync"), _Stream(P, "d_sx1", "sync"),
               _Stream(P, "d_sx2", "sync")]
        sSF = _Stream(P, "d_sf", "sync")
        sSO = [_Stream(P, "d_so0", "sync"), _Stream(P, "d_so1", "sync"),
               _Stream(P, "d_so2", "sync"), _Stream(P, "d_so3", "sync")]
        sSW = [_Stream(P, "d_sw0", "gpsimd"), _Stream(P, "d_sw1", "gpsimd")]
        sTP = [_Stream(P, "d_tp0", "sync"), _Stream(P, "d_tp1", "sync"),
               _Stream(P, "d_tp2", "sync"), _Stream(P, "d_tp3", "sync")]

        sb = lambda name, shape, dt: stk.enter_context(nc.sbuf_tensor(name, shape, dt))
        CONST = sb("CONST", [128, 4], F32)
        SM = sb("SM", [128, 96], F32)
        ACC = sb("ACC", [128, NSLOT * D], F32)
        STG = sb("STG", [128, 3 * D], F32)
        XOWNF = sb("XOWNF", [128, D], F32)
        IDB = sb("IDB", [128, 128], BF16)
        XLNBF = sb("XLNBF", [128, 2 * D], BF16)
        MASKBB = sb("MASKBB", [128, 512], BF16)

        acc = lambda m: ACC[:, m * D:(m + 1) * D]
        smc = lambda c, n=1: SM[:, c:c + n]
        stg = lambda i: STG[:, (i % 3) * D:((i % 3) + 1) * D]
        xlnb = lambda lane: XLNBF[:, lane * D:(lane + 1) * D]

        C_NM, C_LL = 0, 16
        LN_MSUM, LN_SSQ, LN_NMEAN, LN_NM2, LN_VAR, LN_RSTD, LN_NMR = range(7)
        C_NRM, C_TMIN, C_TMD, C_ALPHA, C_RSUM, C_RL = 48, 50, 52, 54, 57, 59
        C_GS, C_GR = 64, 80

        R = _Res
        rCONST = R("const")
        rLN = [R("ln0"), R("ln1")]
        rNM = [R(f"nm{m}") for m in range(NSLOT)]
        rLL = [R(f"ll{m}") for m in range(NSLOT)]
        rNRM = [R("nrm0"), R("nrm1")]
        rTMIN = [R("tmin0"), R("tmin1")]
        rTMD = [R("tmd0"), R("tmd1")]
        rALPHA = [R("al0"), R("al1"), R("al2")]
        rRSUM = [R("rs0"), R("rs1")]
        rRL = [R("rl0"), R("rl1")]
        rACC = [R(f"acc{m}") for m in range(NSLOT)]
        rSTG = [R("stg0"), R("stg1"), R("stg2")]
        rOVB, rMASK, rIDB = R("ovb"), R("mask"), R("idb")
        rMASKBB = R("maskbb")
        rXLNBF = [R("xlnbf0"), R("xlnbf1")]
        rSCR = [R("scr0"), R("scr1")]
        rXOWNs = R("xown")

        def ln_chain(src_ap, src_res, dst_ap, dst_res, lane, se="vector",
                     sum_on_scalar=False):
            rl = rLN[lane]
            base = 32 + 8 * lane
            c = lambda k: smc(base + k)
            dump = xlnb(lane)
            rdump = rXLNBF[lane]
            if sum_on_scalar:
                P.emit("scalar", lambda s: s.activation(
                    dump, src_ap, AF.Identity, accum_out=c(LN_MSUM)),
                    reads=[src_res], writes=[rl, rdump])
            else:
                P.emit(se, lambda v: v.tensor_scalar(
                    dump, src_ap, 1.0, 0.0, ALU.mult, ALU.add,
                    accum_out=c(LN_MSUM)),
                    reads=[src_res], writes=[rl, rdump])
            P.emit(se, lambda v: v.scalar_tensor_tensor(
                dump, src_ap, 1.0, src_ap, ALU.mult, ALU.mult,
                accum_out=c(LN_SSQ)),
                reads=[src_res], writes=[rl, rdump])
            P.emit(se, lambda v: v.tensor_scalar(
                c(LN_NMEAN), c(LN_MSUM), -1.0 / D, None, ALU.mult),
                reads=[rl], writes=[rl])
            P.emit(se, lambda v: v.scalar_tensor_tensor(
                c(LN_NM2), c(LN_NMEAN), -1.0, c(LN_NMEAN), ALU.mult, ALU.mult),
                reads=[rl], writes=[rl])
            P.emit(se, lambda v: v.scalar_tensor_tensor(
                c(LN_VAR), c(LN_SSQ), 1.0 / D, c(LN_NM2), ALU.mult, ALU.add),
                reads=[rl], writes=[rl])
            P.emit("scalar", lambda s: s.activation(
                c(LN_VAR), c(LN_VAR), AF.Ln, bias=CONST[:, 2:3]),
                reads=[rCONST, rl], writes=[rl])
            P.emit("scalar", lambda s: s.activation(
                c(LN_RSTD), c(LN_VAR), AF.Exp, scale=CONST[:, 1:2]),
                reads=[rCONST, rl], writes=[rl])
            P.emit("scalar", lambda s: s.activation(
                c(LN_NMR), c(LN_RSTD), AF.Identity, scale=c(LN_NMEAN)),
                reads=[rl], writes=[rl])
            wl = [dst_res] if dst_res is not None else []
            tnorm = P.emit("scalar", lambda s: s.activation(
                dst_ap, src_ap, AF.Identity, scale=c(LN_RSTD), bias=c(LN_NMR)),
                reads=[src_res, rl], writes=wl)
            return tnorm

        P.emit("gpsimd", lambda g: g.memset(SM[:, :], 0.0), writes=[rLN[0], rLN[1]])
        P.emit("gpsimd", lambda g: g.memset(CONST[:, 0:1], 1.0 / 32.0), writes=[rCONST])
        P.emit("gpsimd", lambda g: g.memset(CONST[:, 1:2], -0.5), writes=[rCONST])
        P.emit("gpsimd", lambda g: g.memset(CONST[:, 2:3], EPS), writes=[rCONST], inc="gp")

        with (
            nc.sbuf_tensor("Q", [128, 8 * NT], BF16) as Q,
            nc.sbuf_tensor("OVB", [128, 8 * D], BF16) as OVB,
        ):
            rQ = R("Q")
            qsl = lambda dch, lo, n: Q[:, dch * NT + lo: dch * NT + lo + n]

            with (
                nc.sbuf_tensor("XLNT", [128, 8 * NT], BF16) as XLNT,
                nc.sbuf_tensor("QKB", [128, 8 * D], BF16) as QKB,
                nc.psum_tensor("PSA", [128, 512], F32) as PSA,
                nc.psum_tensor("PSB", [128, 512], F32) as PSB,
                nc.psum_tensor("PSC", [128, 512], F32) as PSC,
                nc.psum_tensor("PSD", [128, 512], F32) as PSD,
                nc.psum_tensor("PSE", [128, 512], F32) as PSE,
                nc.psum_tensor("PSF", [128, 512], F32) as PSF,
            ):
                rXLNT, rQKB = R("XLNT"), R("QKB")
                QPS = [(PSA, R("PSA")), (PSB, R("PSB")),
                       (PSC, R("PSC")), (PSD, R("PSD")),
                       (PSE, R("PSE")), (PSF, R("PSF"))]
                sGI.issue_batch([
                    (lambda g: g.dma_start(IDB[:, :], identd), (), [rIDB]),
                    (lambda g: g.dma_start(MASKBB[:, :], maskd), (), [rMASKBB]),
                    (lambda g: g.dma_start(OVB[:, :], ov_img), (), [rOVB]),
                    (lambda g: g.dma_start(QKB[:, :], qk_img), (), [rQKB]),
                ])

                def pre_ln(m):
                    lane = m % 2
                    sSX[m % 3].issue(lambda s: s.dma_start(stg(m), x_own[m]),
                                     writes=[rSTG[m % 3]])
                    ln_chain(stg(m), rSTG[m % 3], xlnb(lane), rXLNBF[lane], lane)

                def pre_tp(m):
                    lane = m % 2
                    sTP[m % 4].issue(lambda s: s.dma_start(
                        XLNT[:, :].rearrange("p (a b) -> p a b", b=NT)
                        [:, :, m * 128:(m + 1) * 128],
                        xlnb(lane)[:, :].rearrange("p (c q) -> p c q", q=128),
                        transpose=True),
                        reads=[rXLNBF[lane]], writes=[rXLNT])

                NSG = NT // 512

                def q_group(g):
                    ech, sg = g // NSG, g % NSG
                    bank, rbank = QPS[g % 6]
                    fns = [(lambda t, dch=dch: t.matmul(
                        bank[:, 0:512],
                        QKB[:, dch * D + ech * 128: dch * D + (ech + 1) * 128],
                        XLNT[:, dch * NT + sg * 512: dch * NT + (sg + 1) * 512],
                        start=(dch == 0), stop=(dch == 7))) for dch in range(8)]
                    P.group("tensor", fns, reads=[rQKB, rXLNT], writes=[rbank],
                            inc="pe")
                    if g % 2 == 0:
                        P.emit("scalar", lambda s: s.activation(
                            qsl(ech, sg * 512, 512), bank[:, 0:512], AF.Identity,
                            scale=CONST[:, 0:1]),
                            reads=[rbank, rCONST], writes=[rQ], inc="sc")
                    else:
                        P.emit("vector", lambda v: v.tensor_scalar_mul(
                            qsl(ech, sg * 512, 512), bank[:, 0:512],
                            CONST[:, 0:1]),
                            reads=[rbank, rCONST], writes=[rQ], inc="ve")

                qready = []
                qi = 0
                for m in range(NSLOT):
                    pre_ln(m)
                    if m >= 1:
                        pre_tp(m - 1)
                    if m % 4 == 3 and m >= 7:
                        sg = (m - 4) // 4
                        qready += [ech * NSG + sg for ech in range(8)]
                    while qi < len(qready) and qi < max(0, (m - 6)) * 3:
                        q_group(qready[qi])
                        qi += 1
                pre_tp(NSLOT - 1)
                done = set(qready[:qi])
                rest = [g for g in qready if g not in done]
                rest += [g for g in range(8 * NSG) if g not in done and g not in rest]
                for g in rest:
                    q_group(g)
                P.flush_block()

            with (
                nc.sbuf_tensor("KP0", [128, 8 * 1024], BF16) as KP0,
                nc.sbuf_tensor("KP1", [128, 8 * 1024], BF16) as KP1,
                nc.sbuf_tensor("VP0", [128, 8 * 1024], BF16) as VP0,
                nc.sbuf_tensor("VP1", [128, 8 * 1024], BF16) as VP1,
                nc.sbuf_tensor("PBF", [128, 2048], BF16) as PBF,
                nc.sbuf_tensor("PTS", [128, 3072], BF16) as PTS,
                nc.psum_tensor("PS_SE", [128, 1024], F32) as PS_SE,
                nc.psum_tensor("PS_SO", [128, 1024], F32) as PS_SO,
                nc.psum_tensor("PS_PV", [128, 1024], F32) as PS_PV,
                nc.psum_tensor("PSPT2", [128, 1024], BF16) as PSPT2,
                nc.psum_tensor("PSPT2B", [128, 1024], BF16) as PSPT2B,
            ):
                KP, VP = [KP0, KP1], [VP0, VP1]
                rKP = [R("kp0"), R("kp1")]
                rVP = [R("vp0"), R("vp1")]
                rVPk = [[R(f"vp{b}k{k}") for k in range(PKT)] for b in range(2)]
                rPBF = [R("pbf0"), R("pbf1")]
                rPTS = [R("pts0"), R("pts1"), R("pts2")]
                FINBF, FINT = PBF, PTS
                rFINBF, rFINT = rPBF[0], rPTS[0]
                rSE, rSO, rPV, rPT2 = R("se"), R("so"), R("pv"), R("pt2")
                rPT2B = R("pt2b")

                def kslice(buf, dch, lo, n):
                    return KP[buf][:, dch * 1024 + lo: dch * 1024 + lo + n]

                def vslice(buf, kt, lo, n):
                    return VP[buf][:, kt * 1024 + lo: kt * 1024 + lo + n]

                def emit_prod_load(p, kt):
                    buf = p % 2
                    gi = p * PKT + kt
                    sSX[gi % 3].issue(lambda s: s.dma_start(
                        stg(gi),
                        x_seq[p * 1024 + kt * 128: p * 1024 + (kt + 1) * 128, :]),
                        writes=[rSTG[gi % 3]])
                    ln_chain(stg(gi), rSTG[gi % 3],
                             vslice(buf, kt, 0, 1024), rVPk[buf][kt], kt % 2)

                def emit_prod_tp(p, kt):
                    buf = p % 2
                    sTP[kt % 4].issue(lambda s: s.dma_start(
                        KP[buf][:, :].rearrange("p (a b) -> p a b", b=1024)
                        [:, :, kt * 128:(kt + 1) * 128],
                        vslice(buf, kt, 0, 1024)
                        .rearrange("p (c q) -> p c q", q=128),
                        transpose=True),
                        reads=[rVPk[buf][kt]], writes=[rKP[buf]])

                def emit_qk(p, buf, i, m):
                    nkt = 4 if m == 2 * p else 8
                    nbank = nkt // 4
                    mask_bank = (0 if m == 2 * p else
                                 (1 if m == 2 * p + 1 else None))
                    SP, rSP = ((PS_SE, rSE) if i % 2 == 0 else (PS_SO, rSO))
                    fns = []
                    if mask_bank is not None:
                        fns.append(lambda t, mb=mask_bank: t.matmul(
                            SP[:, mb * 512:(mb + 1) * 512],
                            IDB[:, :], MASKBB[:, :],
                            start=True, stop=False))
                    for bank in range(nbank):
                        for dch in range(8):
                            st = (dch == 0) and (bank != mask_bank)
                            sp_ = (dch == 7)
                            fns.append(lambda t, bank=bank, dch=dch, st=st,
                                       sp_=sp_: t.matmul(
                                SP[:, bank * 512:(bank + 1) * 512],
                                qsl(dch, m * 128, 128),
                                kslice(buf, dch, bank * 512, 512),
                                start=st, stop=sp_))
                    rd = [rQ, rKP[buf]]
                    if mask_bank is not None:
                        rd += [rIDB, rMASKBB]
                    P.group("tensor", fns, reads=rd, writes=[rSP], inc="pe")
                    return nkt, SP, rSP

                def emit_max(p, i, m, nkt, SP, rSP):
                    par = i % 2
                    ext = nkt * 128
                    P.emit("vector", lambda v: v.reduce_max(
                        smc(C_NRM + par), SP[:, 0:ext], axis=AX.X, negate=True),
                        reads=[rSP], writes=[rNRM[par]], inc="ve")
                    if p == 0:
                        P.emit("vector", lambda v: v.tensor_copy(
                            smc(C_NM + m), smc(C_NRM + par)),
                            reads=[rNRM[par]], writes=[rNM[m]], inc="ve")
                    else:
                        P.emit("vector", lambda v: v.tensor_sub(
                            smc(C_TMD + par), smc(C_NRM + par), smc(C_NM + m)),
                            reads=[rNRM[par], rNM[m]], writes=[rTMD[par]])
                        P.emit("vector", lambda v: v.tensor_scalar_min(
                            smc(C_TMD + par), smc(C_TMD + par), 0.0),
                            reads=[rTMD[par]], writes=[rTMD[par]])
                        P.emit("vector", lambda v: v.tensor_add(
                            smc(C_NM + m), smc(C_NM + m), smc(C_TMD + par)),
                            reads=[rTMD[par], rNM[m]], writes=[rNM[m]], inc="ve")

                def emit_exp(p, i, m, nkt, SP, rSP):
                    par = i % 2
                    al = i % 3
                    ext = nkt * 128
                    if p > 0:
                        P.emit("scalar", lambda s: s.activation(
                            smc(C_ALPHA + al), smc(C_TMD + par), AF.Exp),
                            reads=[rTMD[par]], writes=[rALPHA[al]], inc="sc")
                    P.emit("scalar", lambda s: s.activation(
                        PBF[:, par * 1024: par * 1024 + ext], SP[:, 0:ext], AF.Exp,
                        bias=smc(C_NM + m), accum_out=smc(C_RSUM + par)),
                        reads=[rSP, rNM[m]], writes=[rPBF[par], rRSUM[par]],
                        inc="sc")

                def emit_llupd(p, i, m):
                    par = i % 2
                    al = i % 3
                    if p == 0:
                        P.emit("vector", lambda v: v.tensor_copy(
                            smc(C_LL + m), smc(C_RSUM + par)),
                            reads=[rRSUM[par]], writes=[rLL[m]], inc="ve")
                    else:
                        P.emit("vector", lambda v: v.scalar_tensor_tensor(
                            smc(C_LL + m), smc(C_LL + m), smc(C_ALPHA + al),
                            smc(C_RSUM + par), ALU.mult, ALU.add),
                            reads=[rRSUM[par], rALPHA[al]], writes=[rLL[m]],
                            inc="ve")

                def emit_t(p, buf, i, m, nkt):
                    par = i % 2
                    sl3 = i % 3
                    fns = [(lambda t, kt=kt: t.transpose(
                        PSPT2[:, kt * 128:(kt + 1) * 128],
                        PBF[:, par * 1024 + kt * 128: par * 1024 + (kt + 1) * 128],
                        IDB[:, :])) for kt in range(nkt)]
                    P.group("tensor", fns, reads=[rPBF[par], rIDB], writes=[rPT2],
                            inc="pe")
                    P.emit("scalar", lambda s: s.copy(
                        PTS[:, sl3 * 1024: sl3 * 1024 + nkt * 128],
                        PSPT2[:, 0:nkt * 128]),
                        reads=[rPT2], writes=[rPTS[sl3]], inc="sc")

                def emit_pv(p, buf, i, m, nkt):
                    al = i % 3
                    sl3 = i % 3
                    fns = []
                    for kt in range(nkt):
                        for half in range(2):
                            fns.append(lambda t, kt=kt, half=half: t.matmul(
                                PS_PV[:, half * 512:(half + 1) * 512],
                                PTS[:, sl3 * 1024 + kt * 128: sl3 * 1024 + (kt + 1) * 128],
                                vslice(buf, kt, half * 512, 512),
                                start=(kt == 0), stop=(kt == nkt - 1)))
                    P.group("tensor", fns,
                            reads=[rPTS[sl3]] + [rVPk[buf][kt] for kt in range(nkt)],
                            writes=[rPV], inc="pe")
                    if p == 0:
                        P.emit("vector", lambda v: v.tensor_copy(acc(m), PS_PV[:, :]),
                               reads=[rPV], writes=[rACC[m]], inc="ve")
                    else:
                        P.emit("vector", lambda v: v.scalar_tensor_tensor(
                            acc(m), acc(m), smc(C_ALPHA + al), PS_PV[:, :],
                            ALU.mult, ALU.add),
                            reads=[rPV, rALPHA[al]], writes=[rACC[m]], inc="ve")

                def emit_finalize(ma, mb):
                    items = ((ma, 0, PS_SE, rSE), (mb, 1, PS_SO, rSO))
                    sSF.issue(lambda s: s.dma_start(XOWNF[:, :], x_own[ma]),
                              writes=[rXOWNs])
                    for m, h, _, _ in items:
                        P.emit("vector", lambda v, m=m, h=h: v.reciprocal(
                            smc(C_RL + h), smc(C_LL + m)),
                            reads=[rLL[m]], writes=[rRL[h]], inc="ve")
                    for m, h, _, _ in items:
                        P.emit("scalar", lambda s, m=m, h=h: s.activation(
                            PBF[:, h * 1024:(h + 1) * 1024], acc(m), AF.Identity,
                            scale=smc(C_RL + h)),
                            reads=[rACC[m], rRL[h]], writes=[rPBF[h]], inc="sc")
                    for m, h, _, _ in items:
                        fns = [(lambda t, dch=dch, h=h: t.transpose(
                            PSPT2B[:, dch * 128:(dch + 1) * 128] if h == 0 else
                            PSPT2[:, dch * 128:(dch + 1) * 128],
                            PBF[:, h * 1024 + dch * 128: h * 1024 + (dch + 1) * 128],
                            IDB[:, :])) for dch in range(8)]
                        P.group("tensor", fns, reads=[rPBF[h], rIDB],
                                writes=[rPT2B if h == 0 else rPT2], inc="pe")
                        P.emit("scalar", lambda s, h=h: s.copy(
                            PTS[:, h * 1024:(h + 1) * 1024],
                            PSPT2B[:, :] if h == 0 else PSPT2[:, :]),
                            reads=[rPT2B if h == 0 else rPT2],
                            writes=[rPTS[h]], inc="sc")
                    for m, h, PSX, rX in items:
                        fns = []
                        for half2 in range(2):
                            for dch in range(8):
                                fns.append(lambda t, dch=dch, half2=half2,
                                           h=h, PSX=PSX: t.matmul(
                                    PSX[:, half2 * 512:(half2 + 1) * 512],
                                    PTS[:, h * 1024 + dch * 128: h * 1024 + (dch + 1) * 128],
                                    OVB[:, dch * D + half2 * 512: dch * D + (half2 + 1) * 512],
                                    start=(dch == 0), stop=(dch == 7)))
                        P.group("tensor", fns, reads=[rPTS[h], rOVB],
                                writes=[rX], inc="pe")
                        if h == 0:
                            P.emit("vector", lambda v, m=m, PSX=PSX: v.tensor_add(
                                acc(m), XOWNF[:, :], PSX[:, :]),
                                reads=[rXOWNs, rX], writes=[rACC[m]], inc="ve")
                            sSF.issue(lambda s, mb=mb: s.dma_start(
                                XOWNF[:, :], x_own[mb]),
                                writes=[rXOWNs])
                        else:
                            P.emit("vector", lambda v, m=m, PSX=PSX: v.tensor_add(
                                acc(m), XOWNF[:, :], PSX[:, :]),
                                reads=[rXOWNs, rX], writes=[rACC[m]], inc="ve")

                for kt in range(PKT):
                    emit_prod_load(0, kt)
                    if kt >= 1:
                        emit_prod_tp(0, kt - 1)
                emit_prod_tp(0, PKT - 1)

                def do_panel(p):
                    buf = p % 2
                    slots = list(range(2 * p, NSLOT))
                    prod = [(p + 1, kt) for kt in range(PKT)] if p < NPANEL - 1 else []
                    nld = ntp = 0
                    n = len(slots)
                    hist = []
                    for i, m in enumerate(slots):
                        nkt, SP, rSP = emit_qk(p, buf, i, m)
                        hist.append((p, buf, i, m, nkt, SP, rSP))
                        emit_max(p, i, m, nkt, SP, rSP)
                        if i >= 1:
                            emit_t(*hist[i - 1][:5])
                        emit_exp(p, i, m, nkt, SP, rSP)
                        if i >= 2:
                            emit_pv(*hist[i - 2][:5])
                        if i >= 1:
                            emit_llupd(p, i - 1, hist[i - 1][3])
                        want_ld = (len(prod) * (i + 1) + n - 1) // n
                        while nld < min(want_ld, len(prod)):
                            emit_prod_load(*prod[nld])
                            nld += 1
                        while ntp < nld - 4:
                            emit_prod_tp(*prod[ntp])
                            ntp += 1
                    emit_t(*hist[-1][:5])
                    if len(hist) >= 2:
                        emit_pv(*hist[-2][:5])
                    emit_pv(*hist[-1][:5])
                    emit_llupd(p, hist[-1][2], hist[-1][3])
                    while nld < len(prod):
                        emit_prod_load(*prod[nld])
                        nld += 1
                    while ntp < len(prod):
                        emit_prod_tp(*prod[ntp])
                        ntp += 1
                    emit_finalize(2 * p, 2 * p + 1)

                for p in range(NPANEL):
                    do_panel(p)
                P.flush_block()

        if attn_only:
            dbg = nc.dram_tensor("dbg", [128, 96 + 512], F32,
                                 kind="ExternalOutput").ap()
            sSO[0].issue(lambda s: s.dma_start(dbg[:, 0:96], SM[:, :]))
            for m in range(NSLOT):
                sSO[m % 2].issue(lambda s, m=m: s.dma_start(out_d[m], acc(m)),
                                 reads=[rACC[m]])
            P.emit("sync", lambda s: s.nop(),
                   extra=[("d_so0", P.cnt["d_so0"], "dma"),
                          ("d_so1", P.cnt["d_so1"], "dma")])
            P.flush_block()
            return nc

        GT = 16 * NSLOT
        TCH = min(1024, NT)
        NTH = NT // TCH
        s2 = contextlib.ExitStack()
        XF = s2.enter_context(nc.sbuf_tensor("XF", [128, 8 * NT], BF16))
        SCRF32 = s2.enter_context(nc.sbuf_tensor("SCRF32", [128, 2 * D], F32))
        W1B = s2.enter_context(nc.sbuf_tensor("W1B", [128, 2 * 4096], BF16))
        W3B = s2.enter_context(nc.sbuf_tensor("W3B", [128, 2 * 4096], BF16))
        W2B = s2.enter_context(nc.sbuf_tensor("W2B", [128, 2 * 4096], BF16))
        APR = s2.enter_context(nc.sbuf_tensor("APR", [128, 4 * NT], BF16))
        CC = s2.enter_context(nc.sbuf_tensor("CC", [128, GT], F32))
        LOG = s2.enter_context(nc.sbuf_tensor("LOG", [128, GT], F32))
        EXPS = s2.enter_context(nc.sbuf_tensor("EXPS", [128, GT], F32))
        GSC = s2.enter_context(nc.sbuf_tensor("GSC", [128, GT], F32))
        MS = LOG
        WRK = s2.enter_context(nc.sbuf_tensor("WRK", [128, GT], F32))
        SEL = EXPS
        GMAX = WRK[:, 2 * 4 * NSLOT: 3 * 4 * NSLOT]
        GTH = WRK[:, 3 * 4 * NSLOT: 4 * 4 * NSLOT]
        GW = s2.enter_context(nc.sbuf_tensor("GW", [128, 256], BF16))
        XLO = s2.enter_context(nc.sbuf_tensor("XLO", [128, 2048], BF16))
        XLOT = s2.enter_context(nc.sbuf_tensor("XLOT", [128, 2048], BF16))
        if True:
            rXF, rAPR, rCC, rLOG, rEXPS, rGSC = (
                R("xf"), R("apr"), R("cc"), R("log"), R("exps"), R("gsc"))
            rWRK = R("wrk")
            rMS = rLOG
            rGMAX = rGTH = rWRK
            rSEL = rEXPS
            rW1 = [R("w1a"), R("w1b")]
            rW3 = [R("w3a"), R("w3b")]
            rW2 = [R("w2a"), R("w2b")]
            GWF = SCRF32[:, 0:128]
            rGW = R("gw")
            rGWF = rSCR[0]
            rXLOT = [R("xlot0"), R("xlot1")]
            rXLO = [R("xlo0"), R("xlo1")]
            rGSUM = R("gsum")

            def xf(dch, lo, n):
                return XF[:, dch * NT + lo: dch * NT + lo + n]

            def prefetch_expert(e, pi):
                d1 = W1B[:, pi * 4096:(pi + 1) * 4096]
                d3 = W3B[:, pi * 4096:(pi + 1) * 4096]
                d2 = W2B[:, pi * 4096:(pi + 1) * 4096]
                if e < NEXP:
                    s1, s3, s2 = w1_img[e], w3_img[e], w2_img[e]
                else:
                    h = e - NEXP
                    s1 = ws1_img[:, :, h * 512:(h + 1) * 512]
                    s3 = ws3_img[:, :, h * 512:(h + 1) * 512]
                    s2 = ws2_img[:, h * 4:(h + 1) * 4, :]
                    d1 = d1.rearrange("p (a b) -> p a b", b=512)
                    d3 = d3.rearrange("p (a b) -> p a b", b=512)
                    d2 = d2.rearrange("p (a b) -> p a b", b=1024)
                sSW[pi].issue_batch([
                    (lambda g: g.dma_start(d1, s1), (), [rW1[pi]]),
                    (lambda g: g.dma_start(d3, s3), (), [rW3[pi]]),
                    (lambda g: g.dma_start(d2, s2), (), [rW2[pi]]),
                ])

            s2a = contextlib.ExitStack()
            PT2A = s2a.enter_context(nc.psum_tensor("PT2A", [128, 1024], BF16))
            PT2B = s2a.enter_context(nc.psum_tensor("PT2B", [128, 1024], BF16))
            PG0 = s2a.enter_context(nc.psum_tensor("PG0", [128, 512], F32))
            PG1 = s2a.enter_context(nc.psum_tensor("PG1", [128, 512], F32))
            if True:
                rPT2A, rPT2B = R("pt2a"), R("pt2b")
                PG, rPG = [PG0, PG1], [R("pg0"), R("pg1")]

                prefetch_expert(0, 0)
                sSI.issue(lambda s: s.dma_start(GWF, gwt_img),
                          writes=[rGWF])
                P.emit("vector", lambda v: v.tensor_copy(GW[:, 0:128], GWF),
                       reads=[rGWF], writes=[rGW], inc="ve")
                P.emit("vector", lambda v: v.tensor_sub(
                    GW[:, 128:256], GWF, GW[:, 0:128]),
                    reads=[rGWF], writes=[rGW], inc="ve")

                scr = lambda lane: SCRF32[:, lane * D:(lane + 1) * D]
                xlo = lambda lane: XLO[:, lane * D:(lane + 1) * D]

                def do_tt_ln(m):
                    lane = m % 2
                    ln_chain(acc(m), rACC[m], scr(lane), rSCR[lane], lane,
                             sum_on_scalar=True)
                    P.emit("scalar", lambda s, lane=lane: s.copy(
                        xlnb(lane), scr(lane)),
                        reads=[rSCR[lane]], writes=[rXLNBF[lane]])
                    P.emit("vector", lambda v, lane=lane: v.tensor_sub(
                        xlo(lane), scr(lane), xlnb(lane)),
                        reads=[rSCR[lane], rXLNBF[lane]], writes=[rXLO[lane]],
                        inc="ve")

                def do_tt_tp(m):
                    lane = m % 2
                    sTP[m % 2].issue(lambda s: s.dma_start(
                        XF[:, :].rearrange("p (a b) -> p a b", b=NT)
                        [:, :, m * 128:(m + 1) * 128],
                        xlnb(lane)[:, :].rearrange("p (c q) -> p c q", q=128),
                        transpose=True),
                        reads=[rXLNBF[lane]], writes=[rXF])
                    sTP[2 + (m % 2)].issue(lambda s, lane=lane: s.dma_start(
                        XLOT[:, lane * D:(lane + 1) * D]
                        .rearrange("p (c q) -> p c q", q=128),
                        xlo(lane)[:, :].rearrange("p (c q) -> p c q", q=128),
                        transpose=True),
                        reads=[rXLO[lane]], writes=[rXLOT[lane]])

                def do_tt_gate(m):
                    lane = m % 2
                    pg, rpg = PG[m % 2], rPG[m % 2]
                    fns = []
                    n = 24
                    idx = 0
                    for (sn, goff) in (("hi", 0), ("lo", 0), ("hi", 128)):
                        for dch in range(8):
                            def mk(sn=sn, goff=goff, dch=dch, idx=idx, lane=lane,
                                   pg=pg):
                                def fn(t):
                                    lhs = (xf(dch, m * 128, 128) if sn == "hi"
                                           else XLOT[:, lane * D + dch * 128:
                                                     lane * D + (dch + 1) * 128])
                                    return t.matmul(
                                        pg[:, 0:16], lhs,
                                        GW[:, goff + dch * 16: goff + (dch + 1) * 16],
                                        start=(idx == 0), stop=(idx == n - 1))
                                return fn
                            fns.append(mk())
                            idx += 1
                    P.group("tensor", fns, reads=[rXF, rXLOT[lane], rGW],
                            writes=[rpg], inc="pe")
                    P.emit("vector", lambda v, pg=pg: v.tensor_copy(
                        LOG[:, m * 16:(m + 1) * 16], pg[:, 0:16]),
                        reads=[rpg], writes=[rLOG], inc="ve")

                for m in range(NSLOT):
                    if m >= 2:
                        do_tt_gate(m - 2)
                    do_tt_ln(m)
                    if m >= 1:
                        do_tt_tp(m - 1)
                do_tt_tp(NSLOT - 1)
                do_tt_gate(NSLOT - 2)
                do_tt_gate(NSLOT - 1)


                b3 = lambda a, inner: a.rearrange("p (a b) -> p a b", b=inner)
                P.emit("scalar", lambda s: s.activation(
                    EXPS[:, :], LOG[:, :], AF.Exp),
                    reads=[rLOG], writes=[rEXPS], inc="sc")
                P.emit("vector", lambda v: v.reduce_sum(
                    smc(C_GS, NSLOT), b3(EXPS[:, :], 16), axis=AX.X),
                    reads=[rEXPS], writes=[rGSUM])
                P.emit("vector", lambda v: v.reciprocal(
                    smc(C_GR, NSLOT), smc(C_GS, NSLOT)),
                    reads=[rGSUM], writes=[rGSUM])
                P.emit("vector", lambda v: v.tensor_tensor(
                    b3(GSC[:, :], 16), b3(EXPS[:, :], 16),
                    _bcast(smc(C_GR, NSLOT), 16), ALU.mult),
                    reads=[rEXPS, rGSUM], writes=[rGSC])
                P.emit("vector", lambda v: v.reduce_max(
                    GMAX, b3(GSC[:, :], 4), axis=AX.X),
                    reads=[rGSC], writes=[rGMAX])
                P.emit("vector", lambda v: v.reduce_max(
                    smc(C_GS, NSLOT), b3(GMAX, 4), axis=AX.X, negate=True),
                    reads=[rGMAX], writes=[rGSUM])
                P.emit("vector", lambda v: v.tensor_scalar(
                    smc(C_GR, NSLOT), smc(C_GS, NSLOT), -1.0, None, ALU.mult),
                    reads=[rGSUM], writes=[rGSUM])
                P.emit("vector", lambda v: v.tensor_tensor(
                    b3(GTH, 4), b3(GMAX, 4),
                    _bcast(smc(C_GR, NSLOT), 4), ALU.is_ge),
                    reads=[rGMAX, rGSUM], writes=[rGTH])
                P.emit("vector", lambda v: v.scalar_tensor_tensor(
                    WRK[:, 0:4 * NSLOT], GTH, -BIG, GMAX,
                    ALU.mult, ALU.add),
                    reads=[rGTH, rGMAX], writes=[rWRK])
                P.emit("vector", lambda v: v.reduce_max(
                    smc(C_GS, NSLOT), b3(WRK[:, 0:4 * NSLOT], 4), axis=AX.X,
                    negate=True),
                    reads=[rWRK], writes=[rGSUM])
                P.emit("vector", lambda v: v.tensor_scalar(
                    smc(C_GR, NSLOT), smc(C_GS, NSLOT), -1.0, None, ALU.mult),
                    reads=[rGSUM], writes=[rGSUM])
                P.emit("vector", lambda v: v.tensor_tensor(
                    b3(GTH, 4), b3(GMAX, 4),
                    _bcast(smc(C_GR, NSLOT), 4), ALU.is_ge),
                    reads=[rGMAX, rGSUM], writes=[rGTH])
                P.emit("vector", lambda v: v.tensor_tensor(
                    b3(MS[:, :], 4), b3(GSC[:, :], 4),
                    _bcast(GTH, 4), ALU.mult),
                    reads=[rGSC, rGTH], writes=[rMS])
                P.emit("vector", lambda v: v.tensor_copy(WRK[:, :], MS[:, :]),
                       reads=[rMS], writes=[rWRK])
                for _ in range(3):
                    P.emit("vector", lambda v: v.reduce_max(
                        smc(C_GS, NSLOT), b3(WRK[:, :], 16), axis=AX.X, negate=True),
                        reads=[rWRK], writes=[rGSUM])
                    P.emit("vector", lambda v: v.tensor_scalar(
                        smc(C_GR, NSLOT), smc(C_GS, NSLOT), -1.0, None, ALU.mult),
                        reads=[rGSUM], writes=[rGSUM])
                    P.emit("vector", lambda v: v.tensor_tensor(
                        b3(SEL[:, :], 16), b3(WRK[:, :], 16),
                        _bcast(smc(C_GR, NSLOT), 16), ALU.is_ge),
                        reads=[rWRK, rGSUM], writes=[rSEL])
                    P.emit("vector", lambda v: v.scalar_tensor_tensor(
                        WRK[:, :], SEL[:, :], -BIG, WRK[:, :], ALU.mult, ALU.add),
                        reads=[rSEL], writes=[rWRK])
                P.emit("vector", lambda v: v.reduce_max(
                    smc(C_GS, NSLOT), b3(WRK[:, :], 16), axis=AX.X, negate=True),
                    reads=[rWRK], writes=[rGSUM])
                P.emit("vector", lambda v: v.tensor_scalar(
                    smc(C_GR, NSLOT), smc(C_GS, NSLOT), -1.0, None, ALU.mult),
                    reads=[rGSUM], writes=[rGSUM])
                P.emit("vector", lambda v: v.tensor_tensor(
                    b3(SEL[:, :], 16), b3(MS[:, :], 16),
                    _bcast(smc(C_GR, NSLOT), 16), ALU.is_ge),
                    reads=[rMS, rGSUM], writes=[rSEL])
                P.emit("vector", lambda v: v.tensor_mul(
                    CC[:, :], GSC[:, :], SEL[:, :]),
                    reads=[rGSC, rSEL], writes=[rCC], inc="ve")
                P.flush_block()
            s2a.close()

            s2b = contextlib.ExitStack()
            H0 = s2b.enter_context(nc.psum_tensor("H0", [128, 1024], F32))
            H1 = s2b.enter_context(nc.psum_tensor("H1", [128, 1024], F32))
            H2 = s2b.enter_context(nc.psum_tensor("H2", [128, 1024], F32))
            H3 = s2b.enter_context(nc.psum_tensor("H3", [128, 1024], F32))
            if True:
                HH = [H0, H1, H2, H3]
                rH = [R("h0"), R("h1"), R("h2"), R("h3")]

                def do_expert(e):
                    pi = e % 2
                    if e + 1 < NEXP + 2:
                        prefetch_expert(e + 1, 1 - pi)

                    def w2_tt(tt):
                        yp, ryp = HH[tt % 2], rH[tt % 2]
                        fns = []
                        for ich in range(4):
                            for half in range(2):
                                fns.append(
                                    lambda t, ich=ich, half=half, tt=tt, yp=yp,
                                    pi=pi: t.matmul(
                                        yp[:, half * 512:(half + 1) * 512],
                                        APR[:, ich * NT + tt * 128:
                                            ich * NT + (tt + 1) * 128],
                                        W2B[:, pi * 4096 + ich * 1024 + half * 512:
                                            pi * 4096 + ich * 1024 + (half + 1) * 512],
                                        start=(ich == 0), stop=(ich == 3)))
                        P.group("tensor", fns, reads=[rAPR, rW2[pi]], writes=[ryp],
                                inc="pe")
                        if e < NEXP:
                            P.emit("vector",
                                   lambda v, tt=tt, yp=yp, e=e: v.scalar_tensor_tensor(
                                       acc(tt), yp[:, :],
                                       CC[:, tt * 16 + e: tt * 16 + e + 1],
                                       acc(tt), ALU.mult, ALU.add),
                                   reads=[ryp, rCC], writes=[rACC[tt]], inc="ve")
                        else:
                            P.emit("vector", lambda v, tt=tt, yp=yp: v.tensor_add(
                                acc(tt), acc(tt), yp[:, :]),
                                reads=[ryp], writes=[rACC[tt]], inc="ve")
                        if e == NEXP + 1:
                            sSO[tt % 4].issue(lambda s, tt=tt: s.dma_start(
                                out_d[tt], acc(tt)),
                                reads=[rACC[tt]])

                    for th in range(NTH):
                        for ich in range(4):
                            hp = (th * 4 + ich) % 2
                            pairs = ((HH[2 * hp], rH[2 * hp], W1B, rW1[pi]),
                                     (HH[2 * hp + 1], rH[2 * hp + 1], W3B, rW3[pi]))
                            for (ps, rps, wbuf, rwb) in pairs:
                                fns = []
                                for dch in range(8):
                                    for half in range(TCH // 512):
                                        fns.append(
                                            lambda t, dch=dch, half=half, ps=ps,
                                            wbuf=wbuf, ich=ich, th=th, pi=pi: t.matmul(
                                                ps[:, half * 512:(half + 1) * 512],
                                                wbuf[:, pi * 4096 + dch * 512 + ich * 128:
                                                     pi * 4096 + dch * 512 + (ich + 1) * 128],
                                                xf(dch, th * TCH + half * 512, 512),
                                                start=(dch == 0), stop=(dch == 7)))
                                P.group("tensor", fns, reads=[rwb, rXF],
                                        writes=[rps], inc="pe")
                            h1ps, rh1 = HH[2 * hp], rH[2 * hp]
                            h3ps, rh3 = HH[2 * hp + 1], rH[2 * hp + 1]
                            P.emit("scalar", lambda s, h1ps=h1ps: s.activation(
                                SCRF32[:, 0:TCH], h1ps[:, 0:TCH], AF.Silu),
                                reads=[rh1], writes=[rSCR[0]], inc="sc")
                            P.emit("vector",
                                   lambda v, h3ps=h3ps, ich=ich, th=th: v.tensor_mul(
                                       APR[:, ich * NT + th * TCH:
                                           ich * NT + (th + 1) * TCH],
                                       SCRF32[:, 0:TCH], h3ps[:, 0:TCH]),
                                   reads=[rSCR[0], rh3], writes=[rAPR], inc="ve")
                    for tt in range(NSLOT):
                        w2_tt(tt)

                for e in range(NEXP + 2):
                    do_expert(e)

                P.emit("sync", lambda s: s.nop(),
                       extra=[("d_so0", P.cnt["d_so0"], "dma"),
                              ("d_so1", P.cnt["d_so1"], "dma"),
                              ("d_so2", P.cnt["d_so2"], "dma"),
                              ("d_so3", P.cnt["d_so3"], "dma")])
                P.flush_block()
            s2b.close()
        s2.close()

    return nc



def _img_km(M, r):
    R_, C = M.shape
    assert R_ == r * 128
    return np.ascontiguousarray(M.reshape(r, 128, C).transpose(1, 0, 2))


_CACHED = {}


def _get_nc(attn_only=False):
    if attn_only not in _CACHED:
        _CACHED[attn_only] = build_program(attn_only)
    return _CACHED[attn_only]


def kernel(L0_attn_input_data, L0_attn_qk, L0_attn_ov, L0_ffn_gate_w,
           L0_ffn_w1, L0_ffn_w2, L0_ffn_w3, L0_ffn_ws1, L0_ffn_ws2, L0_ffn_ws3,
           _attn_only=False):
    x = np.ascontiguousarray(np.asarray(L0_attn_input_data, dtype=np.float32))
    qk = np.asarray(L0_attn_qk, dtype=np.float32)
    ov = np.asarray(L0_attn_ov, dtype=np.float32)
    gw = np.asarray(L0_ffn_gate_w, dtype=np.float32)
    w1 = np.asarray(L0_ffn_w1, dtype=np.float32)
    w2 = np.asarray(L0_ffn_w2, dtype=np.float32)
    w3 = np.asarray(L0_ffn_w3, dtype=np.float32)
    ws1 = np.asarray(L0_ffn_ws1, dtype=np.float32)
    ws2 = np.asarray(L0_ffn_ws2, dtype=np.float32)
    ws3 = np.asarray(L0_ffn_ws3, dtype=np.float32)

    qk_i = _img_km(qk, 8).reshape(128, 8 * D)
    ov_i = _img_km(ov, 8).reshape(128, 8 * D)
    gwt_i = _img_km(np.ascontiguousarray(gw.T), 8).reshape(128, 128)
    w1_i = np.stack([_img_km(w1[e], 8).reshape(128, 8 * INTER) for e in range(NEXP)])
    w3_i = np.stack([_img_km(w3[e], 8).reshape(128, 8 * INTER) for e in range(NEXP)])
    w2_i = np.stack([_img_km(w2[e], 4).reshape(128, 4 * D) for e in range(NEXP)])
    ws1_i = _img_km(ws1, 8)
    ws3_i = _img_km(ws3, 8)
    ws2_i = _img_km(ws2, 8)
    ident = np.eye(128, dtype=np.float32)

    in_maps = []
    for c in range(8):
        b, c4 = c // 4, c % 4
        xb = np.ascontiguousarray(x[b])
        xo = np.ascontiguousarray(xb.reshape(64, 128, D)[c4::4])
        jj = np.arange(512)[None, :]
        ii = np.arange(128)[:, None]
        mask = np.where(jj <= ii + 128 * c4, 0.0, -BIG).astype(np.float32)
        in_maps.append({
            "x_seq": xb, "x_own": xo, "mask": mask, "ident": ident,
            "qk_img": qk_i, "ov_img": ov_i, "gwt_img": gwt_i,
            "w1_img": w1_i, "w3_img": w3_i, "w2_img": w2_i,
            "ws1_img": ws1_i, "ws3_img": ws3_i, "ws2_img": ws2_i,
        })

    nc = _get_nc(_attn_only)
    res = run_bass_kernel_spmd(nc, in_maps, core_ids=list(range(8)))

    out = np.empty((2, SEQ, D), dtype=np.float32)
    for c in range(8):
        b, c4 = c // 4, c % 4
        out[b].reshape(64, 128, D)[c4::4] = res.results[c]["out"]
    return out


# version 20
# speedup vs baseline: 1.0340x; 1.0078x over previous
import numpy as np
import concourse.bass as bass
import concourse.mybir as mybir
from concourse.bass_utils import run_bass_kernel_spmd

F32 = mybir.dt.float32
BF16 = mybir.dt.bfloat16
AF = mybir.ActivationFunctionType
ALU = mybir.AluOpType
AX = mybir.AxisListType

D = 1024
SEQ = 8192
NSLOT = 16
NPANEL = 8
PKT = 8
NEXP = 16
INTER = 512
EPS = 1e-5
BIG = 1.0e30


class _Res:
    __slots__ = ("w", "r", "name")
    def __init__(self, name):
        self.name = name
        self.w = []
        self.r = []


class _Prog:
    ENGINES = ("tensor", "vector", "scalar", "sync", "gpsimd")

    def __init__(self, nc):
        self.nc = nc
        self.q = {e: [] for e in self.ENGINES}
        self.cnt = {}
        self.semh = {}
        self.waited = {e: {} for e in self.ENGINES}

    def sem(self, name, stack):
        self.semh[name] = stack.enter_context(self.nc.semaphore(name))
        self.cnt[name] = 0

    ESEM = {"tensor": "pe", "vector": "ve", "scalar": "sc", "gpsimd": "gp",
            "sync": None}

    def _resolve(self, eng, toks):
        out = {}
        for t in toks:
            s, v, src = t
            if s is None:
                if src == eng:
                    continue
                raise RuntimeError(
                    f"cross-engine dep on un-inc'd op (src={src}, consumer={eng})")
            out[s] = max(out.get(s, 0), v)
        return sorted(out.items())

    def _hazard_toks(self, reads, writes, extra):
        toks = list(extra)
        for r in reads:
            toks += r.w
        for w in writes:
            toks += w.w + w.r
        return toks

    def _record(self, tok, reads, writes):
        for r in reads:
            r.r.append(tok)
        for w in writes:
            w.w = [tok]
            w.r = []

    def emit(self, eng, fn, reads=(), writes=(), extra=(), inc="auto", inc_by=1):
        if inc == "auto":
            inc = self.ESEM[eng]
        waits = self._resolve(eng, self._hazard_toks(reads, writes, extra))
        if inc is not None:
            self.cnt[inc] += inc_by
            tok = (inc, self.cnt[inc], eng)
        else:
            tok = (None, 0, eng)
        self.q[eng].append((fn, waits, inc, inc_by))
        self._record(tok, reads, writes)
        return tok

    def group(self, eng, fns, reads=(), writes=(), extra=(), inc="auto"):
        if inc == "auto":
            inc = self.ESEM[eng]
        waits = self._resolve(eng, self._hazard_toks(reads, writes, extra))
        tok = (None, 0, eng)
        n = len(fns)
        for i, fn in enumerate(fns):
            s = inc if (i == n - 1 and inc is not None) else None
            if s is not None:
                self.cnt[s] += 1
                tok = (s, self.cnt[s], eng)
            self.q[eng].append((fn, waits if i == 0 else [], s, 1))
        self._record(tok, reads, writes)
        return tok

    def flush_block(self):
        nc = self.nc
        snap = dict(self.cnt)
        qs = {e: list(self.q[e]) for e in self.ENGINES}
        for e in self.ENGINES:
            self.q[e] = []

        def body(engobj, e):
            for fn, waits, inc, inc_by in qs[e]:
                for s, v in waits:
                    engobj.wait_ge(self.semh[s], v)
                inst = fn(engobj)
                if inc is not None:
                    inst.then_inc(self.semh[inc], inc_by)

        with nc.Block() as block:
            if qs["tensor"]:
                block.tensor(lambda t: body(t, "tensor"))
            if qs["vector"]:
                block.vector(lambda t: body(t, "vector"))
            if qs["scalar"]:
                block.scalar(lambda t: body(t, "scalar"))
            if qs["sync"]:
                block.sync(lambda t: body(t, "sync"))
            if qs["gpsimd"]:
                block.gpsimd(lambda t: body(t, "gpsimd"))
        del snap


class _Stream:
    def __init__(self, prog, sem, eng):
        self.P, self.sem, self.eng = prog, sem, eng

    def _idle_wait(self):
        if self.P.cnt[self.sem]:
            return [(self.sem, self.P.cnt[self.sem], "dma")]
        return []

    def issue(self, fn, reads=(), writes=()):
        return self.P.emit(self.eng, fn, reads=reads, writes=writes,
                           extra=self._idle_wait(), inc=self.sem, inc_by=16)

    def issue_batch(self, items):
        P = self.P
        extra = self._idle_wait()
        rws = []
        for i, (fn, reads, writes) in enumerate(items):
            waits = P._resolve(self.eng, P._hazard_toks(
                reads, writes, extra if i == 0 else ()))
            P.cnt[self.sem] += 16
            P.q[self.eng].append((fn, waits, self.sem, 16))
            rws.append((reads, writes))
        tok = (self.sem, P.cnt[self.sem], "dma")
        for reads, writes in rws:
            P._record(tok, reads, writes)
        return tok


def _bcast(ap, inner):
    return bass.AP(ap.tensor, ap.offset,
                   [list(ap.ap[0]), list(ap.ap[1]), [0, inner]])



def build_program(attn_only=False):
    import contextlib
    nc = bass.Bass("TRN2", target_bir_lowering=False)

    x_seq = nc.dram_tensor("x_seq", [SEQ, D], F32, kind="ExternalInput").ap()
    x_own = nc.dram_tensor("x_own", [NSLOT, 128, D], F32, kind="ExternalInput").ap()
    maskd = nc.dram_tensor("mask", [128, 512], F32, kind="ExternalInput").ap()
    identd = nc.dram_tensor("ident", [128, 128], F32, kind="ExternalInput").ap()
    qk_img = nc.dram_tensor("qk_img", [128, 8 * D], F32, kind="ExternalInput").ap()
    ov_img = nc.dram_tensor("ov_img", [128, 8 * D], F32, kind="ExternalInput").ap()
    gwt_img = nc.dram_tensor("gwt_img", [128, 128], F32, kind="ExternalInput").ap()
    w1_img = nc.dram_tensor("w1_img", [NEXP, 128, 8 * INTER], F32, kind="ExternalInput").ap()
    w3_img = nc.dram_tensor("w3_img", [NEXP, 128, 8 * INTER], F32, kind="ExternalInput").ap()
    w2_img = nc.dram_tensor("w2_img", [NEXP, 128, 4 * D], F32, kind="ExternalInput").ap()
    ws1_img = nc.dram_tensor("ws1_img", [128, 8, D], F32, kind="ExternalInput").ap()
    ws3_img = nc.dram_tensor("ws3_img", [128, 8, D], F32, kind="ExternalInput").ap()
    ws2_img = nc.dram_tensor("ws2_img", [128, 8, D], F32, kind="ExternalInput").ap()
    out_d = nc.dram_tensor("out", [NSLOT, 128, D], F32, kind="ExternalOutput").ap()

    P = _Prog(nc)
    NT = NSLOT * 128

    with contextlib.ExitStack() as stk:
        for s in ("d_si", "d_gi", "d_sx0", "d_sx1", "d_sx2", "d_sf",
                  "d_so0", "d_so1", "d_so2", "d_so3",
                  "d_sw0", "d_sw1", "d_tp0", "d_tp1", "d_tp2", "d_tp3",
                  "pe", "ve", "sc", "gp"):
            P.sem(s, stk)
        sSI = _Stream(P, "d_si", "sync")
        sGI = _Stream(P, "d_gi", "gpsimd")
        sSX = [_Stream(P, "d_sx0", "sync"), _Stream(P, "d_sx1", "sync"),
               _Stream(P, "d_sx2", "sync")]
        sSF = _Stream(P, "d_sf", "sync")
        sSO = [_Stream(P, "d_so0", "sync"), _Stream(P, "d_so1", "sync"),
               _Stream(P, "d_so2", "sync"), _Stream(P, "d_so3", "sync")]
        sSW = [_Stream(P, "d_sw0", "gpsimd"), _Stream(P, "d_sw1", "gpsimd")]
        sTP = [_Stream(P, "d_tp0", "sync"), _Stream(P, "d_tp1", "sync"),
               _Stream(P, "d_tp2", "sync"), _Stream(P, "d_tp3", "sync")]

        sb = lambda name, shape, dt: stk.enter_context(nc.sbuf_tensor(name, shape, dt))
        CONST = sb("CONST", [128, 4], F32)
        SM = sb("SM", [128, 96], F32)
        ACC = sb("ACC", [128, NSLOT * D], F32)
        STG = sb("STG", [128, 3 * D], F32)
        XOWNF = sb("XOWNF", [128, D], F32)
        IDB = sb("IDB", [128, 128], BF16)
        XLNBF = sb("XLNBF", [128, 2 * D], BF16)
        MASKBB = sb("MASKBB", [128, 512], BF16)

        acc = lambda m: ACC[:, m * D:(m + 1) * D]
        smc = lambda c, n=1: SM[:, c:c + n]
        stg = lambda i: STG[:, (i % 3) * D:((i % 3) + 1) * D]
        xlnb = lambda lane: XLNBF[:, lane * D:(lane + 1) * D]

        C_NM, C_LL = 0, 16
        LN_MSUM, LN_SSQ, LN_NMEAN, LN_NM2, LN_VAR, LN_RSTD, LN_NMR = range(7)
        C_NRM, C_TMIN, C_TMD, C_ALPHA, C_RSUM, C_RL = 48, 50, 52, 54, 57, 59
        C_GS, C_GR = 64, 80

        R = _Res
        rCONST = R("const")
        rLN = [R("ln0"), R("ln1")]
        rNM = [R(f"nm{m}") for m in range(NSLOT)]
        rLL = [R(f"ll{m}") for m in range(NSLOT)]
        rNRM = [R("nrm0"), R("nrm1")]
        rTMIN = [R("tmin0"), R("tmin1")]
        rTMD = [R("tmd0"), R("tmd1")]
        rALPHA = [R("al0"), R("al1"), R("al2")]
        rRSUM = [R("rs0"), R("rs1")]
        rRL = [R("rl0"), R("rl1")]
        rACC = [R(f"acc{m}") for m in range(NSLOT)]
        rSTG = [R("stg0"), R("stg1"), R("stg2")]
        rOVB, rMASK, rIDB = R("ovb"), R("mask"), R("idb")
        rMASKBB = R("maskbb")
        rXLNBF = [R("xlnbf0"), R("xlnbf1")]
        rSCR = [R("scr0"), R("scr1")]
        rXOWNs = R("xown")

        def ln_chain(src_ap, src_res, dst_ap, dst_res, lane, se="vector",
                     sum_on_scalar=False):
            rl = rLN[lane]
            base = 32 + 8 * lane
            c = lambda k: smc(base + k)
            dump = xlnb(lane)
            rdump = rXLNBF[lane]
            if sum_on_scalar:
                P.emit("scalar", lambda s: s.activation(
                    dump, src_ap, AF.Identity, accum_out=c(LN_MSUM)),
                    reads=[src_res], writes=[rl, rdump])
            else:
                P.emit(se, lambda v: v.tensor_scalar(
                    dump, src_ap, 1.0, 0.0, ALU.mult, ALU.add,
                    accum_out=c(LN_MSUM)),
                    reads=[src_res], writes=[rl, rdump])
            P.emit(se, lambda v: v.scalar_tensor_tensor(
                dump, src_ap, 1.0, src_ap, ALU.mult, ALU.mult,
                accum_out=c(LN_SSQ)),
                reads=[src_res], writes=[rl, rdump])
            P.emit(se, lambda v: v.tensor_scalar(
                c(LN_NMEAN), c(LN_MSUM), -1.0 / D, None, ALU.mult),
                reads=[rl], writes=[rl])
            P.emit(se, lambda v: v.scalar_tensor_tensor(
                c(LN_NM2), c(LN_NMEAN), -1.0, c(LN_NMEAN), ALU.mult, ALU.mult),
                reads=[rl], writes=[rl])
            P.emit(se, lambda v: v.scalar_tensor_tensor(
                c(LN_VAR), c(LN_SSQ), 1.0 / D, c(LN_NM2), ALU.mult, ALU.add),
                reads=[rl], writes=[rl])
            P.emit("scalar", lambda s: s.activation(
                c(LN_VAR), c(LN_VAR), AF.Ln, bias=CONST[:, 2:3]),
                reads=[rCONST, rl], writes=[rl])
            P.emit("scalar", lambda s: s.activation(
                c(LN_RSTD), c(LN_VAR), AF.Exp, scale=CONST[:, 1:2]),
                reads=[rCONST, rl], writes=[rl])
            P.emit("scalar", lambda s: s.activation(
                c(LN_NMR), c(LN_RSTD), AF.Identity, scale=c(LN_NMEAN)),
                reads=[rl], writes=[rl])
            wl = [dst_res] if dst_res is not None else []
            tnorm = P.emit("scalar", lambda s: s.activation(
                dst_ap, src_ap, AF.Identity, scale=c(LN_RSTD), bias=c(LN_NMR)),
                reads=[src_res, rl], writes=wl)
            return tnorm

        P.emit("gpsimd", lambda g: g.memset(SM[:, :], 0.0), writes=[rLN[0], rLN[1]])
        P.emit("gpsimd", lambda g: g.memset(CONST[:, 0:1], 1.0 / 32.0), writes=[rCONST])
        P.emit("gpsimd", lambda g: g.memset(CONST[:, 1:2], -0.5), writes=[rCONST])
        P.emit("gpsimd", lambda g: g.memset(CONST[:, 2:3], EPS), writes=[rCONST], inc="gp")

        with (
            nc.sbuf_tensor("Q", [128, 8 * NT], BF16) as Q,
            nc.sbuf_tensor("OVB", [128, 8 * D], BF16) as OVB,
        ):
            rQ = R("Q")
            qsl = lambda dch, lo, n: Q[:, dch * NT + lo: dch * NT + lo + n]

            with (
                nc.sbuf_tensor("XLNT", [128, 8 * NT], BF16) as XLNT,
                nc.sbuf_tensor("QKB", [128, 8 * D], BF16) as QKB,
                nc.psum_tensor("PSA", [128, 512], F32) as PSA,
                nc.psum_tensor("PSB", [128, 512], F32) as PSB,
                nc.psum_tensor("PSC", [128, 512], F32) as PSC,
                nc.psum_tensor("PSD", [128, 512], F32) as PSD,
                nc.psum_tensor("PSE", [128, 512], F32) as PSE,
                nc.psum_tensor("PSF", [128, 512], F32) as PSF,
            ):
                rXLNT, rQKB = R("XLNT"), R("QKB")
                QPS = [(PSA, R("PSA")), (PSB, R("PSB")),
                       (PSC, R("PSC")), (PSD, R("PSD")),
                       (PSE, R("PSE")), (PSF, R("PSF"))]
                sGI.issue_batch([
                    (lambda g: g.dma_start(IDB[:, :], identd), (), [rIDB]),
                    (lambda g: g.dma_start(MASKBB[:, :], maskd), (), [rMASKBB]),
                    (lambda g: g.dma_start(OVB[:, :], ov_img), (), [rOVB]),
                    (lambda g: g.dma_start(QKB[:, :], qk_img), (), [rQKB]),
                ])

                def pre_ln(m):
                    lane = m % 2
                    sSX[m % 3].issue(lambda s: s.dma_start(stg(m), x_own[m]),
                                     writes=[rSTG[m % 3]])
                    ln_chain(stg(m), rSTG[m % 3], xlnb(lane), rXLNBF[lane], lane)

                def pre_tp(m):
                    lane = m % 2
                    sTP[m % 4].issue(lambda s: s.dma_start(
                        XLNT[:, :].rearrange("p (a b) -> p a b", b=NT)
                        [:, :, m * 128:(m + 1) * 128],
                        xlnb(lane)[:, :].rearrange("p (c q) -> p c q", q=128),
                        transpose=True),
                        reads=[rXLNBF[lane]], writes=[rXLNT])

                NSG = NT // 512

                def q_group(g):
                    ech, sg = g // NSG, g % NSG
                    bank, rbank = QPS[g % 6]
                    fns = [(lambda t, dch=dch: t.matmul(
                        bank[:, 0:512],
                        QKB[:, dch * D + ech * 128: dch * D + (ech + 1) * 128],
                        XLNT[:, dch * NT + sg * 512: dch * NT + (sg + 1) * 512],
                        start=(dch == 0), stop=(dch == 7))) for dch in range(8)]
                    P.group("tensor", fns, reads=[rQKB, rXLNT], writes=[rbank],
                            inc="pe")
                    if g % 2 == 0:
                        P.emit("scalar", lambda s: s.activation(
                            qsl(ech, sg * 512, 512), bank[:, 0:512], AF.Identity,
                            scale=CONST[:, 0:1]),
                            reads=[rbank, rCONST], writes=[rQ], inc="sc")
                    else:
                        P.emit("vector", lambda v: v.tensor_scalar_mul(
                            qsl(ech, sg * 512, 512), bank[:, 0:512],
                            CONST[:, 0:1]),
                            reads=[rbank, rCONST], writes=[rQ], inc="ve")

                qready = []
                qi = 0
                for m in range(NSLOT):
                    pre_ln(m)
                    if m >= 1:
                        pre_tp(m - 1)
                    if m % 4 == 3 and m >= 7:
                        sg = (m - 4) // 4
                        qready += [ech * NSG + sg for ech in range(8)]
                    while qi < len(qready) and qi < max(0, (m - 6)) * 3:
                        q_group(qready[qi])
                        qi += 1
                pre_tp(NSLOT - 1)
                done = set(qready[:qi])
                rest = [g for g in qready if g not in done]
                rest += [g for g in range(8 * NSG) if g not in done and g not in rest]
                for g in rest:
                    q_group(g)
                P.flush_block()

            with (
                nc.sbuf_tensor("KP0", [128, 8 * 1024], BF16) as KP0,
                nc.sbuf_tensor("KP1", [128, 8 * 1024], BF16) as KP1,
                nc.sbuf_tensor("VP0", [128, 8 * 1024], BF16) as VP0,
                nc.sbuf_tensor("VP1", [128, 8 * 1024], BF16) as VP1,
                nc.sbuf_tensor("PBF", [128, 2048], BF16) as PBF,
                nc.sbuf_tensor("PTS", [128, 3072], BF16) as PTS,
                nc.psum_tensor("PS_SE", [128, 1024], F32) as PS_SE,
                nc.psum_tensor("PS_SO", [128, 1024], F32) as PS_SO,
                nc.psum_tensor("PS_PV", [128, 1024], F32) as PS_PV,
                nc.psum_tensor("PSPT2", [128, 1024], BF16) as PSPT2,
                nc.psum_tensor("PSPT2B", [128, 1024], BF16) as PSPT2B,
            ):
                KP, VP = [KP0, KP1], [VP0, VP1]
                rKP = [R("kp0"), R("kp1")]
                rVP = [R("vp0"), R("vp1")]
                rVPk = [[R(f"vp{b}k{k}") for k in range(PKT)] for b in range(2)]
                rPBF = [R("pbf0"), R("pbf1")]
                rPTS = [R("pts0"), R("pts1"), R("pts2")]
                FINBF, FINT = PBF, PTS
                rFINBF, rFINT = rPBF[0], rPTS[0]
                rSE, rSO, rPV, rPT2 = R("se"), R("so"), R("pv"), R("pt2")
                rPT2B = R("pt2b")

                def kslice(buf, dch, lo, n):
                    return KP[buf][:, dch * 1024 + lo: dch * 1024 + lo + n]

                def vslice(buf, kt, lo, n):
                    return VP[buf][:, kt * 1024 + lo: kt * 1024 + lo + n]

                def emit_prod_load(p, kt):
                    buf = p % 2
                    gi = p * PKT + kt
                    sSX[gi % 3].issue(lambda s: s.dma_start(
                        stg(gi),
                        x_seq[p * 1024 + kt * 128: p * 1024 + (kt + 1) * 128, :]),
                        writes=[rSTG[gi % 3]])
                    ln_chain(stg(gi), rSTG[gi % 3],
                             vslice(buf, kt, 0, 1024), rVPk[buf][kt], kt % 2)

                def emit_prod_tp(p, kt):
                    buf = p % 2
                    sTP[kt % 4].issue(lambda s: s.dma_start(
                        KP[buf][:, :].rearrange("p (a b) -> p a b", b=1024)
                        [:, :, kt * 128:(kt + 1) * 128],
                        vslice(buf, kt, 0, 1024)
                        .rearrange("p (c q) -> p c q", q=128),
                        transpose=True),
                        reads=[rVPk[buf][kt]], writes=[rKP[buf]])

                def emit_qk(p, buf, i, m):
                    nkt = 4 if m == 2 * p else 8
                    nbank = nkt // 4
                    mask_bank = (0 if m == 2 * p else
                                 (1 if m == 2 * p + 1 else None))
                    SP, rSP = ((PS_SE, rSE) if i % 2 == 0 else (PS_SO, rSO))
                    fns = []
                    if mask_bank is not None:
                        fns.append(lambda t, mb=mask_bank: t.matmul(
                            SP[:, mb * 512:(mb + 1) * 512],
                            IDB[:, :], MASKBB[:, :],
                            start=True, stop=False))
                    for dch in range(8):
                        for bank in range(nbank):
                            st = (dch == 0) and (bank != mask_bank)
                            sp_ = (dch == 7)
                            fns.append(lambda t, bank=bank, dch=dch, st=st,
                                       sp_=sp_: t.matmul(
                                SP[:, bank * 512:(bank + 1) * 512],
                                qsl(dch, m * 128, 128),
                                kslice(buf, dch, bank * 512, 512),
                                start=st, stop=sp_))
                    rd = [rQ, rKP[buf]]
                    if mask_bank is not None:
                        rd += [rIDB, rMASKBB]
                    P.group("tensor", fns, reads=rd, writes=[rSP], inc="pe")
                    return nkt, SP, rSP

                def emit_max(p, i, m, nkt, SP, rSP):
                    par = i % 2
                    ext = nkt * 128
                    P.emit("vector", lambda v: v.reduce_max(
                        smc(C_NRM + par), SP[:, 0:ext], axis=AX.X, negate=True),
                        reads=[rSP], writes=[rNRM[par]], inc="ve")
                    if p == 0:
                        P.emit("vector", lambda v: v.tensor_copy(
                            smc(C_NM + m), smc(C_NRM + par)),
                            reads=[rNRM[par]], writes=[rNM[m]], inc="ve")
                    else:
                        P.emit("vector", lambda v: v.tensor_sub(
                            smc(C_TMD + par), smc(C_NRM + par), smc(C_NM + m)),
                            reads=[rNRM[par], rNM[m]], writes=[rTMD[par]])
                        P.emit("vector", lambda v: v.tensor_scalar_min(
                            smc(C_TMD + par), smc(C_TMD + par), 0.0),
                            reads=[rTMD[par]], writes=[rTMD[par]])
                        P.emit("vector", lambda v: v.tensor_add(
                            smc(C_NM + m), smc(C_NM + m), smc(C_TMD + par)),
                            reads=[rTMD[par], rNM[m]], writes=[rNM[m]], inc="ve")

                def emit_exp(p, i, m, nkt, SP, rSP):
                    par = i % 2
                    al = i % 3
                    ext = nkt * 128
                    if p > 0:
                        P.emit("scalar", lambda s: s.activation(
                            smc(C_ALPHA + al), smc(C_TMD + par), AF.Exp),
                            reads=[rTMD[par]], writes=[rALPHA[al]], inc="sc")
                    P.emit("scalar", lambda s: s.activation(
                        PBF[:, par * 1024: par * 1024 + ext], SP[:, 0:ext], AF.Exp,
                        bias=smc(C_NM + m), accum_out=smc(C_RSUM + par)),
                        reads=[rSP, rNM[m]], writes=[rPBF[par], rRSUM[par]],
                        inc="sc")

                def emit_llupd(p, i, m):
                    par = i % 2
                    al = i % 3
                    if p == 0:
                        P.emit("vector", lambda v: v.tensor_copy(
                            smc(C_LL + m), smc(C_RSUM + par)),
                            reads=[rRSUM[par]], writes=[rLL[m]], inc="ve")
                    else:
                        P.emit("vector", lambda v: v.scalar_tensor_tensor(
                            smc(C_LL + m), smc(C_LL + m), smc(C_ALPHA + al),
                            smc(C_RSUM + par), ALU.mult, ALU.add),
                            reads=[rRSUM[par], rALPHA[al]], writes=[rLL[m]],
                            inc="ve")

                def emit_t(p, buf, i, m, nkt):
                    par = i % 2
                    sl3 = i % 3
                    fns = [(lambda t, kt=kt: t.transpose(
                        PSPT2[:, kt * 128:(kt + 1) * 128],
                        PBF[:, par * 1024 + kt * 128: par * 1024 + (kt + 1) * 128],
                        IDB[:, :])) for kt in range(nkt)]
                    P.group("tensor", fns, reads=[rPBF[par], rIDB], writes=[rPT2],
                            inc="pe")
                    P.emit("scalar", lambda s: s.copy(
                        PTS[:, sl3 * 1024: sl3 * 1024 + nkt * 128],
                        PSPT2[:, 0:nkt * 128]),
                        reads=[rPT2], writes=[rPTS[sl3]], inc="sc")

                def emit_pv(p, buf, i, m, nkt):
                    al = i % 3
                    sl3 = i % 3
                    fns = []
                    for kt in range(nkt):
                        for half in range(2):
                            fns.append(lambda t, kt=kt, half=half: t.matmul(
                                PS_PV[:, half * 512:(half + 1) * 512],
                                PTS[:, sl3 * 1024 + kt * 128: sl3 * 1024 + (kt + 1) * 128],
                                vslice(buf, kt, half * 512, 512),
                                start=(kt == 0), stop=(kt == nkt - 1)))
                    P.group("tensor", fns,
                            reads=[rPTS[sl3]] + [rVPk[buf][kt] for kt in range(nkt)],
                            writes=[rPV], inc="pe")
                    if p == 0:
                        P.emit("vector", lambda v: v.tensor_copy(acc(m), PS_PV[:, :]),
                               reads=[rPV], writes=[rACC[m]], inc="ve")
                    else:
                        P.emit("vector", lambda v: v.scalar_tensor_tensor(
                            acc(m), acc(m), smc(C_ALPHA + al), PS_PV[:, :],
                            ALU.mult, ALU.add),
                            reads=[rPV, rALPHA[al]], writes=[rACC[m]], inc="ve")

                def emit_finalize(ma, mb):
                    items = ((ma, 0, PS_SE, rSE), (mb, 1, PS_SO, rSO))
                    sSF.issue(lambda s: s.dma_start(XOWNF[:, :], x_own[ma]),
                              writes=[rXOWNs])
                    for m, h, _, _ in items:
                        P.emit("vector", lambda v, m=m, h=h: v.reciprocal(
                            smc(C_RL + h), smc(C_LL + m)),
                            reads=[rLL[m]], writes=[rRL[h]], inc="ve")
                    for m, h, _, _ in items:
                        P.emit("scalar", lambda s, m=m, h=h: s.activation(
                            PBF[:, h * 1024:(h + 1) * 1024], acc(m), AF.Identity,
                            scale=smc(C_RL + h)),
                            reads=[rACC[m], rRL[h]], writes=[rPBF[h]], inc="sc")
                    for m, h, _, _ in items:
                        fns = [(lambda t, dch=dch, h=h: t.transpose(
                            PSPT2B[:, dch * 128:(dch + 1) * 128] if h == 0 else
                            PSPT2[:, dch * 128:(dch + 1) * 128],
                            PBF[:, h * 1024 + dch * 128: h * 1024 + (dch + 1) * 128],
                            IDB[:, :])) for dch in range(8)]
                        P.group("tensor", fns, reads=[rPBF[h], rIDB],
                                writes=[rPT2B if h == 0 else rPT2], inc="pe")
                        P.emit("scalar", lambda s, h=h: s.copy(
                            PTS[:, h * 1024:(h + 1) * 1024],
                            PSPT2B[:, :] if h == 0 else PSPT2[:, :]),
                            reads=[rPT2B if h == 0 else rPT2],
                            writes=[rPTS[h]], inc="sc")
                    for m, h, PSX, rX in items:
                        fns = []
                        for half2 in range(2):
                            for dch in range(8):
                                fns.append(lambda t, dch=dch, half2=half2,
                                           h=h, PSX=PSX: t.matmul(
                                    PSX[:, half2 * 512:(half2 + 1) * 512],
                                    PTS[:, h * 1024 + dch * 128: h * 1024 + (dch + 1) * 128],
                                    OVB[:, dch * D + half2 * 512: dch * D + (half2 + 1) * 512],
                                    start=(dch == 0), stop=(dch == 7)))
                        P.group("tensor", fns, reads=[rPTS[h], rOVB],
                                writes=[rX], inc="pe")
                        if h == 0:
                            P.emit("vector", lambda v, m=m, PSX=PSX: v.tensor_add(
                                acc(m), XOWNF[:, :], PSX[:, :]),
                                reads=[rXOWNs, rX], writes=[rACC[m]], inc="ve")
                            sSF.issue(lambda s, mb=mb: s.dma_start(
                                XOWNF[:, :], x_own[mb]),
                                writes=[rXOWNs])
                        else:
                            P.emit("vector", lambda v, m=m, PSX=PSX: v.tensor_add(
                                acc(m), XOWNF[:, :], PSX[:, :]),
                                reads=[rXOWNs, rX], writes=[rACC[m]], inc="ve")

                for kt in range(PKT):
                    emit_prod_load(0, kt)
                    if kt >= 1:
                        emit_prod_tp(0, kt - 1)
                emit_prod_tp(0, PKT - 1)

                def do_panel(p):
                    buf = p % 2
                    slots = list(range(2 * p, NSLOT))
                    prod = [(p + 1, kt) for kt in range(PKT)] if p < NPANEL - 1 else []
                    nld = ntp = 0
                    n = len(slots)
                    hist = []
                    for i, m in enumerate(slots):
                        nkt, SP, rSP = emit_qk(p, buf, i, m)
                        hist.append((p, buf, i, m, nkt, SP, rSP))
                        emit_max(p, i, m, nkt, SP, rSP)
                        if i >= 1:
                            emit_t(*hist[i - 1][:5])
                        emit_exp(p, i, m, nkt, SP, rSP)
                        if i >= 2:
                            emit_pv(*hist[i - 2][:5])
                        if i >= 1:
                            emit_llupd(p, i - 1, hist[i - 1][3])
                        want_ld = (len(prod) * (i + 1) + n - 1) // n
                        while nld < min(want_ld, len(prod)):
                            emit_prod_load(*prod[nld])
                            nld += 1
                        while ntp < nld - 4:
                            emit_prod_tp(*prod[ntp])
                            ntp += 1
                    emit_t(*hist[-1][:5])
                    if len(hist) >= 2:
                        emit_pv(*hist[-2][:5])
                    emit_pv(*hist[-1][:5])
                    emit_llupd(p, hist[-1][2], hist[-1][3])
                    while nld < len(prod):
                        emit_prod_load(*prod[nld])
                        nld += 1
                    while ntp < len(prod):
                        emit_prod_tp(*prod[ntp])
                        ntp += 1
                    emit_finalize(2 * p, 2 * p + 1)

                for p in range(NPANEL):
                    do_panel(p)
                P.flush_block()

        if attn_only:
            dbg = nc.dram_tensor("dbg", [128, 96 + 512], F32,
                                 kind="ExternalOutput").ap()
            sSO[0].issue(lambda s: s.dma_start(dbg[:, 0:96], SM[:, :]))
            for m in range(NSLOT):
                sSO[m % 2].issue(lambda s, m=m: s.dma_start(out_d[m], acc(m)),
                                 reads=[rACC[m]])
            P.emit("sync", lambda s: s.nop(),
                   extra=[("d_so0", P.cnt["d_so0"], "dma"),
                          ("d_so1", P.cnt["d_so1"], "dma")])
            P.flush_block()
            return nc

        GT = 16 * NSLOT
        TCH = min(1024, NT)
        NTH = NT // TCH
        s2 = contextlib.ExitStack()
        XF = s2.enter_context(nc.sbuf_tensor("XF", [128, 8 * NT], BF16))
        SCRF32 = s2.enter_context(nc.sbuf_tensor("SCRF32", [128, 2 * D], F32))
        W1B = s2.enter_context(nc.sbuf_tensor("W1B", [128, 2 * 4096], BF16))
        W3B = s2.enter_context(nc.sbuf_tensor("W3B", [128, 2 * 4096], BF16))
        W2B = s2.enter_context(nc.sbuf_tensor("W2B", [128, 2 * 4096], BF16))
        APR = s2.enter_context(nc.sbuf_tensor("APR", [128, 4 * NT], BF16))
        CC = s2.enter_context(nc.sbuf_tensor("CC", [128, GT], F32))
        LOG = s2.enter_context(nc.sbuf_tensor("LOG", [128, GT], F32))
        EXPS = s2.enter_context(nc.sbuf_tensor("EXPS", [128, GT], F32))
        GSC = s2.enter_context(nc.sbuf_tensor("GSC", [128, GT], F32))
        MS = LOG
        WRK = s2.enter_context(nc.sbuf_tensor("WRK", [128, GT], F32))
        SEL = EXPS
        GMAX = WRK[:, 2 * 4 * NSLOT: 3 * 4 * NSLOT]
        GTH = WRK[:, 3 * 4 * NSLOT: 4 * 4 * NSLOT]
        GW = s2.enter_context(nc.sbuf_tensor("GW", [128, 256], BF16))
        XLO = s2.enter_context(nc.sbuf_tensor("XLO", [128, 2048], BF16))
        XLOT = s2.enter_context(nc.sbuf_tensor("XLOT", [128, 2048], BF16))
        if True:
            rXF, rAPR, rCC, rLOG, rEXPS, rGSC = (
                R("xf"), R("apr"), R("cc"), R("log"), R("exps"), R("gsc"))
            rWRK = R("wrk")
            rMS = rLOG
            rGMAX = rGTH = rWRK
            rSEL = rEXPS
            rW1 = [R("w1a"), R("w1b")]
            rW3 = [R("w3a"), R("w3b")]
            rW2 = [R("w2a"), R("w2b")]
            GWF = SCRF32[:, 0:128]
            rGW = R("gw")
            rGWF = rSCR[0]
            rXLOT = [R("xlot0"), R("xlot1")]
            rXLO = [R("xlo0"), R("xlo1")]
            rGSUM = R("gsum")

            def xf(dch, lo, n):
                return XF[:, dch * NT + lo: dch * NT + lo + n]

            def prefetch_expert(e, pi):
                d1 = W1B[:, pi * 4096:(pi + 1) * 4096]
                d3 = W3B[:, pi * 4096:(pi + 1) * 4096]
                d2 = W2B[:, pi * 4096:(pi + 1) * 4096]
                if e < NEXP:
                    s1, s3, s2 = w1_img[e], w3_img[e], w2_img[e]
                else:
                    h = e - NEXP
                    s1 = ws1_img[:, :, h * 512:(h + 1) * 512]
                    s3 = ws3_img[:, :, h * 512:(h + 1) * 512]
                    s2 = ws2_img[:, h * 4:(h + 1) * 4, :]
                    d1 = d1.rearrange("p (a b) -> p a b", b=512)
                    d3 = d3.rearrange("p (a b) -> p a b", b=512)
                    d2 = d2.rearrange("p (a b) -> p a b", b=1024)
                sSW[pi].issue_batch([
                    (lambda g: g.dma_start(d1, s1), (), [rW1[pi]]),
                    (lambda g: g.dma_start(d3, s3), (), [rW3[pi]]),
                    (lambda g: g.dma_start(d2, s2), (), [rW2[pi]]),
                ])

            s2a = contextlib.ExitStack()
            PT2A = s2a.enter_context(nc.psum_tensor("PT2A", [128, 1024], BF16))
            PT2B = s2a.enter_context(nc.psum_tensor("PT2B", [128, 1024], BF16))
            PG0 = s2a.enter_context(nc.psum_tensor("PG0", [128, 512], F32))
            PG1 = s2a.enter_context(nc.psum_tensor("PG1", [128, 512], F32))
            if True:
                rPT2A, rPT2B = R("pt2a"), R("pt2b")
                PG, rPG = [PG0, PG1], [R("pg0"), R("pg1")]

                prefetch_expert(0, 0)
                sSI.issue(lambda s: s.dma_start(GWF, gwt_img),
                          writes=[rGWF])
                P.emit("vector", lambda v: v.tensor_copy(GW[:, 0:128], GWF),
                       reads=[rGWF], writes=[rGW], inc="ve")
                P.emit("vector", lambda v: v.tensor_sub(
                    GW[:, 128:256], GWF, GW[:, 0:128]),
                    reads=[rGWF], writes=[rGW], inc="ve")

                scr = lambda lane: SCRF32[:, lane * D:(lane + 1) * D]
                xlo = lambda lane: XLO[:, lane * D:(lane + 1) * D]

                def do_tt_ln(m):
                    lane = m % 2
                    ln_chain(acc(m), rACC[m], scr(lane), rSCR[lane], lane,
                             sum_on_scalar=True)
                    P.emit("scalar", lambda s, lane=lane: s.copy(
                        xlnb(lane), scr(lane)),
                        reads=[rSCR[lane]], writes=[rXLNBF[lane]])
                    P.emit("vector", lambda v, lane=lane: v.tensor_sub(
                        xlo(lane), scr(lane), xlnb(lane)),
                        reads=[rSCR[lane], rXLNBF[lane]], writes=[rXLO[lane]],
                        inc="ve")

                def do_tt_tp(m):
                    lane = m % 2
                    sTP[m % 2].issue(lambda s: s.dma_start(
                        XF[:, :].rearrange("p (a b) -> p a b", b=NT)
                        [:, :, m * 128:(m + 1) * 128],
                        xlnb(lane)[:, :].rearrange("p (c q) -> p c q", q=128),
                        transpose=True),
                        reads=[rXLNBF[lane]], writes=[rXF])
                    sTP[2 + (m % 2)].issue(lambda s, lane=lane: s.dma_start(
                        XLOT[:, lane * D:(lane + 1) * D]
                        .rearrange("p (c q) -> p c q", q=128),
                        xlo(lane)[:, :].rearrange("p (c q) -> p c q", q=128),
                        transpose=True),
                        reads=[rXLO[lane]], writes=[rXLOT[lane]])

                def do_tt_gate(m):
                    lane = m % 2
                    pg, rpg = PG[m % 2], rPG[m % 2]
                    fns = []
                    n = 24
                    idx = 0
                    for (sn, goff) in (("hi", 0), ("lo", 0), ("hi", 128)):
                        for dch in range(8):
                            def mk(sn=sn, goff=goff, dch=dch, idx=idx, lane=lane,
                                   pg=pg):
                                def fn(t):
                                    lhs = (xf(dch, m * 128, 128) if sn == "hi"
                                           else XLOT[:, lane * D + dch * 128:
                                                     lane * D + (dch + 1) * 128])
                                    return t.matmul(
                                        pg[:, 0:16], lhs,
                                        GW[:, goff + dch * 16: goff + (dch + 1) * 16],
                                        start=(idx == 0), stop=(idx == n - 1))
                                return fn
                            fns.append(mk())
                            idx += 1
                    P.group("tensor", fns, reads=[rXF, rXLOT[lane], rGW],
                            writes=[rpg], inc="pe")
                    P.emit("vector", lambda v, pg=pg: v.tensor_copy(
                        LOG[:, m * 16:(m + 1) * 16], pg[:, 0:16]),
                        reads=[rpg], writes=[rLOG], inc="ve")

                for m in range(NSLOT):
                    if m >= 2:
                        do_tt_gate(m - 2)
                    do_tt_ln(m)
                    if m >= 1:
                        do_tt_tp(m - 1)
                do_tt_tp(NSLOT - 1)
                do_tt_gate(NSLOT - 2)
                do_tt_gate(NSLOT - 1)


                b3 = lambda a, inner: a.rearrange("p (a b) -> p a b", b=inner)
                P.emit("scalar", lambda s: s.activation(
                    EXPS[:, :], LOG[:, :], AF.Exp),
                    reads=[rLOG], writes=[rEXPS], inc="sc")
                P.emit("vector", lambda v: v.reduce_sum(
                    smc(C_GS, NSLOT), b3(EXPS[:, :], 16), axis=AX.X),
                    reads=[rEXPS], writes=[rGSUM])
                P.emit("vector", lambda v: v.reciprocal(
                    smc(C_GR, NSLOT), smc(C_GS, NSLOT)),
                    reads=[rGSUM], writes=[rGSUM])
                P.emit("vector", lambda v: v.tensor_tensor(
                    b3(GSC[:, :], 16), b3(EXPS[:, :], 16),
                    _bcast(smc(C_GR, NSLOT), 16), ALU.mult),
                    reads=[rEXPS, rGSUM], writes=[rGSC])
                P.emit("vector", lambda v: v.reduce_max(
                    GMAX, b3(GSC[:, :], 4), axis=AX.X),
                    reads=[rGSC], writes=[rGMAX])
                P.emit("vector", lambda v: v.reduce_max(
                    smc(C_GS, NSLOT), b3(GMAX, 4), axis=AX.X, negate=True),
                    reads=[rGMAX], writes=[rGSUM])
                P.emit("vector", lambda v: v.tensor_scalar(
                    smc(C_GR, NSLOT), smc(C_GS, NSLOT), -1.0, None, ALU.mult),
                    reads=[rGSUM], writes=[rGSUM])
                P.emit("vector", lambda v: v.tensor_tensor(
                    b3(GTH, 4), b3(GMAX, 4),
                    _bcast(smc(C_GR, NSLOT), 4), ALU.is_ge),
                    reads=[rGMAX, rGSUM], writes=[rGTH])
                P.emit("vector", lambda v: v.scalar_tensor_tensor(
                    WRK[:, 0:4 * NSLOT], GTH, -BIG, GMAX,
                    ALU.mult, ALU.add),
                    reads=[rGTH, rGMAX], writes=[rWRK])
                P.emit("vector", lambda v: v.reduce_max(
                    smc(C_GS, NSLOT), b3(WRK[:, 0:4 * NSLOT], 4), axis=AX.X,
                    negate=True),
                    reads=[rWRK], writes=[rGSUM])
                P.emit("vector", lambda v: v.tensor_scalar(
                    smc(C_GR, NSLOT), smc(C_GS, NSLOT), -1.0, None, ALU.mult),
                    reads=[rGSUM], writes=[rGSUM])
                P.emit("vector", lambda v: v.tensor_tensor(
                    b3(GTH, 4), b3(GMAX, 4),
                    _bcast(smc(C_GR, NSLOT), 4), ALU.is_ge),
                    reads=[rGMAX, rGSUM], writes=[rGTH])
                P.emit("vector", lambda v: v.tensor_tensor(
                    b3(MS[:, :], 4), b3(GSC[:, :], 4),
                    _bcast(GTH, 4), ALU.mult),
                    reads=[rGSC, rGTH], writes=[rMS])
                P.emit("vector", lambda v: v.tensor_copy(WRK[:, :], MS[:, :]),
                       reads=[rMS], writes=[rWRK])
                for _ in range(3):
                    P.emit("vector", lambda v: v.reduce_max(
                        smc(C_GS, NSLOT), b3(WRK[:, :], 16), axis=AX.X, negate=True),
                        reads=[rWRK], writes=[rGSUM])
                    P.emit("vector", lambda v: v.tensor_scalar(
                        smc(C_GR, NSLOT), smc(C_GS, NSLOT), -1.0, None, ALU.mult),
                        reads=[rGSUM], writes=[rGSUM])
                    P.emit("vector", lambda v: v.tensor_tensor(
                        b3(SEL[:, :], 16), b3(WRK[:, :], 16),
                        _bcast(smc(C_GR, NSLOT), 16), ALU.is_ge),
                        reads=[rWRK, rGSUM], writes=[rSEL])
                    P.emit("vector", lambda v: v.scalar_tensor_tensor(
                        WRK[:, :], SEL[:, :], -BIG, WRK[:, :], ALU.mult, ALU.add),
                        reads=[rSEL], writes=[rWRK])
                P.emit("vector", lambda v: v.reduce_max(
                    smc(C_GS, NSLOT), b3(WRK[:, :], 16), axis=AX.X, negate=True),
                    reads=[rWRK], writes=[rGSUM])
                P.emit("vector", lambda v: v.tensor_scalar(
                    smc(C_GR, NSLOT), smc(C_GS, NSLOT), -1.0, None, ALU.mult),
                    reads=[rGSUM], writes=[rGSUM])
                P.emit("vector", lambda v: v.tensor_tensor(
                    b3(SEL[:, :], 16), b3(MS[:, :], 16),
                    _bcast(smc(C_GR, NSLOT), 16), ALU.is_ge),
                    reads=[rMS, rGSUM], writes=[rSEL])
                P.emit("vector", lambda v: v.tensor_mul(
                    CC[:, :], GSC[:, :], SEL[:, :]),
                    reads=[rGSC, rSEL], writes=[rCC], inc="ve")
                P.flush_block()
            s2a.close()

            s2b = contextlib.ExitStack()
            H0 = s2b.enter_context(nc.psum_tensor("H0", [128, 1024], F32))
            H1 = s2b.enter_context(nc.psum_tensor("H1", [128, 1024], F32))
            H2 = s2b.enter_context(nc.psum_tensor("H2", [128, 1024], F32))
            H3 = s2b.enter_context(nc.psum_tensor("H3", [128, 1024], F32))
            if True:
                HH = [H0, H1, H2, H3]
                rH = [R("h0"), R("h1"), R("h2"), R("h3")]

                def do_expert(e):
                    pi = e % 2
                    if e + 1 < NEXP + 2:
                        prefetch_expert(e + 1, 1 - pi)

                    def w2_tt(tt):
                        yp, ryp = HH[tt % 2], rH[tt % 2]
                        fns = []
                        for ich in range(4):
                            for half in range(2):
                                fns.append(
                                    lambda t, ich=ich, half=half, tt=tt, yp=yp,
                                    pi=pi: t.matmul(
                                        yp[:, half * 512:(half + 1) * 512],
                                        APR[:, ich * NT + tt * 128:
                                            ich * NT + (tt + 1) * 128],
                                        W2B[:, pi * 4096 + ich * 1024 + half * 512:
                                            pi * 4096 + ich * 1024 + (half + 1) * 512],
                                        start=(ich == 0), stop=(ich == 3)))
                        P.group("tensor", fns, reads=[rAPR, rW2[pi]], writes=[ryp],
                                inc="pe")
                        if e < NEXP:
                            P.emit("vector",
                                   lambda v, tt=tt, yp=yp, e=e: v.scalar_tensor_tensor(
                                       acc(tt), yp[:, :],
                                       CC[:, tt * 16 + e: tt * 16 + e + 1],
                                       acc(tt), ALU.mult, ALU.add),
                                   reads=[ryp, rCC], writes=[rACC[tt]], inc="ve")
                        else:
                            P.emit("vector", lambda v, tt=tt, yp=yp: v.tensor_add(
                                acc(tt), acc(tt), yp[:, :]),
                                reads=[ryp], writes=[rACC[tt]], inc="ve")
                        if e == NEXP + 1:
                            sSO[tt % 4].issue(lambda s, tt=tt: s.dma_start(
                                out_d[tt], acc(tt)),
                                reads=[rACC[tt]])

                    for th in range(NTH):
                        for ich in range(4):
                            hp = (th * 4 + ich) % 2
                            pairs = ((HH[2 * hp], rH[2 * hp], W1B, rW1[pi]),
                                     (HH[2 * hp + 1], rH[2 * hp + 1], W3B, rW3[pi]))
                            for (ps, rps, wbuf, rwb) in pairs:
                                fns = []
                                for dch in range(8):
                                    for half in range(TCH // 512):
                                        fns.append(
                                            lambda t, dch=dch, half=half, ps=ps,
                                            wbuf=wbuf, ich=ich, th=th, pi=pi: t.matmul(
                                                ps[:, half * 512:(half + 1) * 512],
                                                wbuf[:, pi * 4096 + dch * 512 + ich * 128:
                                                     pi * 4096 + dch * 512 + (ich + 1) * 128],
                                                xf(dch, th * TCH + half * 512, 512),
                                                start=(dch == 0), stop=(dch == 7)))
                                P.group("tensor", fns, reads=[rwb, rXF],
                                        writes=[rps], inc="pe")
                            h1ps, rh1 = HH[2 * hp], rH[2 * hp]
                            h3ps, rh3 = HH[2 * hp + 1], rH[2 * hp + 1]
                            P.emit("scalar", lambda s, h1ps=h1ps: s.activation(
                                SCRF32[:, 0:TCH], h1ps[:, 0:TCH], AF.Silu),
                                reads=[rh1], writes=[rSCR[0]], inc="sc")
                            P.emit("vector",
                                   lambda v, h3ps=h3ps, ich=ich, th=th: v.tensor_mul(
                                       APR[:, ich * NT + th * TCH:
                                           ich * NT + (th + 1) * TCH],
                                       SCRF32[:, 0:TCH], h3ps[:, 0:TCH]),
                                   reads=[rSCR[0], rh3], writes=[rAPR], inc="ve")
                    for tt in range(NSLOT):
                        w2_tt(tt)

                for e in range(NEXP + 2):
                    do_expert(e)

                P.emit("sync", lambda s: s.nop(),
                       extra=[("d_so0", P.cnt["d_so0"], "dma"),
                              ("d_so1", P.cnt["d_so1"], "dma"),
                              ("d_so2", P.cnt["d_so2"], "dma"),
                              ("d_so3", P.cnt["d_so3"], "dma")])
                P.flush_block()
            s2b.close()
        s2.close()

    return nc



def _img_km(M, r):
    R_, C = M.shape
    assert R_ == r * 128
    return np.ascontiguousarray(M.reshape(r, 128, C).transpose(1, 0, 2))


_CACHED = {}


def _get_nc(attn_only=False):
    if attn_only not in _CACHED:
        _CACHED[attn_only] = build_program(attn_only)
    return _CACHED[attn_only]


def kernel(L0_attn_input_data, L0_attn_qk, L0_attn_ov, L0_ffn_gate_w,
           L0_ffn_w1, L0_ffn_w2, L0_ffn_w3, L0_ffn_ws1, L0_ffn_ws2, L0_ffn_ws3,
           _attn_only=False):
    x = np.ascontiguousarray(np.asarray(L0_attn_input_data, dtype=np.float32))
    qk = np.asarray(L0_attn_qk, dtype=np.float32)
    ov = np.asarray(L0_attn_ov, dtype=np.float32)
    gw = np.asarray(L0_ffn_gate_w, dtype=np.float32)
    w1 = np.asarray(L0_ffn_w1, dtype=np.float32)
    w2 = np.asarray(L0_ffn_w2, dtype=np.float32)
    w3 = np.asarray(L0_ffn_w3, dtype=np.float32)
    ws1 = np.asarray(L0_ffn_ws1, dtype=np.float32)
    ws2 = np.asarray(L0_ffn_ws2, dtype=np.float32)
    ws3 = np.asarray(L0_ffn_ws3, dtype=np.float32)

    qk_i = _img_km(qk, 8).reshape(128, 8 * D)
    ov_i = _img_km(ov, 8).reshape(128, 8 * D)
    gwt_i = _img_km(np.ascontiguousarray(gw.T), 8).reshape(128, 128)
    w1_i = np.stack([_img_km(w1[e], 8).reshape(128, 8 * INTER) for e in range(NEXP)])
    w3_i = np.stack([_img_km(w3[e], 8).reshape(128, 8 * INTER) for e in range(NEXP)])
    w2_i = np.stack([_img_km(w2[e], 4).reshape(128, 4 * D) for e in range(NEXP)])
    ws1_i = _img_km(ws1, 8)
    ws3_i = _img_km(ws3, 8)
    ws2_i = _img_km(ws2, 8)
    ident = np.eye(128, dtype=np.float32)

    in_maps = []
    for c in range(8):
        b, c4 = c // 4, c % 4
        xb = np.ascontiguousarray(x[b])
        xo = np.ascontiguousarray(xb.reshape(64, 128, D)[c4::4])
        jj = np.arange(512)[None, :]
        ii = np.arange(128)[:, None]
        mask = np.where(jj <= ii + 128 * c4, 0.0, -BIG).astype(np.float32)
        in_maps.append({
            "x_seq": xb, "x_own": xo, "mask": mask, "ident": ident,
            "qk_img": qk_i, "ov_img": ov_i, "gwt_img": gwt_i,
            "w1_img": w1_i, "w3_img": w3_i, "w2_img": w2_i,
            "ws1_img": ws1_i, "ws3_img": ws3_i, "ws2_img": ws2_i,
        })

    nc = _get_nc(_attn_only)
    res = run_bass_kernel_spmd(nc, in_maps, core_ids=list(range(8)))

    out = np.empty((2, SEQ, D), dtype=np.float32)
    for c in range(8):
        b, c4 = c // 4, c % 4
        out[b].reshape(64, 128, D)[c4::4] = res.results[c]["out"]
    return out


# version 21
# speedup vs baseline: 1.0400x; 1.0057x over previous
import numpy as np
import concourse.bass as bass
import concourse.mybir as mybir
from concourse.bass_utils import run_bass_kernel_spmd

F32 = mybir.dt.float32
BF16 = mybir.dt.bfloat16
AF = mybir.ActivationFunctionType
ALU = mybir.AluOpType
AX = mybir.AxisListType

D = 1024
SEQ = 8192
NSLOT = 16
NPANEL = 8
PKT = 8
NEXP = 16
INTER = 512
EPS = 1e-5
BIG = 1.0e30


class _Res:
    __slots__ = ("w", "r", "name")
    def __init__(self, name):
        self.name = name
        self.w = []
        self.r = []


class _Prog:
    ENGINES = ("tensor", "vector", "scalar", "sync", "gpsimd")

    def __init__(self, nc):
        self.nc = nc
        self.q = {e: [] for e in self.ENGINES}
        self.cnt = {}
        self.semh = {}
        self.waited = {e: {} for e in self.ENGINES}

    def sem(self, name, stack):
        self.semh[name] = stack.enter_context(self.nc.semaphore(name))
        self.cnt[name] = 0

    ESEM = {"tensor": "pe", "vector": "ve", "scalar": "sc", "gpsimd": "gp",
            "sync": None}

    def _resolve(self, eng, toks):
        out = {}
        for t in toks:
            s, v, src = t
            if s is None:
                if src == eng:
                    continue
                raise RuntimeError(
                    f"cross-engine dep on un-inc'd op (src={src}, consumer={eng})")
            out[s] = max(out.get(s, 0), v)
        return sorted(out.items())

    def _hazard_toks(self, reads, writes, extra):
        toks = list(extra)
        for r in reads:
            toks += r.w
        for w in writes:
            toks += w.w + w.r
        return toks

    def _record(self, tok, reads, writes):
        for r in reads:
            r.r.append(tok)
        for w in writes:
            w.w = [tok]
            w.r = []

    def emit(self, eng, fn, reads=(), writes=(), extra=(), inc="auto", inc_by=1):
        if inc == "auto":
            inc = self.ESEM[eng]
        waits = self._resolve(eng, self._hazard_toks(reads, writes, extra))
        if inc is not None:
            self.cnt[inc] += inc_by
            tok = (inc, self.cnt[inc], eng)
        else:
            tok = (None, 0, eng)
        self.q[eng].append((fn, waits, inc, inc_by))
        self._record(tok, reads, writes)
        return tok

    def group(self, eng, fns, reads=(), writes=(), extra=(), inc="auto"):
        if inc == "auto":
            inc = self.ESEM[eng]
        waits = self._resolve(eng, self._hazard_toks(reads, writes, extra))
        tok = (None, 0, eng)
        n = len(fns)
        for i, fn in enumerate(fns):
            s = inc if (i == n - 1 and inc is not None) else None
            if s is not None:
                self.cnt[s] += 1
                tok = (s, self.cnt[s], eng)
            self.q[eng].append((fn, waits if i == 0 else [], s, 1))
        self._record(tok, reads, writes)
        return tok

    def flush_block(self):
        nc = self.nc
        snap = dict(self.cnt)
        qs = {e: list(self.q[e]) for e in self.ENGINES}
        for e in self.ENGINES:
            self.q[e] = []

        def body(engobj, e):
            for fn, waits, inc, inc_by in qs[e]:
                for s, v in waits:
                    engobj.wait_ge(self.semh[s], v)
                inst = fn(engobj)
                if inc is not None:
                    inst.then_inc(self.semh[inc], inc_by)

        with nc.Block() as block:
            if qs["tensor"]:
                block.tensor(lambda t: body(t, "tensor"))
            if qs["vector"]:
                block.vector(lambda t: body(t, "vector"))
            if qs["scalar"]:
                block.scalar(lambda t: body(t, "scalar"))
            if qs["sync"]:
                block.sync(lambda t: body(t, "sync"))
            if qs["gpsimd"]:
                block.gpsimd(lambda t: body(t, "gpsimd"))
        del snap


class _Stream:
    def __init__(self, prog, sem, eng):
        self.P, self.sem, self.eng = prog, sem, eng

    def _idle_wait(self):
        if self.P.cnt[self.sem]:
            return [(self.sem, self.P.cnt[self.sem], "dma")]
        return []

    def issue(self, fn, reads=(), writes=()):
        return self.P.emit(self.eng, fn, reads=reads, writes=writes,
                           extra=self._idle_wait(), inc=self.sem, inc_by=16)

    def issue_batch(self, items):
        P = self.P
        extra = self._idle_wait()
        rws = []
        for i, (fn, reads, writes) in enumerate(items):
            waits = P._resolve(self.eng, P._hazard_toks(
                reads, writes, extra if i == 0 else ()))
            P.cnt[self.sem] += 16
            P.q[self.eng].append((fn, waits, self.sem, 16))
            rws.append((reads, writes))
        tok = (self.sem, P.cnt[self.sem], "dma")
        for reads, writes in rws:
            P._record(tok, reads, writes)
        return tok


def _bcast(ap, inner):
    return bass.AP(ap.tensor, ap.offset,
                   [list(ap.ap[0]), list(ap.ap[1]), [0, inner]])



def build_program(attn_only=False):
    import contextlib
    nc = bass.Bass("TRN2", target_bir_lowering=False)

    x_seq = nc.dram_tensor("x_seq", [SEQ, D], F32, kind="ExternalInput").ap()
    x_own = nc.dram_tensor("x_own", [NSLOT, 128, D], F32, kind="ExternalInput").ap()
    maskd = nc.dram_tensor("mask", [128, 512], F32, kind="ExternalInput").ap()
    identd = nc.dram_tensor("ident", [128, 128], F32, kind="ExternalInput").ap()
    qk_img = nc.dram_tensor("qk_img", [128, 8 * D], F32, kind="ExternalInput").ap()
    ov_img = nc.dram_tensor("ov_img", [128, 8 * D], F32, kind="ExternalInput").ap()
    gwt_img = nc.dram_tensor("gwt_img", [128, 128], F32, kind="ExternalInput").ap()
    w1_img = nc.dram_tensor("w1_img", [NEXP, 128, 8 * INTER], F32, kind="ExternalInput").ap()
    w3_img = nc.dram_tensor("w3_img", [NEXP, 128, 8 * INTER], F32, kind="ExternalInput").ap()
    w2_img = nc.dram_tensor("w2_img", [NEXP, 128, 4 * D], F32, kind="ExternalInput").ap()
    ws1_img = nc.dram_tensor("ws1_img", [128, 8, D], F32, kind="ExternalInput").ap()
    ws3_img = nc.dram_tensor("ws3_img", [128, 8, D], F32, kind="ExternalInput").ap()
    ws2_img = nc.dram_tensor("ws2_img", [128, 8, D], F32, kind="ExternalInput").ap()
    out_d = nc.dram_tensor("out", [NSLOT, 128, D], F32, kind="ExternalOutput").ap()

    P = _Prog(nc)
    NT = NSLOT * 128

    with contextlib.ExitStack() as stk:
        for s in ("d_si", "d_gi", "d_sx0", "d_sx1", "d_sx2", "d_sf",
                  "d_so0", "d_so1", "d_so2", "d_so3",
                  "d_sw0", "d_sw1", "d_tp0", "d_tp1", "d_tp2", "d_tp3",
                  "pe", "ve", "sc", "gp"):
            P.sem(s, stk)
        sSI = _Stream(P, "d_si", "sync")
        sGI = _Stream(P, "d_gi", "gpsimd")
        sSX = [_Stream(P, "d_sx0", "sync"), _Stream(P, "d_sx1", "sync"),
               _Stream(P, "d_sx2", "sync")]
        sSF = _Stream(P, "d_sf", "sync")
        sSO = [_Stream(P, "d_so0", "sync"), _Stream(P, "d_so1", "sync"),
               _Stream(P, "d_so2", "sync"), _Stream(P, "d_so3", "sync")]
        sSW = [_Stream(P, "d_sw0", "gpsimd"), _Stream(P, "d_sw1", "gpsimd")]
        sTP = [_Stream(P, "d_tp0", "sync"), _Stream(P, "d_tp1", "sync"),
               _Stream(P, "d_tp2", "sync"), _Stream(P, "d_tp3", "sync")]

        sb = lambda name, shape, dt: stk.enter_context(nc.sbuf_tensor(name, shape, dt))
        CONST = sb("CONST", [128, 4], F32)
        SM = sb("SM", [128, 96], F32)
        ACC = sb("ACC", [128, NSLOT * D], F32)
        STG = sb("STG", [128, 3 * D], F32)
        XOWNF = sb("XOWNF", [128, D], F32)
        IDB = sb("IDB", [128, 128], BF16)
        XLNBF = sb("XLNBF", [128, 2 * D], BF16)
        MASKBB = sb("MASKBB", [128, 512], BF16)

        acc = lambda m: ACC[:, m * D:(m + 1) * D]
        smc = lambda c, n=1: SM[:, c:c + n]
        stg = lambda i: STG[:, (i % 3) * D:((i % 3) + 1) * D]
        xlnb = lambda lane: XLNBF[:, lane * D:(lane + 1) * D]

        C_NM, C_LL = 0, 16
        LN_MSUM, LN_SSQ, LN_NMEAN, LN_NM2, LN_VAR, LN_RSTD, LN_NMR = range(7)
        C_NRM, C_TMIN, C_TMD, C_ALPHA, C_RSUM, C_RL = 48, 50, 52, 54, 57, 59
        C_GS, C_GR = 64, 80

        R = _Res
        rCONST = R("const")
        rLN = [R("ln0"), R("ln1")]
        rNM = [R(f"nm{m}") for m in range(NSLOT)]
        rLL = [R(f"ll{m}") for m in range(NSLOT)]
        rNRM = [R("nrm0"), R("nrm1")]
        rTMIN = [R("tmin0"), R("tmin1")]
        rTMD = [R("tmd0"), R("tmd1")]
        rALPHA = [R("al0"), R("al1"), R("al2")]
        rRSUM = [R("rs0"), R("rs1")]
        rRL = [R("rl0"), R("rl1")]
        rACC = [R(f"acc{m}") for m in range(NSLOT)]
        rSTG = [R("stg0"), R("stg1"), R("stg2")]
        rOVB, rMASK, rIDB = R("ovb"), R("mask"), R("idb")
        rMASKBB = R("maskbb")
        rXLNBF = [R("xlnbf0"), R("xlnbf1")]
        rSCR = [R("scr0"), R("scr1")]
        rXOWNs = R("xown")

        def ln_chain(src_ap, src_res, dst_ap, dst_res, lane, se="vector",
                     sum_on_scalar=False):
            rl = rLN[lane]
            base = 32 + 8 * lane
            c = lambda k: smc(base + k)
            dump = xlnb(lane)
            rdump = rXLNBF[lane]
            if sum_on_scalar:
                P.emit("scalar", lambda s: s.activation(
                    dump, src_ap, AF.Identity, accum_out=c(LN_MSUM)),
                    reads=[src_res], writes=[rl, rdump])
            else:
                P.emit(se, lambda v: v.tensor_scalar(
                    dump, src_ap, 1.0, 0.0, ALU.mult, ALU.add,
                    accum_out=c(LN_MSUM)),
                    reads=[src_res], writes=[rl, rdump])
            P.emit(se, lambda v: v.scalar_tensor_tensor(
                dump, src_ap, 1.0, src_ap, ALU.mult, ALU.mult,
                accum_out=c(LN_SSQ)),
                reads=[src_res], writes=[rl, rdump])
            P.emit(se, lambda v: v.tensor_scalar(
                c(LN_NMEAN), c(LN_MSUM), -1.0 / D, None, ALU.mult),
                reads=[rl], writes=[rl])
            P.emit(se, lambda v: v.scalar_tensor_tensor(
                c(LN_NM2), c(LN_NMEAN), -1.0, c(LN_NMEAN), ALU.mult, ALU.mult),
                reads=[rl], writes=[rl])
            P.emit(se, lambda v: v.scalar_tensor_tensor(
                c(LN_VAR), c(LN_SSQ), 1.0 / D, c(LN_NM2), ALU.mult, ALU.add),
                reads=[rl], writes=[rl])
            P.emit("scalar", lambda s: s.activation(
                c(LN_VAR), c(LN_VAR), AF.Ln, bias=CONST[:, 2:3]),
                reads=[rCONST, rl], writes=[rl])
            P.emit("scalar", lambda s: s.activation(
                c(LN_RSTD), c(LN_VAR), AF.Exp, scale=CONST[:, 1:2]),
                reads=[rCONST, rl], writes=[rl])
            P.emit("scalar", lambda s: s.activation(
                c(LN_NMR), c(LN_RSTD), AF.Identity, scale=c(LN_NMEAN)),
                reads=[rl], writes=[rl])
            wl = [dst_res] if dst_res is not None else []
            tnorm = P.emit("scalar", lambda s: s.activation(
                dst_ap, src_ap, AF.Identity, scale=c(LN_RSTD), bias=c(LN_NMR)),
                reads=[src_res, rl], writes=wl)
            return tnorm

        P.emit("gpsimd", lambda g: g.memset(SM[:, :], 0.0), writes=[rLN[0], rLN[1]])
        P.emit("gpsimd", lambda g: g.memset(CONST[:, 0:1], 1.0 / 32.0), writes=[rCONST])
        P.emit("gpsimd", lambda g: g.memset(CONST[:, 1:2], -0.5), writes=[rCONST])
        P.emit("gpsimd", lambda g: g.memset(CONST[:, 2:3], EPS), writes=[rCONST], inc="gp")

        with (
            nc.sbuf_tensor("Q", [128, 8 * NT], BF16) as Q,
            nc.sbuf_tensor("OVB", [128, 8 * D], BF16) as OVB,
        ):
            rQ = R("Q")
            qsl = lambda dch, lo, n: Q[:, dch * NT + lo: dch * NT + lo + n]

            with (
                nc.sbuf_tensor("XLNT", [128, 8 * NT], BF16) as XLNT,
                nc.sbuf_tensor("QKB", [128, 8 * D], BF16) as QKB,
                nc.psum_tensor("PSA", [128, 512], F32) as PSA,
                nc.psum_tensor("PSB", [128, 512], F32) as PSB,
                nc.psum_tensor("PSC", [128, 512], F32) as PSC,
                nc.psum_tensor("PSD", [128, 512], F32) as PSD,
                nc.psum_tensor("PSE", [128, 512], F32) as PSE,
                nc.psum_tensor("PSF", [128, 512], F32) as PSF,
            ):
                rXLNT, rQKB = R("XLNT"), R("QKB")
                QPS = [(PSA, R("PSA")), (PSB, R("PSB")),
                       (PSC, R("PSC")), (PSD, R("PSD")),
                       (PSE, R("PSE")), (PSF, R("PSF"))]
                sGI.issue_batch([
                    (lambda g: g.dma_start(IDB[:, :], identd), (), [rIDB]),
                    (lambda g: g.dma_start(MASKBB[:, :], maskd), (), [rMASKBB]),
                    (lambda g: g.dma_start(OVB[:, :], ov_img), (), [rOVB]),
                    (lambda g: g.dma_start(QKB[:, :], qk_img), (), [rQKB]),
                ])

                def pre_ln(m):
                    lane = m % 2
                    sSX[m % 3].issue(lambda s: s.dma_start(stg(m), x_own[m]),
                                     writes=[rSTG[m % 3]])
                    ln_chain(stg(m), rSTG[m % 3], xlnb(lane), rXLNBF[lane], lane)

                def pre_tp(m):
                    lane = m % 2
                    sTP[m % 4].issue(lambda s: s.dma_start(
                        XLNT[:, :].rearrange("p (a b) -> p a b", b=NT)
                        [:, :, m * 128:(m + 1) * 128],
                        xlnb(lane)[:, :].rearrange("p (c q) -> p c q", q=128),
                        transpose=True),
                        reads=[rXLNBF[lane]], writes=[rXLNT])

                NSG = NT // 512

                def q_group(g):
                    ech, sg = g // NSG, g % NSG
                    bank, rbank = QPS[g % 6]
                    fns = [(lambda t, dch=dch: t.matmul(
                        bank[:, 0:512],
                        QKB[:, dch * D + ech * 128: dch * D + (ech + 1) * 128],
                        XLNT[:, dch * NT + sg * 512: dch * NT + (sg + 1) * 512],
                        start=(dch == 0), stop=(dch == 7))) for dch in range(8)]
                    P.group("tensor", fns, reads=[rQKB, rXLNT], writes=[rbank],
                            inc="pe")
                    if g % 2 == 0:
                        P.emit("scalar", lambda s: s.activation(
                            qsl(ech, sg * 512, 512), bank[:, 0:512], AF.Identity,
                            scale=CONST[:, 0:1]),
                            reads=[rbank, rCONST], writes=[rQ], inc="sc")
                    else:
                        P.emit("vector", lambda v: v.tensor_scalar_mul(
                            qsl(ech, sg * 512, 512), bank[:, 0:512],
                            CONST[:, 0:1]),
                            reads=[rbank, rCONST], writes=[rQ], inc="ve")

                qready = []
                qi = 0
                for m in range(NSLOT):
                    pre_ln(m)
                    if m >= 1:
                        pre_tp(m - 1)
                    if m % 4 == 3 and m >= 7:
                        sg = (m - 4) // 4
                        qready += [ech * NSG + sg for ech in range(8)]
                    while qi < len(qready) and qi < max(0, (m - 6)) * 3:
                        q_group(qready[qi])
                        qi += 1
                pre_tp(NSLOT - 1)
                done = set(qready[:qi])
                rest = [g for g in qready if g not in done]
                rest += [g for g in range(8 * NSG) if g not in done and g not in rest]
                for g in rest:
                    q_group(g)
                P.flush_block()

            with (
                nc.sbuf_tensor("KP0", [128, 8 * 1024], BF16) as KP0,
                nc.sbuf_tensor("KP1", [128, 8 * 1024], BF16) as KP1,
                nc.sbuf_tensor("VP0", [128, 8 * 1024], BF16) as VP0,
                nc.sbuf_tensor("VP1", [128, 8 * 1024], BF16) as VP1,
                nc.sbuf_tensor("PBF", [128, 2048], BF16) as PBF,
                nc.sbuf_tensor("PTS", [128, 3072], BF16) as PTS,
                nc.psum_tensor("PS_SE", [128, 1024], F32) as PS_SE,
                nc.psum_tensor("PS_SO", [128, 1024], F32) as PS_SO,
                nc.psum_tensor("PS_PV", [128, 1024], F32) as PS_PV,
                nc.psum_tensor("PSPT2", [128, 1024], BF16) as PSPT2,
                nc.psum_tensor("PSPT2B", [128, 1024], BF16) as PSPT2B,
            ):
                KP, VP = [KP0, KP1], [VP0, VP1]
                rKP = [R("kp0"), R("kp1")]
                rVP = [R("vp0"), R("vp1")]
                rVPk = [[R(f"vp{b}k{k}") for k in range(PKT)] for b in range(2)]
                rPBF = [R("pbf0"), R("pbf1")]
                rPTS = [R("pts0"), R("pts1"), R("pts2")]
                FINBF, FINT = PBF, PTS
                rFINBF, rFINT = rPBF[0], rPTS[0]
                rSE, rSO, rPV, rPT2 = R("se"), R("so"), R("pv"), R("pt2")
                rPT2B = R("pt2b")

                def kslice(buf, dch, lo, n):
                    return KP[buf][:, dch * 1024 + lo: dch * 1024 + lo + n]

                def vslice(buf, kt, lo, n):
                    return VP[buf][:, kt * 1024 + lo: kt * 1024 + lo + n]

                def emit_prod_load(p, kt):
                    buf = p % 2
                    gi = p * PKT + kt
                    sSX[gi % 3].issue(lambda s: s.dma_start(
                        stg(gi),
                        x_seq[p * 1024 + kt * 128: p * 1024 + (kt + 1) * 128, :]),
                        writes=[rSTG[gi % 3]])
                    ln_chain(stg(gi), rSTG[gi % 3],
                             vslice(buf, kt, 0, 1024), rVPk[buf][kt], kt % 2)

                def emit_prod_tp(p, kt):
                    buf = p % 2
                    sTP[kt % 4].issue(lambda s: s.dma_start(
                        KP[buf][:, :].rearrange("p (a b) -> p a b", b=1024)
                        [:, :, kt * 128:(kt + 1) * 128],
                        vslice(buf, kt, 0, 1024)
                        .rearrange("p (c q) -> p c q", q=128),
                        transpose=True),
                        reads=[rVPk[buf][kt]], writes=[rKP[buf]])

                def emit_qk(p, buf, i, m):
                    nkt = 4 if m == 2 * p else 8
                    nbank = nkt // 4
                    mask_bank = (0 if m == 2 * p else
                                 (1 if m == 2 * p + 1 else None))
                    SP, rSP = ((PS_SE, rSE) if i % 2 == 0 else (PS_SO, rSO))
                    fns = []
                    if mask_bank is not None:
                        fns.append(lambda t, mb=mask_bank: t.matmul(
                            SP[:, mb * 512:(mb + 1) * 512],
                            IDB[:, :], MASKBB[:, :],
                            start=True, stop=False))
                    for dch in range(8):
                        for bank in range(nbank):
                            st = (dch == 0) and (bank != mask_bank)
                            sp_ = (dch == 7)
                            fns.append(lambda t, bank=bank, dch=dch, st=st,
                                       sp_=sp_: t.matmul(
                                SP[:, bank * 512:(bank + 1) * 512],
                                qsl(dch, m * 128, 128),
                                kslice(buf, dch, bank * 512, 512),
                                start=st, stop=sp_))
                    rd = [rQ, rKP[buf]]
                    if mask_bank is not None:
                        rd += [rIDB, rMASKBB]
                    P.group("tensor", fns, reads=rd, writes=[rSP], inc="pe")
                    return nkt, SP, rSP

                def emit_max(p, i, m, nkt, SP, rSP):
                    par = i % 2
                    ext = nkt * 128
                    P.emit("vector", lambda v: v.reduce_max(
                        smc(C_NRM + par), SP[:, 0:ext], axis=AX.X, negate=True),
                        reads=[rSP], writes=[rNRM[par]], inc="ve")
                    if p == 0:
                        P.emit("vector", lambda v: v.tensor_copy(
                            smc(C_NM + m), smc(C_NRM + par)),
                            reads=[rNRM[par]], writes=[rNM[m]], inc="ve")
                    else:
                        P.emit("vector", lambda v: v.tensor_sub(
                            smc(C_TMD + par), smc(C_NRM + par), smc(C_NM + m)),
                            reads=[rNRM[par], rNM[m]], writes=[rTMD[par]])
                        P.emit("vector", lambda v: v.tensor_scalar_min(
                            smc(C_TMD + par), smc(C_TMD + par), 0.0),
                            reads=[rTMD[par]], writes=[rTMD[par]])
                        P.emit("vector", lambda v: v.tensor_add(
                            smc(C_NM + m), smc(C_NM + m), smc(C_TMD + par)),
                            reads=[rTMD[par], rNM[m]], writes=[rNM[m]], inc="ve")

                def emit_exp(p, i, m, nkt, SP, rSP):
                    par = i % 2
                    al = i % 3
                    ext = nkt * 128
                    if p > 0:
                        P.emit("scalar", lambda s: s.activation(
                            smc(C_ALPHA + al), smc(C_TMD + par), AF.Exp),
                            reads=[rTMD[par]], writes=[rALPHA[al]], inc="sc")
                    P.emit("scalar", lambda s: s.activation(
                        PBF[:, par * 1024: par * 1024 + ext], SP[:, 0:ext], AF.Exp,
                        bias=smc(C_NM + m), accum_out=smc(C_RSUM + par)),
                        reads=[rSP, rNM[m]], writes=[rPBF[par], rRSUM[par]],
                        inc="sc")

                def emit_llupd(p, i, m):
                    par = i % 2
                    al = i % 3
                    if p == 0:
                        P.emit("vector", lambda v: v.tensor_copy(
                            smc(C_LL + m), smc(C_RSUM + par)),
                            reads=[rRSUM[par]], writes=[rLL[m]], inc="ve")
                    else:
                        P.emit("vector", lambda v: v.scalar_tensor_tensor(
                            smc(C_LL + m), smc(C_LL + m), smc(C_ALPHA + al),
                            smc(C_RSUM + par), ALU.mult, ALU.add),
                            reads=[rRSUM[par], rALPHA[al]], writes=[rLL[m]],
                            inc="ve")

                def emit_t(p, buf, i, m, nkt):
                    par = i % 2
                    sl3 = i % 3
                    fns = [(lambda t, kt=kt: t.transpose(
                        PSPT2[:, kt * 128:(kt + 1) * 128],
                        PBF[:, par * 1024 + kt * 128: par * 1024 + (kt + 1) * 128],
                        IDB[:, :])) for kt in range(nkt)]
                    P.group("tensor", fns, reads=[rPBF[par], rIDB], writes=[rPT2],
                            inc="pe")
                    P.emit("scalar", lambda s: s.copy(
                        PTS[:, sl3 * 1024: sl3 * 1024 + nkt * 128],
                        PSPT2[:, 0:nkt * 128]),
                        reads=[rPT2], writes=[rPTS[sl3]], inc="sc")

                def emit_pv(p, buf, i, m, nkt):
                    al = i % 3
                    sl3 = i % 3
                    fns = []
                    for kt in range(nkt):
                        for half in range(2):
                            fns.append(lambda t, kt=kt, half=half: t.matmul(
                                PS_PV[:, half * 512:(half + 1) * 512],
                                PTS[:, sl3 * 1024 + kt * 128: sl3 * 1024 + (kt + 1) * 128],
                                vslice(buf, kt, half * 512, 512),
                                start=(kt == 0), stop=(kt == nkt - 1)))
                    P.group("tensor", fns,
                            reads=[rPTS[sl3]] + [rVPk[buf][kt] for kt in range(nkt)],
                            writes=[rPV], inc="pe")
                    if p == 0:
                        P.emit("vector", lambda v: v.tensor_copy(acc(m), PS_PV[:, :]),
                               reads=[rPV], writes=[rACC[m]], inc="ve")
                    else:
                        P.emit("vector", lambda v: v.scalar_tensor_tensor(
                            acc(m), acc(m), smc(C_ALPHA + al), PS_PV[:, :],
                            ALU.mult, ALU.add),
                            reads=[rPV, rALPHA[al]], writes=[rACC[m]], inc="ve")

                def emit_finalize(ma, mb):
                    items = ((ma, 0, PS_SE, rSE), (mb, 1, PS_SO, rSO))
                    sSF.issue(lambda s: s.dma_start(XOWNF[:, :], x_own[ma]),
                              writes=[rXOWNs])
                    for m, h, _, _ in items:
                        P.emit("vector", lambda v, m=m, h=h: v.reciprocal(
                            smc(C_RL + h), smc(C_LL + m)),
                            reads=[rLL[m]], writes=[rRL[h]], inc="ve")
                    for m, h, _, _ in items:
                        P.emit("scalar", lambda s, m=m, h=h: s.activation(
                            PBF[:, h * 1024:(h + 1) * 1024], acc(m), AF.Identity,
                            scale=smc(C_RL + h)),
                            reads=[rACC[m], rRL[h]], writes=[rPBF[h]], inc="sc")
                    for m, h, _, _ in items:
                        fns = [(lambda t, dch=dch, h=h: t.transpose(
                            PSPT2B[:, dch * 128:(dch + 1) * 128] if h == 0 else
                            PSPT2[:, dch * 128:(dch + 1) * 128],
                            PBF[:, h * 1024 + dch * 128: h * 1024 + (dch + 1) * 128],
                            IDB[:, :])) for dch in range(8)]
                        P.group("tensor", fns, reads=[rPBF[h], rIDB],
                                writes=[rPT2B if h == 0 else rPT2], inc="pe")
                        P.emit("scalar", lambda s, h=h: s.copy(
                            PTS[:, h * 1024:(h + 1) * 1024],
                            PSPT2B[:, :] if h == 0 else PSPT2[:, :]),
                            reads=[rPT2B if h == 0 else rPT2],
                            writes=[rPTS[h]], inc="sc")
                    for m, h, PSX, rX in items:
                        fns = []
                        for half2 in range(2):
                            for dch in range(8):
                                fns.append(lambda t, dch=dch, half2=half2,
                                           h=h, PSX=PSX: t.matmul(
                                    PSX[:, half2 * 512:(half2 + 1) * 512],
                                    PTS[:, h * 1024 + dch * 128: h * 1024 + (dch + 1) * 128],
                                    OVB[:, dch * D + half2 * 512: dch * D + (half2 + 1) * 512],
                                    start=(dch == 0), stop=(dch == 7)))
                        P.group("tensor", fns, reads=[rPTS[h], rOVB],
                                writes=[rX], inc="pe")
                        if h == 0:
                            P.emit("vector", lambda v, m=m, PSX=PSX: v.tensor_add(
                                acc(m), XOWNF[:, :], PSX[:, :]),
                                reads=[rXOWNs, rX], writes=[rACC[m]], inc="ve")
                            sSF.issue(lambda s, mb=mb: s.dma_start(
                                XOWNF[:, :], x_own[mb]),
                                writes=[rXOWNs])
                        else:
                            P.emit("vector", lambda v, m=m, PSX=PSX: v.tensor_add(
                                acc(m), XOWNF[:, :], PSX[:, :]),
                                reads=[rXOWNs, rX], writes=[rACC[m]], inc="ve")

                for kt in range(PKT):
                    emit_prod_load(0, kt)
                    if kt >= 1:
                        emit_prod_tp(0, kt - 1)
                emit_prod_tp(0, PKT - 1)

                def do_panel(p):
                    buf = p % 2
                    slots = list(range(2 * p, NSLOT))
                    prod = [(p + 1, kt) for kt in range(PKT)] if p < NPANEL - 1 else []
                    nld = ntp = 0
                    n = len(slots)
                    hist = []
                    for i, m in enumerate(slots):
                        nkt, SP, rSP = emit_qk(p, buf, i, m)
                        hist.append((p, buf, i, m, nkt, SP, rSP))
                        emit_max(p, i, m, nkt, SP, rSP)
                        if i >= 1:
                            emit_t(*hist[i - 1][:5])
                        emit_exp(p, i, m, nkt, SP, rSP)
                        if i >= 2:
                            emit_pv(*hist[i - 2][:5])
                        if i >= 1:
                            emit_llupd(p, i - 1, hist[i - 1][3])
                        want_ld = (len(prod) * (i + 1) + n - 1) // n
                        while nld < min(want_ld, len(prod)):
                            emit_prod_load(*prod[nld])
                            nld += 1
                        while ntp < nld - 4:
                            emit_prod_tp(*prod[ntp])
                            ntp += 1
                    emit_t(*hist[-1][:5])
                    if len(hist) >= 2:
                        emit_pv(*hist[-2][:5])
                    emit_pv(*hist[-1][:5])
                    emit_llupd(p, hist[-1][2], hist[-1][3])
                    while nld < len(prod):
                        emit_prod_load(*prod[nld])
                        nld += 1
                    while ntp < len(prod):
                        emit_prod_tp(*prod[ntp])
                        ntp += 1
                    emit_finalize(2 * p, 2 * p + 1)

                for p in range(NPANEL):
                    do_panel(p)
                P.flush_block()

        if attn_only:
            dbg = nc.dram_tensor("dbg", [128, 96 + 512], F32,
                                 kind="ExternalOutput").ap()
            sSO[0].issue(lambda s: s.dma_start(dbg[:, 0:96], SM[:, :]))
            for m in range(NSLOT):
                sSO[m % 2].issue(lambda s, m=m: s.dma_start(out_d[m], acc(m)),
                                 reads=[rACC[m]])
            P.emit("sync", lambda s: s.nop(),
                   extra=[("d_so0", P.cnt["d_so0"], "dma"),
                          ("d_so1", P.cnt["d_so1"], "dma")])
            P.flush_block()
            return nc

        GT = 16 * NSLOT
        TCH = min(1024, NT)
        NTH = NT // TCH
        s2 = contextlib.ExitStack()
        XF = s2.enter_context(nc.sbuf_tensor("XF", [128, 8 * NT], BF16))
        SCRF32 = s2.enter_context(nc.sbuf_tensor("SCRF32", [128, D], F32))
        W1B = s2.enter_context(nc.sbuf_tensor("W1B", [128, 2 * 4096], BF16))
        W3B = s2.enter_context(nc.sbuf_tensor("W3B", [128, 2 * 4096], BF16))
        W2B = s2.enter_context(nc.sbuf_tensor("W2B", [128, 2 * 4096], BF16))
        APR = s2.enter_context(nc.sbuf_tensor("APR", [128, 4 * NT], BF16))
        CC = s2.enter_context(nc.sbuf_tensor("CC", [128, GT], F32))
        LOG = s2.enter_context(nc.sbuf_tensor("LOG", [128, GT], F32))
        EXPS = s2.enter_context(nc.sbuf_tensor("EXPS", [128, GT], F32))
        GSC = s2.enter_context(nc.sbuf_tensor("GSC", [128, GT], F32))
        MS = LOG
        WRK = s2.enter_context(nc.sbuf_tensor("WRK", [128, GT], F32))
        SEL = EXPS
        GMAX = WRK[:, 2 * 4 * NSLOT: 3 * 4 * NSLOT]
        GTH = WRK[:, 3 * 4 * NSLOT: 4 * 4 * NSLOT]
        GW = s2.enter_context(nc.sbuf_tensor("GW", [128, 256], BF16))
        if True:
            rXF, rAPR, rCC, rLOG, rEXPS, rGSC = (
                R("xf"), R("apr"), R("cc"), R("log"), R("exps"), R("gsc"))
            rWRK = R("wrk")
            rMS = rLOG
            rGMAX = rGTH = rWRK
            rSEL = rEXPS
            rW1 = [R("w1a"), R("w1b")]
            rW3 = [R("w3a"), R("w3b")]
            rW2 = [R("w2a"), R("w2b")]
            GWF = SCRF32[:, 0:128]
            rGW = R("gw")
            rGWF = rSCR[0]
            rGSUM = R("gsum")

            def xf(dch, lo, n):
                return XF[:, dch * NT + lo: dch * NT + lo + n]

            def prefetch_expert(e, pi):
                d1 = W1B[:, pi * 4096:(pi + 1) * 4096]
                d3 = W3B[:, pi * 4096:(pi + 1) * 4096]
                d2 = W2B[:, pi * 4096:(pi + 1) * 4096]
                if e < NEXP:
                    s1, s3, s2 = w1_img[e], w3_img[e], w2_img[e]
                else:
                    h = e - NEXP
                    s1 = ws1_img[:, :, h * 512:(h + 1) * 512]
                    s3 = ws3_img[:, :, h * 512:(h + 1) * 512]
                    s2 = ws2_img[:, h * 4:(h + 1) * 4, :]
                    d1 = d1.rearrange("p (a b) -> p a b", b=512)
                    d3 = d3.rearrange("p (a b) -> p a b", b=512)
                    d2 = d2.rearrange("p (a b) -> p a b", b=1024)
                sSW[pi].issue_batch([
                    (lambda g: g.dma_start(d1, s1), (), [rW1[pi]]),
                    (lambda g: g.dma_start(d3, s3), (), [rW3[pi]]),
                    (lambda g: g.dma_start(d2, s2), (), [rW2[pi]]),
                ])

            s2a = contextlib.ExitStack()
            PT2A = s2a.enter_context(nc.psum_tensor("PT2A", [128, 1024], BF16))
            PT2B = s2a.enter_context(nc.psum_tensor("PT2B", [128, 1024], BF16))
            PG0 = s2a.enter_context(nc.psum_tensor("PG0", [128, 512], F32))
            PG1 = s2a.enter_context(nc.psum_tensor("PG1", [128, 512], F32))
            if True:
                rPT2A, rPT2B = R("pt2a"), R("pt2b")
                PG, rPG = [PG0, PG1], [R("pg0"), R("pg1")]

                prefetch_expert(0, 0)
                sSI.issue(lambda s: s.dma_start(GWF, gwt_img),
                          writes=[rGWF])
                P.emit("vector", lambda v: v.tensor_copy(GW[:, 0:128], GWF),
                       reads=[rGWF], writes=[rGW], inc="ve")
                P.emit("vector", lambda v: v.tensor_sub(
                    GW[:, 128:256], GWF, GW[:, 0:128]),
                    reads=[rGWF], writes=[rGW], inc="ve")

                scr = lambda lane: SCRF32[:, lane * D:(lane + 1) * D]

                def do_tt_ln(m):
                    lane = m % 2
                    ln_chain(acc(m), rACC[m], xlnb(lane), rXLNBF[lane], lane,
                             sum_on_scalar=True)

                def do_tt_tp(m):
                    lane = m % 2
                    sTP[m % 2].issue(lambda s: s.dma_start(
                        XF[:, :].rearrange("p (a b) -> p a b", b=NT)
                        [:, :, m * 128:(m + 1) * 128],
                        xlnb(lane)[:, :].rearrange("p (c q) -> p c q", q=128),
                        transpose=True),
                        reads=[rXLNBF[lane]], writes=[rXF])

                def do_tt_gate(m):
                    lane = m % 2
                    pg, rpg = PG[m % 2], rPG[m % 2]
                    fns = []
                    n = 16
                    idx = 0
                    for goff in (0, 128):
                        for dch in range(8):
                            def mk(goff=goff, dch=dch, idx=idx, pg=pg):
                                def fn(t):
                                    return t.matmul(
                                        pg[:, 0:16], xf(dch, m * 128, 128),
                                        GW[:, goff + dch * 16: goff + (dch + 1) * 16],
                                        start=(idx == 0), stop=(idx == n - 1))
                                return fn
                            fns.append(mk())
                            idx += 1
                    P.group("tensor", fns, reads=[rXF, rGW],
                            writes=[rpg], inc="pe")
                    P.emit("vector", lambda v, pg=pg: v.tensor_copy(
                        LOG[:, m * 16:(m + 1) * 16], pg[:, 0:16]),
                        reads=[rpg], writes=[rLOG], inc="ve")

                for m in range(NSLOT):
                    if m >= 2:
                        do_tt_gate(m - 2)
                    do_tt_ln(m)
                    if m >= 1:
                        do_tt_tp(m - 1)
                do_tt_tp(NSLOT - 1)
                do_tt_gate(NSLOT - 2)
                do_tt_gate(NSLOT - 1)


                b3 = lambda a, inner: a.rearrange("p (a b) -> p a b", b=inner)
                P.emit("scalar", lambda s: s.activation(
                    EXPS[:, :], LOG[:, :], AF.Exp),
                    reads=[rLOG], writes=[rEXPS], inc="sc")
                P.emit("vector", lambda v: v.reduce_sum(
                    smc(C_GS, NSLOT), b3(EXPS[:, :], 16), axis=AX.X),
                    reads=[rEXPS], writes=[rGSUM])
                P.emit("vector", lambda v: v.reciprocal(
                    smc(C_GR, NSLOT), smc(C_GS, NSLOT)),
                    reads=[rGSUM], writes=[rGSUM])
                P.emit("vector", lambda v: v.tensor_tensor(
                    b3(GSC[:, :], 16), b3(EXPS[:, :], 16),
                    _bcast(smc(C_GR, NSLOT), 16), ALU.mult),
                    reads=[rEXPS, rGSUM], writes=[rGSC])
                P.emit("vector", lambda v: v.reduce_max(
                    GMAX, b3(GSC[:, :], 4), axis=AX.X),
                    reads=[rGSC], writes=[rGMAX])
                P.emit("vector", lambda v: v.reduce_max(
                    smc(C_GS, NSLOT), b3(GMAX, 4), axis=AX.X, negate=True),
                    reads=[rGMAX], writes=[rGSUM])
                P.emit("vector", lambda v: v.tensor_scalar(
                    smc(C_GR, NSLOT), smc(C_GS, NSLOT), -1.0, None, ALU.mult),
                    reads=[rGSUM], writes=[rGSUM])
                P.emit("vector", lambda v: v.tensor_tensor(
                    b3(GTH, 4), b3(GMAX, 4),
                    _bcast(smc(C_GR, NSLOT), 4), ALU.is_ge),
                    reads=[rGMAX, rGSUM], writes=[rGTH])
                P.emit("vector", lambda v: v.scalar_tensor_tensor(
                    WRK[:, 0:4 * NSLOT], GTH, -BIG, GMAX,
                    ALU.mult, ALU.add),
                    reads=[rGTH, rGMAX], writes=[rWRK])
                P.emit("vector", lambda v: v.reduce_max(
                    smc(C_GS, NSLOT), b3(WRK[:, 0:4 * NSLOT], 4), axis=AX.X,
                    negate=True),
                    reads=[rWRK], writes=[rGSUM])
                P.emit("vector", lambda v: v.tensor_scalar(
                    smc(C_GR, NSLOT), smc(C_GS, NSLOT), -1.0, None, ALU.mult),
                    reads=[rGSUM], writes=[rGSUM])
                P.emit("vector", lambda v: v.tensor_tensor(
                    b3(GTH, 4), b3(GMAX, 4),
                    _bcast(smc(C_GR, NSLOT), 4), ALU.is_ge),
                    reads=[rGMAX, rGSUM], writes=[rGTH])
                P.emit("vector", lambda v: v.tensor_tensor(
                    b3(MS[:, :], 4), b3(GSC[:, :], 4),
                    _bcast(GTH, 4), ALU.mult),
                    reads=[rGSC, rGTH], writes=[rMS])
                P.emit("vector", lambda v: v.tensor_copy(WRK[:, :], MS[:, :]),
                       reads=[rMS], writes=[rWRK])
                for _ in range(3):
                    P.emit("vector", lambda v: v.reduce_max(
                        smc(C_GS, NSLOT), b3(WRK[:, :], 16), axis=AX.X, negate=True),
                        reads=[rWRK], writes=[rGSUM])
                    P.emit("vector", lambda v: v.tensor_scalar(
                        smc(C_GR, NSLOT), smc(C_GS, NSLOT), -1.0, None, ALU.mult),
                        reads=[rGSUM], writes=[rGSUM])
                    P.emit("vector", lambda v: v.tensor_tensor(
                        b3(SEL[:, :], 16), b3(WRK[:, :], 16),
                        _bcast(smc(C_GR, NSLOT), 16), ALU.is_ge),
                        reads=[rWRK, rGSUM], writes=[rSEL])
                    P.emit("vector", lambda v: v.scalar_tensor_tensor(
                        WRK[:, :], SEL[:, :], -BIG, WRK[:, :], ALU.mult, ALU.add),
                        reads=[rSEL], writes=[rWRK])
                P.emit("vector", lambda v: v.reduce_max(
                    smc(C_GS, NSLOT), b3(WRK[:, :], 16), axis=AX.X, negate=True),
                    reads=[rWRK], writes=[rGSUM])
                P.emit("vector", lambda v: v.tensor_scalar(
                    smc(C_GR, NSLOT), smc(C_GS, NSLOT), -1.0, None, ALU.mult),
                    reads=[rGSUM], writes=[rGSUM])
                P.emit("vector", lambda v: v.tensor_tensor(
                    b3(SEL[:, :], 16), b3(MS[:, :], 16),
                    _bcast(smc(C_GR, NSLOT), 16), ALU.is_ge),
                    reads=[rMS, rGSUM], writes=[rSEL])
                P.emit("vector", lambda v: v.tensor_mul(
                    CC[:, :], GSC[:, :], SEL[:, :]),
                    reads=[rGSC, rSEL], writes=[rCC], inc="ve")
                P.flush_block()
            s2a.close()

            s2b = contextlib.ExitStack()
            H0 = s2b.enter_context(nc.psum_tensor("H0", [128, 1024], F32))
            H1 = s2b.enter_context(nc.psum_tensor("H1", [128, 1024], F32))
            H2 = s2b.enter_context(nc.psum_tensor("H2", [128, 1024], F32))
            H3 = s2b.enter_context(nc.psum_tensor("H3", [128, 1024], F32))
            if True:
                HH = [H0, H1, H2, H3]
                rH = [R("h0"), R("h1"), R("h2"), R("h3")]

                def do_expert(e):
                    pi = e % 2
                    if e + 1 < NEXP + 2:
                        prefetch_expert(e + 1, 1 - pi)

                    def w2_tt(tt):
                        yp, ryp = HH[tt % 2], rH[tt % 2]
                        fns = []
                        for ich in range(4):
                            for half in range(2):
                                fns.append(
                                    lambda t, ich=ich, half=half, tt=tt, yp=yp,
                                    pi=pi: t.matmul(
                                        yp[:, half * 512:(half + 1) * 512],
                                        APR[:, ich * NT + tt * 128:
                                            ich * NT + (tt + 1) * 128],
                                        W2B[:, pi * 4096 + ich * 1024 + half * 512:
                                            pi * 4096 + ich * 1024 + (half + 1) * 512],
                                        start=(ich == 0), stop=(ich == 3)))
                        P.group("tensor", fns, reads=[rAPR, rW2[pi]], writes=[ryp],
                                inc="pe")
                        if e < NEXP:
                            P.emit("vector",
                                   lambda v, tt=tt, yp=yp, e=e: v.scalar_tensor_tensor(
                                       acc(tt), yp[:, :],
                                       CC[:, tt * 16 + e: tt * 16 + e + 1],
                                       acc(tt), ALU.mult, ALU.add),
                                   reads=[ryp, rCC], writes=[rACC[tt]], inc="ve")
                        else:
                            P.emit("vector", lambda v, tt=tt, yp=yp: v.tensor_add(
                                acc(tt), acc(tt), yp[:, :]),
                                reads=[ryp], writes=[rACC[tt]], inc="ve")
                        if e == NEXP + 1:
                            sSO[tt % 4].issue(lambda s, tt=tt: s.dma_start(
                                out_d[tt], acc(tt)),
                                reads=[rACC[tt]])

                    for th in range(NTH):
                        for ich in range(4):
                            hp = (th * 4 + ich) % 2
                            pairs = ((HH[2 * hp], rH[2 * hp], W1B, rW1[pi]),
                                     (HH[2 * hp + 1], rH[2 * hp + 1], W3B, rW3[pi]))
                            for (ps, rps, wbuf, rwb) in pairs:
                                fns = []
                                for dch in range(8):
                                    for half in range(TCH // 512):
                                        fns.append(
                                            lambda t, dch=dch, half=half, ps=ps,
                                            wbuf=wbuf, ich=ich, th=th, pi=pi: t.matmul(
                                                ps[:, half * 512:(half + 1) * 512],
                                                wbuf[:, pi * 4096 + dch * 512 + ich * 128:
                                                     pi * 4096 + dch * 512 + (ich + 1) * 128],
                                                xf(dch, th * TCH + half * 512, 512),
                                                start=(dch == 0), stop=(dch == 7)))
                                P.group("tensor", fns, reads=[rwb, rXF],
                                        writes=[rps], inc="pe")
                            h1ps, rh1 = HH[2 * hp], rH[2 * hp]
                            h3ps, rh3 = HH[2 * hp + 1], rH[2 * hp + 1]
                            P.emit("scalar", lambda s, h1ps=h1ps: s.activation(
                                SCRF32[:, 0:TCH], h1ps[:, 0:TCH], AF.Silu),
                                reads=[rh1], writes=[rSCR[0]], inc="sc")
                            P.emit("vector",
                                   lambda v, h3ps=h3ps, ich=ich, th=th: v.tensor_mul(
                                       APR[:, ich * NT + th * TCH:
                                           ich * NT + (th + 1) * TCH],
                                       SCRF32[:, 0:TCH], h3ps[:, 0:TCH]),
                                   reads=[rSCR[0], rh3], writes=[rAPR], inc="ve")
                    for tt in range(NSLOT):
                        w2_tt(tt)

                for e in range(NEXP + 2):
                    do_expert(e)

                P.emit("sync", lambda s: s.nop(),
                       extra=[("d_so0", P.cnt["d_so0"], "dma"),
                              ("d_so1", P.cnt["d_so1"], "dma"),
                              ("d_so2", P.cnt["d_so2"], "dma"),
                              ("d_so3", P.cnt["d_so3"], "dma")])
                P.flush_block()
            s2b.close()
        s2.close()

    return nc



def _img_km(M, r):
    R_, C = M.shape
    assert R_ == r * 128
    return np.ascontiguousarray(M.reshape(r, 128, C).transpose(1, 0, 2))


_CACHED = {}


def _get_nc(attn_only=False):
    if attn_only not in _CACHED:
        _CACHED[attn_only] = build_program(attn_only)
    return _CACHED[attn_only]


def kernel(L0_attn_input_data, L0_attn_qk, L0_attn_ov, L0_ffn_gate_w,
           L0_ffn_w1, L0_ffn_w2, L0_ffn_w3, L0_ffn_ws1, L0_ffn_ws2, L0_ffn_ws3,
           _attn_only=False):
    x = np.ascontiguousarray(np.asarray(L0_attn_input_data, dtype=np.float32))
    qk = np.asarray(L0_attn_qk, dtype=np.float32)
    ov = np.asarray(L0_attn_ov, dtype=np.float32)
    gw = np.asarray(L0_ffn_gate_w, dtype=np.float32)
    w1 = np.asarray(L0_ffn_w1, dtype=np.float32)
    w2 = np.asarray(L0_ffn_w2, dtype=np.float32)
    w3 = np.asarray(L0_ffn_w3, dtype=np.float32)
    ws1 = np.asarray(L0_ffn_ws1, dtype=np.float32)
    ws2 = np.asarray(L0_ffn_ws2, dtype=np.float32)
    ws3 = np.asarray(L0_ffn_ws3, dtype=np.float32)

    qk_i = _img_km(qk, 8).reshape(128, 8 * D)
    ov_i = _img_km(ov, 8).reshape(128, 8 * D)
    gwt_i = _img_km(np.ascontiguousarray(gw.T), 8).reshape(128, 128)
    w1_i = np.stack([_img_km(w1[e], 8).reshape(128, 8 * INTER) for e in range(NEXP)])
    w3_i = np.stack([_img_km(w3[e], 8).reshape(128, 8 * INTER) for e in range(NEXP)])
    w2_i = np.stack([_img_km(w2[e], 4).reshape(128, 4 * D) for e in range(NEXP)])
    ws1_i = _img_km(ws1, 8)
    ws3_i = _img_km(ws3, 8)
    ws2_i = _img_km(ws2, 8)
    ident = np.eye(128, dtype=np.float32)

    in_maps = []
    for c in range(8):
        b, c4 = c // 4, c % 4
        xb = np.ascontiguousarray(x[b])
        xo = np.ascontiguousarray(xb.reshape(64, 128, D)[c4::4])
        jj = np.arange(512)[None, :]
        ii = np.arange(128)[:, None]
        mask = np.where(jj <= ii + 128 * c4, 0.0, -BIG).astype(np.float32)
        in_maps.append({
            "x_seq": xb, "x_own": xo, "mask": mask, "ident": ident,
            "qk_img": qk_i, "ov_img": ov_i, "gwt_img": gwt_i,
            "w1_img": w1_i, "w3_img": w3_i, "w2_img": w2_i,
            "ws1_img": ws1_i, "ws3_img": ws3_i, "ws2_img": ws2_i,
        })

    nc = _get_nc(_attn_only)
    res = run_bass_kernel_spmd(nc, in_maps, core_ids=list(range(8)))

    out = np.empty((2, SEQ, D), dtype=np.float32)
    for c in range(8):
        b, c4 = c // 4, c % 4
        out[b].reshape(64, 128, D)[c4::4] = res.results[c]["out"]
    return out
